# version 01
# baseline (speedup 1.0000x reference)
from contextlib import ExitStack

import numpy as np
import ml_dtypes
import concourse.bass as bass
import concourse.bass_isa as bass_isa
import concourse.mybir as mybir
from concourse.bass_utils import run_bass_kernel_spmd

F32 = mybir.dt.float32
BF16 = mybir.dt.bfloat16
U32 = mybir.dt.uint32
AL = mybir.AluOpType
AF = mybir.ActivationFunctionType
AX = mybir.AxisListType

NCORE = 8
SEQ = 4096
D = 2048
HD = 128
TOK = 1024
NT = TOK // 128
QBN = SEQ // 512
NCH = D // 128
SC = float(1.0 / np.sqrt(HD))
NEG = -1e30
f32 = np.float32

_PH = 99
_ASTOP = 99
_ARNDS = 99
_DBG = 0



def _blockmajor(w):
    i, o = w.shape
    return np.ascontiguousarray(w.reshape(i, o // 128, 128).transpose(1, 0, 2))


def _rope_tables():
    half = HD // 2
    inv = 1.0 / (10000.0 ** (np.arange(half, dtype=np.float64) / half))
    t = np.arange(SEQ, dtype=np.float64)
    fr = t[None, :] * inv[:, None]
    cos = np.cos(fr)
    sin = np.sin(fr)
    cosf = np.concatenate([cos, cos], axis=0).astype(ml_dtypes.bfloat16)
    sinsg = np.concatenate([sin, -sin], axis=0).astype(ml_dtypes.bfloat16)
    return cosf, sinsg


_PERM = np.concatenate([np.arange(0, HD, 2), np.arange(1, HD, 2)])


def _prep(inp):
    bf = ml_dtypes.bfloat16
    x = np.ascontiguousarray(np.asarray(inp["L0_attn_x"], f32).reshape(NCORE, TOK, D))

    wqT = np.ascontiguousarray(np.asarray(inp["L0_attn_Wq"], f32).T)
    wkT = np.ascontiguousarray(np.asarray(inp["L0_attn_Wk"], f32).T)
    wvT = np.ascontiguousarray(np.asarray(inp["L0_attn_Wv"], f32).T)
    woT = np.ascontiguousarray(np.asarray(inp["L0_attn_Wo"], f32).T)
    bq = np.asarray(inp["L0_attn_bq"], f32)
    bk = np.asarray(inp["L0_attn_bk"], f32)
    bv = np.asarray(inp["L0_attn_bv"], f32)
    bo = np.asarray(inp["L0_attn_bo"], f32)

    colperm = (np.arange(16)[:, None] * HD + _PERM[None, :]).reshape(-1)
    wqTp = wqT[:, colperm]
    wkTp = wkT[:, colperm]
    bqp = bq[colperm]
    bkp = bk[colperm]

    gwT = np.ascontiguousarray(np.asarray(inp["L0_ffn_gate_weight"], f32).T)
    gAT = np.ascontiguousarray(np.asarray(inp["L0_ffn_gate_lora_A"], f32).T)
    gBT = np.ascontiguousarray(np.asarray(inp["L0_ffn_gate_lora_B"], f32).T)
    gup = np.asarray(inp["L0_ffn_gate_up_proj"], f32)
    gupb = np.ascontiguousarray(gup.reshape(8, D, NCH, 128).transpose(0, 2, 1, 3))
    dwn = np.ascontiguousarray(np.asarray(inp["L0_ffn_down_proj"], f32))
    gupAT = np.ascontiguousarray(np.asarray(inp["L0_ffn_gup_lora_A"], f32).transpose(0, 2, 1))
    gupBT = np.ascontiguousarray(np.asarray(inp["L0_ffn_gup_lora_B"], f32).transpose(0, 2, 1))
    dwnAT = np.ascontiguousarray(np.asarray(inp["L0_ffn_down_lora_A"], f32).transpose(0, 2, 1))
    dwnBT = np.ascontiguousarray(np.asarray(inp["L0_ffn_down_lora_B"], f32).transpose(0, 2, 1))

    w1q = np.asarray(inp["L1_attn_wq"], f32)
    w1k = np.asarray(inp["L1_attn_wk"], f32)
    w1v = np.asarray(inp["L1_attn_wv"], f32)
    w1o = np.ascontiguousarray(np.asarray(inp["L1_attn_wo"], f32))
    fwT = np.ascontiguousarray(np.asarray(inp["L1_ffn_weight"], f32).T)
    fb = np.asarray(inp["L1_ffn_bias"], f32)

    cosf, sinsg = _rope_tables()
    ident = np.eye(128, dtype=bf)
    kk = np.arange(128)
    maskT = np.where(kk[:, None] <= kk[None, :], 0.0, NEG).astype(f32)
    ones_bf = np.ones((128, 512), dtype=bf)

    in_maps = []
    for c in range(NCORE):
        g = c % 4
        hcols = np.concatenate([np.arange(h * HD, (h + 1) * HD)
                                for h in range(4 * g, 4 * g + 4)])
        frow = np.arange(D)
        m = {
            "x_own": np.ascontiguousarray(x[c]),
            "wq0": _blockmajor(np.ascontiguousarray(wqTp[:, hcols])),
            "wk0": _blockmajor(np.ascontiguousarray(wkTp[:, hcols])),
            "wv0": _blockmajor(np.ascontiguousarray(wvT[:, hcols])),
            "bq0": np.ascontiguousarray(bqp[hcols]).reshape(1, 512),
            "bk0": np.ascontiguousarray(bkp[hcols]).reshape(1, 512),
            "bv0": np.ascontiguousarray(bv[hcols]).reshape(1, 512),
            "wo0": woT,
            "bo0": bo.reshape(1, D),
            "gwT": gwT, "gAT": gAT, "gBT": gBT,
            "gupb": gupb, "dwn": dwn,
            "gupAT": gupAT, "gupBT": gupBT, "dwnAT": dwnAT, "dwnBT": dwnBT,
            "wq1": _blockmajor(np.ascontiguousarray(w1q[:, hcols])),
            "wk1": _blockmajor(np.ascontiguousarray(w1k[:, g * HD:(g + 1) * HD])),
            "wv1": _blockmajor(np.ascontiguousarray(w1v[:, g * HD:(g + 1) * HD])),
            "wo1": w1o,
            "fwT": fwT, "fb1": fb.reshape(1, D),
            "cosf": cosf, "sinsg": sinsg,
            "ident": ident, "maskT": maskT, "ones": ones_bf,
            "oidx": np.ascontiguousarray(
                ((frow.reshape(NCH, 128) * 4 + g).astype(np.uint32)).T),
        }
        in_maps.append(m)
    return in_maps



class Sem:
    __slots__ = ("h", "n")

    def __init__(self, h):
        self.h = h
        self.n = 0


class Builder:
    def __init__(self):
        self.nc = bass.Bass("TRN2", target_bir_lowering=False)
        self.root = ExitStack()
        self.xfm_keep = None

    def sem(self, name):
        return Sem(self.root.enter_context(self.nc.semaphore(name)))

    def sbuf(self, es, name, shape, dt):
        return es.enter_context(self.nc.sbuf_tensor(name, shape, dt))

    def psum(self, es, name, shape, dt=F32):
        return es.enter_context(self.nc.psum_tensor(name, shape, dt))

    @staticmethod
    def dma(eng, out, in_, sem):
        eng.dma_start(out=out, in_=in_).then_inc(sem.h, 16)
        sem.n += 16

    @staticmethod
    def inc(instr, sem, k=1):
        instr.then_inc(sem.h, k)
        sem.n += k

    def dinc(self, eng, sem, k=1):
        eng.drain()
        self.inc(eng.nop(), sem, k)

    @staticmethod
    def w(eng, sem, val=None):
        v = sem.n if val is None else val
        if v > 0:
            eng.wait_ge(sem.h, v)

    def barrier(self):
        bl = self.bl
        s = self.s_ph
        base = s.n

        @bl.sync
        def _(e):
            self.w(e, self.s_x)
            self.w(e, self.s_o)
            for ss in self.s_xgs:
                self.w(e, ss)
            e.nop().then_inc(s.h, 1)

        @bl.gpsimd
        def _(e):
            self.w(e, self.s_w)
            self.w(e, self.s_g)
            self.w(e, self.s_wd)
            self.w(e, self.s_wl)
            for ss in self.s_wgs:
                self.w(e, ss)
            e.nop().then_inc(s.h, 1)

        for nm in ("tensor", "vector", "scalar"):
            @getattr(bl, nm)
            def _(e):
                e.nop().then_inc(s.h, 1)
        s.n += 5

        for nm in ("tensor", "vector", "scalar", "sync", "gpsimd"):
            @getattr(bl, nm)
            def _(e):
                e.wait_ge(s.h, base + 5)

    def build(self):
        nc = self.nc
        d = {}

        def din(name, shape, dt=F32):
            d[name] = nc.dram_tensor(name, shape, dt, kind="ExternalInput").ap()

        din("x_own", [TOK, D])
        din("wq0", [4, D, 128]); din("wk0", [4, D, 128]); din("wv0", [4, D, 128])
        din("bq0", [1, 512]); din("bk0", [1, 512]); din("bv0", [1, 512])
        din("wo0", [D, D]); din("bo0", [1, D])
        din("gwT", [D, 8]); din("gAT", [D, 16]); din("gBT", [16, 8])
        din("gupb", [8, NCH, D, 128]); din("dwn", [8, 1024, D])
        din("gupAT", [8, D, 16]); din("gupBT", [8, 16, D])
        din("dwnAT", [8, 1024, 16]); din("dwnBT", [8, 16, D])
        din("wq1", [4, D, 128]); din("wk1", [1, D, 128]); din("wv1", [1, D, 128])
        din("wo1", [D, D])
        din("fwT", [D, D]); din("fb1", [1, D])
        din("cosf", [128, SEQ], BF16); din("sinsg", [128, SEQ], BF16)
        din("ident", [128, 128], BF16); din("maskT", [128, 128], F32)
        din("ones", [128, 512], BF16)
        din("oidx", [128, NCH], U32)
        self.d = d

        y_out = nc.dram_tensor("y", [TOK, D], F32, kind="ExternalOutput").ap()
        if _DBG:
            self.dbg = nc.dram_tensor("dbg", [128, 6 * SEQ], F32,
                                      kind="ExternalOutput").ap()

        self.RG = [[0, 1, 2, 3], [4, 5, 6, 7]]
        cc = {}
        for L in (0, 1):
            cc[f"xin{L}"] = nc.dram_tensor(f"cc_xin{L}", [D, TOK], BF16)
            cc[f"xout{L}"] = nc.dram_tensor(f"cc_xout{L}", [4 * D, TOK], BF16)
            cc[f"ain{L}"] = nc.dram_tensor(f"cc_ain{L}", [512, SEQ], BF16)
            cc[f"aout{L}"] = nc.dram_tensor(f"cc_aout{L}", [4 * 512, SEQ], BF16)
        self.cc = cc

        es = self.root
        self.stream = self.sbuf(es, "stream", [128, NT * D], F32)
        self.ident = self.sbuf(es, "sb_ident", [128, 128], BF16)
        self.maskT = self.sbuf(es, "sb_maskT", [128, 128], F32)
        self.ones = self.sbuf(es, "sb_ones", [128, 512], BF16)
        self.oidx = self.sbuf(es, "sb_oidx", [128, NCH], U32)
        self.bq0 = self.sbuf(es, "sb_bq0", [1, 512], BF16)
        self.bk0 = self.sbuf(es, "sb_bk0", [1, 512], BF16)
        self.bv0 = self.sbuf(es, "sb_bv0", [1, 512], BF16)
        self.gw = self.sbuf(es, "sb_gw", [128, NCH * 8], BF16)
        self.gA = self.sbuf(es, "sb_gA", [128, NCH * 16], BF16)
        self.gB = self.sbuf(es, "sb_gB", [16, 8], BF16)

        for nm in ("ph", "w", "x", "g", "o", "ck", "qg", "aq", "tp", "af",
                   "ve", "ac", "st", "ex", "pv", "dv", "cc", "nr",
                   "gu", "sl", "h", "t1", "t1e", "t2", "t2e", "y", "ys", "ng",
                   "wd", "wl"):
            setattr(self, "s_" + nm, self.sem("sem_" + nm))
        self.s_xgs = [self.sem(f"sem_xg{i}") for i in range(6)]
        self.s_wgs = [self.sem(f"sem_wg{i}") for i in range(2)]

        with nc.Block() as bl:
            self.bl = bl
            self.ph_init()
            if _PH >= 1:
                self.ph_ln(L=0, ag=True, keep=False, first=True)
            if _PH >= 2:
                self.ph_attn(L=0)
            if _PH >= 3:
                self.ph_outproj(L=0)
            if _PH >= 4:
                self.ph_ln(L=0, ag=False, keep=True)
                self.ph_moe()
            if _PH >= 5:
                self.ph_ln(L=1, ag=True, keep=False)
            if _PH >= 6:
                self.ph_attn(L=1)
            if _PH >= 7:
                self.ph_outproj(L=1)
            if _PH >= 8:
                self.ph_ln(L=1, ag=False, keep=True)
                self.ph_l1ffn()
            self.ph_out(y_out)

        self.root.close()
        return nc

    def ph_init(self):
        bl, d = self.bl, self.d

        @bl.sync
        def _(sy):
            for t in range(NT):
                self.dma(sy, self.stream[:, t * D:(t + 1) * D],
                         d["x_own"][t * 128:(t + 1) * 128, :], self.s_x)
            self.dma(sy, self.ident[:, :], d["ident"][:, :], self.s_x)
            self.dma(sy, self.maskT[:, :], d["maskT"][:, :], self.s_x)
            self.dma(sy, self.ones[:, :], d["ones"][:, :], self.s_x)
            self.dma(sy, self.oidx[:, :], d["oidx"][:, :], self.s_x)

        @bl.gpsimd
        def _(g):
            self.dma(g, self.bq0[:, :], d["bq0"][:, :], self.s_w)
            self.dma(g, self.bk0[:, :], d["bk0"][:, :], self.s_w)
            self.dma(g, self.bv0[:, :], d["bv0"][:, :], self.s_w)
            self.dma(g, self.gw.ap().rearrange("p (c e) -> p c e", c=NCH),
                     d["gwT"].rearrange("(c p) e -> p c e", p=128), self.s_w)
            self.dma(g, self.gA.ap().rearrange("p (c e) -> p c e", c=NCH),
                     d["gAT"].rearrange("(c p) e -> p c e", p=128), self.s_w)
            self.dma(g, self.gB[:, :], d["gBT"][:, :], self.s_w)
        self.barrier()

    def ph_ln(self, L, ag, keep, first=False):
        bl, nc = self.bl, self.nc
        tag = f"ln{L}{int(keep)}{int(ag)}"
        if keep:
            self.keep_es = ExitStack()
            self.xfm_keep = self.keep_es.enter_context(
                nc.sbuf_tensor("xfm_keep" + tag, [128, NCH * TOK], BF16))
        es = ExitStack()
        stat = self.sbuf(es, tag + "_stat", [128, 64], F32)
        xtm = self.sbuf(es, tag + "_xtm", [128, 2 * D], BF16)
        if keep:
            xfm = self.xfm_keep
        else:
            xfm = self.sbuf(es, tag + "_xfm", [128, NCH * TOK], BF16)
        pT = [self.psum(es, tag + f"_pT{i}", [128, 512], BF16) for i in range(2)]

        s_ve, s_ac, s_tp, s_nr = self.s_ve, self.s_ac, self.s_tp, self.s_nr
        ve0, ac0, tp0, nr0 = s_ve.n, s_ac.n, s_tp.n, s_nr.n
        stream = self.stream

        @bl.vector
        def _(v):
            if first:
                self.w(v, self.s_x, 12 * 16)
            for t in range(NT):
                for j in range(4):
                    v.bn_stats(out=stat[:, 16 + 6 * j:22 + 6 * j],
                               in_=stream[:, t * D + j * 512:t * D + (j + 1) * 512])
                v.drain()
                v.bn_aggr(out=stat[:, 2 * t:2 * t + 2], in_=stat[:, 16:40])
                v.drain()
            self.dinc(v, s_ve)

        @bl.scalar
        def _(sc):
            self.w(sc, s_ve, ve0 + 1)
            varv = stat.ap()[:, 1:17:2]
            sc.activation(out=stat[:, 40:48], in_=varv, func=AF.Ln,
                          bias=np.float32(1e-5))
            sc.drain()
            sc.activation(out=stat[:, 48:56], in_=stat[:, 40:48], func=AF.Exp,
                          scale=np.float32(-0.5))
            self.dinc(sc, s_ac)

        @bl.vector
        def _(v):
            self.w(v, s_ac, ac0 + 1)
            meanv = stat.ap()[:, 0:16:2]
            v.scalar_tensor_tensor(out=stat[:, 56:64], in0=meanv, scalar=-1.0,
                                   in1=stat[:, 48:56], op0=AL.mult, op1=AL.mult)
            self.dinc(v, s_ve)

        @bl.scalar
        def _(sc):
            self.w(sc, s_ve, ve0 + 2)

            def norm(t):
                if t >= 2:
                    self.w(sc, s_tp, tp0 + 4 * (t - 1))
                sc.activation(out=xtm[:, (t % 2) * D:(t % 2 + 1) * D],
                              in_=stream[:, t * D:(t + 1) * D], func=AF.Identity,
                              bias=stat[:, 56 + t:57 + t],
                              scale=stat[:, 48 + t:49 + t])
                self.dinc(sc, s_nr)

            def evac(t):
                for g in range(4):
                    G = 4 * t + g
                    self.w(sc, s_tp, tp0 + G + 1)
                    sc.activation(
                        out=xfm.ap().rearrange("p (c k) -> p c k", c=NCH)[
                            :, 4 * g:4 * g + 4, t * 128:(t + 1) * 128],
                        in_=pT[G % 2].ap().rearrange("p (c k) -> p c k", c=4),
                        func=AF.Identity)
                    self.dinc(sc, s_ac)

            norm(0)
            norm(1)
            for t in range(2, NT):
                evac(t - 2)
                norm(t)
            evac(NT - 2)
            evac(NT - 1)

        @bl.tensor
        def _(te):
            for t in range(NT):
                self.w(te, s_nr, nr0 + t + 1)
                for g in range(4):
                    G = 4 * t + g
                    if G >= 2:
                        self.w(te, s_ac, ac0 + 1 + (G - 1))
                    for ch in range(4 * g, 4 * g + 4):
                        lm = te.transpose(
                            pT[G % 2][:, (ch % 4) * 128:(ch % 4 + 1) * 128],
                            xtm[:, (t % 2) * D + ch * 128:
                                (t % 2) * D + (ch + 1) * 128],
                            self.ident[:, :])
                    self.inc(lm, s_tp)

        if ag:
            @bl.sync
            def _(sy):
                self.w(sy, s_ac, ac0 + 1 + 32)
                for ch in range(NCH):
                    self.dma(sy, self.cc[f"xin{L}"].ap()[ch * 128:(ch + 1) * 128, :],
                             xfm[:, ch * TOK:(ch + 1) * TOK], self.s_o)

            @bl.gpsimd
            def _(g):
                self.w(g, self.s_o)
                self.inc(g.collective_compute(
                    "AllGather", AL.bypass, replica_groups=self.RG,
                    ins=[self.cc[f"xin{L}"].ap()],
                    outs=[self.cc[f"xout{L}"].ap()]), self.s_cc)
        self.barrier()
        es.close()

    def ph_attn(self, L):
        bl, d = self.bl, self.d
        rope = (L == 0)
        rounds = 2 if L == 0 else 1
        hpr = 2 if L == 0 else 4
        kvpr = 2 if L == 0 else 1
        nrt = hpr + kvpr
        ngrp = hpr + 2 * kvpr
        wn = ("wq0", "wk0", "wv0") if L == 0 else ("wq1", "wk1", "wv1")
        xg_dram = self.cc[f"xout{L}"].ap()
        a_in = self.cc[f"ain{L}"].ap()

        es = ExitStack()
        XSL = 6
        xg = self.sbuf(es, f"at{L}_xg", [128, XSL * 512], BF16)
        wq = self.sbuf(es, f"at{L}_wq", [128, hpr * NCH * 128], BF16)
        wk = self.sbuf(es, f"at{L}_wk", [128, kvpr * NCH * 128], BF16)
        wv = self.sbuf(es, f"at{L}_wv", [128, kvpr * NCH * 128], BF16)
        qbuf = self.sbuf(es, f"at{L}_q", [128, hpr * SEQ], BF16)
        kbuf = self.sbuf(es, f"at{L}_k", [128, kvpr * SEQ], BF16)
        vbuf = self.sbuf(es, f"at{L}_v", [128, kvpr * 32 * 132], BF16)
        if rope:
            tcos = self.sbuf(es, "at_cos", [128, SEQ], BF16)
            tsin = self.sbuf(es, "at_sin", [128, SEQ], BF16)
        if rope:
            qscr = self.sbuf(es, "at_qscr", [128, nrt * 512], BF16)
            tA = self.sbuf(es, "at_tA", [128, nrt * 512], BF16)
            tB = self.sbuf(es, "at_tB", [128, nrt * 512], BF16)
        vfm = self.sbuf(es, f"at{L}_vfm", [128, kvpr * 512], BF16)
        sq = self.sbuf(es, f"at{L}_sq", [128, 1024], BF16)
        kmax = self.sbuf(es, f"at{L}_km", [1, 2 * kvpr * 16], F32)
        negm = self.sbuf(es, f"at{L}_ng", [1, SEQ], BF16)
        ntmp = self.sbuf(es, f"at{L}_nt", [1, 1024], F32)
        pt = [self.sbuf(es, f"at{L}_pt{i}", [128, 512], BF16) for i in range(2)]
        atm = self.sbuf(es, f"at{L}_atm", [128, 512], BF16)
        afm = self.sbuf(es, f"at{L}_afm", [128, 512], BF16)
        lrec = self.sbuf(es, f"at{L}_lr", [128, 8], F32)

        s_w, s_x, s_ck, s_qg, s_aq = self.s_w, self.s_x, self.s_ck, self.s_qg, self.s_aq
        s_tp, s_af, s_ve, s_ac, s_nr = self.s_tp, self.s_af, self.s_ve, self.s_ac, self.s_nr
        s_st, s_ex, s_pv, s_dv, s_o, s_g = (self.s_st, self.s_ex, self.s_pv,
                                            self.s_dv, self.s_o, self.s_g)

        @bl.gpsimd
        def _(g):
            if rope:
                self.dma(g, tcos[:, :], d["cosf"][:, :], s_w)
                self.dma(g, tsin[:, :], d["sinsg"][:, :], s_w)
            for kv in range(kvpr):
                g.memset(vbuf.ap().rearrange(
                    "p (kv s c) -> p kv s c", kv=kvpr, s=32)[:, kv, :, 128:129],
                    1.0)
            self.dinc(g, self.s_sl)
        g_ready = self.s_sl.n

        ckP = s_ck.n
        xgP = [s.n for s in self.s_xgs]

        for rnd in range(min(rounds, _ARNDS)):
            @bl.gpsimd
            def _(g, rnd=rnd):
                if rnd > 0 and _ASTOP >= 1:
                    self.w(g, s_qg, self.att_qg_end)
                for i in range(hpr):
                    self.dma(g, wq.ap().rearrange("p (i c k) -> p i c k",
                                                  i=hpr, c=NCH)[:, i],
                             d[wn[0]].rearrange("h (c p) k -> h p c k", p=128)[
                                 rnd * hpr + i], s_w)
                for i in range(kvpr):
                    self.dma(g, wk.ap().rearrange("p (i c k) -> p i c k",
                                                  i=kvpr, c=NCH)[:, i],
                             d[wn[1]].rearrange("h (c p) k -> h p c k", p=128)[
                                 rnd * kvpr + i], s_w)
                    self.dma(g, wv.ap().rearrange("p (i c k) -> p i c k",
                                                  i=kvpr, c=NCH)[:, i],
                             d[wn[2]].rearrange("h (c p) k -> h p c k", p=128)[
                                 rnd * kvpr + i], s_w)
            w_ready = s_w.n

            if _ASTOP < 1:
                continue
            with ExitStack() as ps:
                pq = [self.psum(ps, f"a{L}{rnd}pq{i}", [128, 512])
                      for i in range(hpr)]
                pk = [self.psum(ps, f"a{L}{rnd}pk{i}", [128, 512])
                      for i in range(kvpr)]
                pv = [self.psum(ps, f"a{L}{rnd}pv{i}", [128, 512])
                      for i in range(kvpr)]
                pTq = self.psum(ps, f"a{L}{rnd}pT", [128, 512], BF16)

                qg0, aq0, tp0, af0, ve0 = (s_qg.n, s_aq.n, s_tp.n, s_af.n,
                                           s_ve.n)

                cc_t = self.s_cc.n

                @bl.sync
                def _(sy, rnd=rnd, cc_t=cc_t):
                    self.w(sy, self.s_cc, cc_t)
                    for tb in range(QBN):
                        rb, half = tb // 2, tb % 2
                        for ch in range(NCH):
                            n = rnd * QBN * NCH + tb * NCH + ch
                            if n >= XSL:
                                self.w(sy, s_ck, ckP + n - XSL + 1)
                            self.dma(sy, xg[:, (n % XSL) * 512:(n % XSL + 1) * 512],
                                     xg_dram[rb * D + ch * 128:
                                             rb * D + (ch + 1) * 128,
                                             half * 512:(half + 1) * 512],
                                     self.s_xgs[n % XSL])

                ve_cur, af_cur = s_ve.n, s_af.n

                @bl.tensor
                def _(te, rnd=rnd):
                    self.w(te, s_w, w_ready)
                    self.w(te, self.s_sl, g_ready)
                    self.w(te, s_ve, ve_cur)
                    self.w(te, s_af, af_cur)
                    for tb in range(QBN):
                        if tb >= 1:
                            self.w(te, s_aq, aq0 + tb)
                        for ch in range(NCH):
                            n = rnd * QBN * NCH + tb * NCH + ch
                            self.w(te, self.s_xgs[n % XSL],
                                   xgP[n % XSL] + 16 * (n // XSL + 1))
                            xsl = xg[:, (n % XSL) * 512:(n % XSL + 1) * 512]
                            st = (ch == 0)
                            last = (ch == NCH - 1) and (L == 1)
                            for i in range(hpr):
                                lm = te.matmul(pq[i][:, :],
                                               wq[:, (i * NCH + ch) * 128:
                                                  (i * NCH + ch + 1) * 128],
                                               xsl, start=st, stop=last)
                            for i in range(kvpr):
                                lm = te.matmul(pk[i][:, :],
                                               wk[:, (i * NCH + ch) * 128:
                                                  (i * NCH + ch + 1) * 128],
                                               xsl, start=st, stop=last)
                                lm = te.matmul(pv[i][:, :],
                                               wv[:, (i * NCH + ch) * 128:
                                                  (i * NCH + ch + 1) * 128],
                                               xsl, start=st, stop=last)
                            self.inc(lm, s_ck)
                        if L == 0:
                            for i in range(hpr):
                                lm = te.matmul(pq[i][:, :],
                                               self.bq0[0:1, (rnd * hpr + i) * 128:
                                                        (rnd * hpr + i + 1) * 128],
                                               self.ones[0:1, 0:512],
                                               start=False, stop=True)
                            for i in range(kvpr):
                                lm = te.matmul(pk[i][:, :],
                                               self.bk0[0:1, (rnd * kvpr + i) * 128:
                                                        (rnd * kvpr + i + 1) * 128],
                                               self.ones[0:1, 0:512],
                                               start=False, stop=True)
                                lm = te.matmul(pv[i][:, :],
                                               self.bv0[0:1, (rnd * kvpr + i) * 128:
                                                        (rnd * kvpr + i + 1) * 128],
                                               self.ones[0:1, 0:512],
                                               start=False, stop=True)
                            self.inc(lm, s_qg)
                        else:
                            te.drain()
                            self.inc(te.nop(), s_qg)
                        for kv in range(kvpr):
                            self.w(te, s_aq, aq0 + tb + 1)
                            if tb * kvpr + kv >= 1:
                                self.w(te, s_af, af0 + tb * kvpr + kv)
                            for sub in range(4):
                                lm = te.transpose(
                                    pTq[:, sub * 128:(sub + 1) * 128],
                                    vfm[:, kv * 512 + sub * 128:
                                        kv * 512 + (sub + 1) * 128],
                                    self.ident[:, :])
                            self.inc(lm, s_tp)
                self.att_qg_end = s_qg.n

                @bl.scalar
                def _(sc, rnd=rnd):
                    for tb in range(QBN):
                        self.w(sc, s_qg, qg0 + tb + 1)
                        if rope:
                            if tb >= 1:
                                self.w(sc, s_ve, ve0 + tb)
                            for i in range(hpr):
                                sc.activation(out=qscr[:, i * 512:(i + 1) * 512],
                                              in_=pq[i][:, :], func=AF.Identity)
                            for i in range(kvpr):
                                sc.activation(
                                    out=qscr[:, (hpr + i) * 512:
                                             (hpr + i + 1) * 512],
                                    in_=pk[i][:, :], func=AF.Identity)
                        else:
                            for i in range(hpr):
                                sc.activation(out=qbuf[:, i * SEQ + tb * 512:
                                                       i * SEQ + (tb + 1) * 512],
                                              in_=pq[i][:, :], func=AF.Identity)
                            for i in range(kvpr):
                                sc.activation(out=kbuf[:, i * SEQ + tb * 512:
                                                       i * SEQ + (tb + 1) * 512],
                                              in_=pk[i][:, :], func=AF.Identity)
                        for kv in range(kvpr):
                            sc.activation(out=vfm[:, kv * 512:(kv + 1) * 512],
                                          in_=pv[kv][:, :], func=AF.Identity)
                        self.dinc(sc, s_aq)
                        for kv in range(kvpr):
                            self.w(sc, s_tp, tp0 + tb * kvpr + kv + 1)
                            sc.activation(
                                out=vbuf.ap().rearrange(
                                    "p (kv s c) -> p kv s c", kv=kvpr, s=32)[
                                    :, kv, tb * 4:tb * 4 + 4, 0:128],
                                in_=pTq.ap().rearrange("p (s c) -> p s c", s=4),
                                func=AF.Identity)
                            self.dinc(sc, s_af)

                if rope:
                    @bl.vector
                    def _(v, rnd=rnd):
                        for tb in range(QBN):
                            self.w(v, s_aq, aq0 + tb + 1)
                            tc = tcos[:, tb * 512:(tb + 1) * 512]
                            tslo = tsin[64:128, tb * 512:(tb + 1) * 512]
                            tshi = tsin[0:64, tb * 512:(tb + 1) * 512]
                            for i in range(4):
                                v.tensor_tensor(out=tA[:, i * 512:(i + 1) * 512],
                                                in0=qscr[:, i * 512:(i + 1) * 512],
                                                in1=tc, op=AL.mult)
                            for i in range(4):
                                v.tensor_tensor(
                                    out=tB[0:64, i * 512:(i + 1) * 512],
                                    in0=qscr[64:128, i * 512:(i + 1) * 512],
                                    in1=tslo, op=AL.mult)
                            for i in range(4):
                                v.tensor_tensor(
                                    out=tB[64:128, i * 512:(i + 1) * 512],
                                    in0=qscr[0:64, i * 512:(i + 1) * 512],
                                    in1=tshi, op=AL.mult)
                            v.drain()
                            for i in range(4):
                                dst = (qbuf[:, i * SEQ + tb * 512:
                                            i * SEQ + (tb + 1) * 512] if i < hpr
                                       else kbuf[:, (i - hpr) * SEQ + tb * 512:
                                                 (i - hpr) * SEQ + (tb + 1) * 512])
                                v.tensor_tensor(out=dst,
                                                in0=tA[:, i * 512:(i + 1) * 512],
                                                in1=tB[:, i * 512:(i + 1) * 512],
                                                op=AL.add)
                            self.dinc(v, s_ve)
                qkv_ve_end = s_ve.n
                qkv_af_end = s_af.n

            if _DBG and rnd == 0:
                @bl.gpsimd
                def _(sy):
                    self.w(sy, s_ve)
                    self.w(sy, s_af)
                    self.dma(sy, self.dbg[:, 0:SEQ], qbuf[:, 0:SEQ], self.s_w)
                    self.dma(sy, self.dbg[:, SEQ:2 * SEQ], kbuf[:, 0:SEQ], self.s_w)
                    dv = self.dbg[:, 2 * SEQ:2 * SEQ + 32 * 129]
                    self.dma(sy, dv.rearrange("p (s c) -> p s c", c=129),
                             vbuf.ap().rearrange("p (s c) -> p s c", c=132)[
                                 :, 0:32, 0:129], self.s_w)

            if _ASTOP < 2:
                continue
            with ExitStack() as ps:
                pnr = [self.psum(ps, f"a{L}{rnd}pn{i}", [1, 512]) for i in range(2)]
                nr0b, qg0b, af0b, ng0b = s_nr.n, s_qg.n, s_af.n, self.s_ng.n
                NJ = kvpr * QBN

                @bl.scalar
                def _(sc, rnd=rnd):
                    self.w(sc, s_ve, qkv_ve_end)
                    self.w(sc, s_af, qkv_af_end)
                    for J in range(NJ):
                        kv, j = J // QBN, J % QBN
                        if J >= 2:
                            self.w(sc, s_qg, qg0b + J - 1)
                        sc.activation(out=sq[:, (J % 2) * 512:(J % 2 + 1) * 512],
                                      in_=kbuf[:, kv * SEQ + j * 512:
                                              kv * SEQ + (j + 1) * 512],
                                      func=AF.Square)
                        self.dinc(sc, s_nr)
                        if J >= 1:
                            if J >= 3:
                                self.w(sc, self.s_ng, ng0b + J - 2)
                            self.w(sc, s_qg, qg0b + J)
                            sc.activation(out=ntmp[0:1, ((J - 1) % 2) * 512:
                                                   ((J - 1) % 2 + 1) * 512],
                                          in_=pnr[(J - 1) % 2][0:1, :],
                                          func=AF.Sqrt)
                            self.dinc(sc, s_af)
                    self.w(sc, s_qg, qg0b + NJ)
                    if NJ >= 2:
                        self.w(sc, self.s_ng, ng0b + NJ - 1)
                    sc.activation(out=ntmp[0:1, ((NJ - 1) % 2) * 512:
                                           ((NJ - 1) % 2 + 1) * 512],
                                  in_=pnr[(NJ - 1) % 2][0:1, :], func=AF.Sqrt)
                    self.dinc(sc, s_af)

                @bl.tensor
                def _(te, rnd=rnd):
                    for J in range(NJ):
                        self.w(te, s_nr, nr0b + J + 1)
                        if J >= 2:
                            self.w(te, s_af, af0b + J - 1)
                        lm = te.matmul(pnr[J % 2][0:1, :], self.ones[:, 0:1],
                                       sq[:, (J % 2) * 512:(J % 2 + 1) * 512],
                                       start=True, stop=True)
                        self.inc(lm, s_qg)

                @bl.vector
                def _(v, rnd=rnd):
                    for J in range(NJ):
                        kv, j = J // QBN, J % QBN
                        self.w(v, s_af, af0b + J + 1)
                        v.tensor_reduce(out=kmax[0:1, kv * 16 + j:kv * 16 + j + 1],
                                        in_=ntmp[0:1, (J % 2) * 512:
                                                 (J % 2 + 1) * 512],
                                        axis=AX.X, op=AL.max)
                        self.dinc(v, self.s_ng)
                    v.drain()
                    for kv in range(kvpr):
                        for j in range(QBN):
                            v.tensor_reduce(out=kmax[0:1, kv * 16 + 8 + j:
                                                     kv * 16 + 9 + j],
                                            in_=kmax[0:1, kv * 16:kv * 16 + j + 1],
                                            axis=AX.X, op=AL.max)
                    self.dinc(v, self.s_ng)
                kmax_ng = self.s_ng.n

            with ExitStack() as ps:
                stp = [self.psum(ps, f"a{L}{rnd}st{i}", [128, 512])
                       for i in range(2)]
                dps = self.psum(ps, f"a{L}{rnd}dp", [128, 128])
                pva = self.psum(ps, f"a{L}{rnd}va", [128, 512])
                pvb = self.psum(ps, f"a{L}{rnd}vb", [128, 512])
                pTa = self.psum(ps, f"a{L}{rnd}pTa", [128, 512], BF16)
                pnr = [self.psum(ps, f"a{L}{rnd}qn{i}", [1, 512])
                       for i in range(2)]

                for p in range(hpr):
                    if _ASTOP < 3 or (_ASTOP == 4 and p > 0):
                        continue
                    kv = p if L == 0 else 0
                    pg = rnd * hpr + p

                    nr0f, qg0f, afn0, ng0f = s_nr.n, s_qg.n, s_af.n, self.s_ng.n

                    @bl.scalar
                    def _(sc, p=p):
                        if p == 0:
                            self.w(sc, self.s_ng, kmax_ng)
                        for J in range(QBN):
                            if J >= 2:
                                self.w(sc, s_qg, qg0f + J - 1)
                            sc.activation(out=sq[:, (J % 2) * 512:(J % 2 + 1) * 512],
                                          in_=qbuf[:, p * SEQ + J * 512:
                                                  p * SEQ + (J + 1) * 512],
                                          func=AF.Square)
                            self.dinc(sc, s_nr)
                            if J >= 1:
                                self.w(sc, self.s_ng, max(ng0f, ng0f + J - 2))
                                self.w(sc, s_qg, qg0f + J)
                                sc.activation(out=ntmp[0:1, ((J - 1) % 2) * 512:
                                                       ((J - 1) % 2 + 1) * 512],
                                              in_=pnr[(J - 1) % 2][0:1, :],
                                              func=AF.Sqrt)
                                self.dinc(sc, s_af)
                        self.w(sc, s_qg, qg0f + QBN)
                        self.w(sc, self.s_ng, ng0f + QBN - 1)
                        sc.activation(out=ntmp[0:1, ((QBN - 1) % 2) * 512:
                                               ((QBN - 1) % 2 + 1) * 512],
                                      in_=pnr[(QBN - 1) % 2][0:1, :], func=AF.Sqrt)
                        self.dinc(sc, s_af)

                    @bl.tensor
                    def _(te, p=p):
                        if p == 0:
                            self.w(te, self.s_ng, kmax_ng)
                        for J in range(QBN):
                            self.w(te, s_nr, nr0f + J + 1)
                            self.w(te, s_af, max(afn0, afn0 + J - 1))
                            lm = te.matmul(pnr[J % 2][0:1, :],
                                           self.ones[:, 0:1],
                                           sq[:, (J % 2) * 512:(J % 2 + 1) * 512],
                                           start=True, stop=True)
                            self.inc(lm, s_qg)

                    @bl.vector
                    def _(v, p=p, kv=kv):
                        for J in range(QBN):
                            self.w(v, s_af, afn0 + J + 1)
                            v.tensor_scalar(
                                out=negm[0:1, J * 512:(J + 1) * 512],
                                in0=ntmp[0:1, (J % 2) * 512:(J % 2 + 1) * 512],
                                scalar1=kmax[0:1, kv * 16 + 8 + J:
                                             kv * 16 + 9 + J],
                                scalar2=-1.02,
                                op0=AL.mult, op1=AL.mult)
                            self.dinc(v, self.s_ng)
                    negm_ng = self.s_ng.n

                    st0, ex0, pv0, dv0 = s_st.n, s_ex.n, s_pv.n, s_dv.n
                    ve0f, tp0f, af0f, o0f = s_ve.n, s_tp.n, s_af.n, s_o.n
                    if _ASTOP < 4:
                        continue

                    units = []
                    for qb in range(QBN):
                        for kb in range(qb + 1):
                            for i in range(4):
                                units.append((qb, kb, i))
                    NU = len(units)
                    diag_idx = {}
                    nd = 0
                    for u, (qb, kb, i) in enumerate(units):
                        if kb == qb:
                            diag_idx[u] = nd
                            nd += 1
                    qb_last = {}
                    for u, (qb, kb, i) in enumerate(units):
                        qb_last[qb] = u

                    @bl.tensor
                    def _(te, p=p, kv=kv, units=units, NU=NU, st0=st0, ex0=ex0,
                          pv0=pv0, ve0f=ve0f, af0f=af0f, qb_last=qb_last):
                        self.w(te, self.s_ng, negm_ng)
                        bank_started = set()

                        def emit_st(u):
                            qb, kb, i = units[u]
                            diag = (kb == qb)
                            cb = i * 128 if diag else 0
                            ncol = 512 - cb
                            self.w(te, s_ex, max(ex0, ex0 + u - 1))
                            te.matmul(stp[u % 2][:, 0:ncol],
                                      kbuf[:, kv * SEQ + kb * 512 + i * 128:
                                           kv * SEQ + kb * 512 + (i + 1) * 128],
                                      qbuf[:, p * SEQ + qb * 512 + cb:
                                           p * SEQ + (qb + 1) * 512],
                                      start=True, stop=False)
                            lm = te.matmul(stp[u % 2][:, 0:ncol],
                                           self.ones[0:1, 0:128],
                                           negm[0:1, qb * 512 + cb:(qb + 1) * 512],
                                           start=False, stop=True)
                            self.inc(lm, s_st)

                        def emit_pv(u):
                            qb, kb, i = units[u]
                            diag = (kb == qb)
                            cb = i * 128 if diag else 0
                            if kb == 0 and i == 0:
                                self.w(te, s_ve, ve0f + qb)
                            self.w(te, s_ex, ex0 + u + 1)
                            trange = range(i, 4) if diag else range(4)
                            for t in trange:
                                bank = pva if t < 2 else pvb
                                col = (t % 2) * 132
                                key = (qb, t < 2)
                                stt = key not in bank_started
                                bank_started.add(key)
                                lm = te.matmul(bank[:, col:col + 129],
                                               pt[u % 2][:, t * 128 - cb:
                                                         (t + 1) * 128 - cb],
                                               vbuf[:, kv * 32 * 132 +
                                                    (kb * 4 + i) * 132:
                                                    kv * 32 * 132 +
                                                    (kb * 4 + i) * 132 + 129],
                                               start=stt, stop=False,
                                               skip_group_check=True)
                            self.inc(lm, s_pv)

                        def emit_trans(qb):
                            self.w(te, s_ve, ve0f + qb + 1)
                            self.w(te, s_af, af0f + qb)
                            for sub in range(4):
                                lm = te.transpose(
                                    pTa[:, sub * 128:(sub + 1) * 128],
                                    atm[:, sub * 128:(sub + 1) * 128],
                                    self.ident[:, :])
                            self.inc(lm, s_tp)

                        pv_done = -1
                        for u in range(NU):
                            emit_st(u)
                            if u >= 1 and pv_done < u - 1:
                                emit_pv(u - 1)
                                pv_done = u - 1
                            qb = units[u][0]
                            if qb_last[qb] == u:
                                emit_pv(u)
                                pv_done = u
                                emit_trans(qb)

                    @bl.vector
                    def _(v, p=p, units=units, NU=NU, st0=st0, ex0=ex0,
                          pv0=pv0, tp0f=tp0f, diag_idx=diag_idx,
                          qb_last=qb_last):
                        for u, (qb, kb, i) in enumerate(units):
                            if kb == qb:
                                self.w(v, s_st, st0 + u + 1)
                                self.w(v, s_ex, max(ex0, ex0 + u))
                                v.tensor_tensor(out=dps[:, :],
                                                in0=stp[u % 2][:, 0:128],
                                                in1=self.maskT[:, :], op=AL.add)
                                self.dinc(v, s_dv)
                            if qb_last[qb] == u:
                                self.w(v, s_pv, pv0 + u + 1)
                                self.w(v, s_tp, tp0f + qb)
                                for t in range(4):
                                    bank = pva if t < 2 else pvb
                                    col = (t % 2) * 132
                                    v.reciprocal(out=lrec[:, t:t + 1],
                                                 in_=bank[:, col + 128:col + 129])
                                v.drain()
                                for t in range(4):
                                    bank = pva if t < 2 else pvb
                                    col = (t % 2) * 132
                                    v.tensor_scalar(
                                        out=atm[:, t * 128:(t + 1) * 128],
                                        in0=bank[:, col:col + 128],
                                        scalar1=lrec[:, t:t + 1],
                                        scalar2=None, op0=AL.mult)
                                self.dinc(v, s_ve)

                    @bl.scalar
                    def _(sc, p=p, units=units, NU=NU, st0=st0, pv0=pv0,
                          dv0=dv0, tp0f=tp0f, af0f=af0f, o0f=o0f,
                          diag_idx=diag_idx, qb_last=qb_last):
                        for u, (qb, kb, i) in enumerate(units):
                            diag = (kb == qb)
                            cb = i * 128 if diag else 0
                            ncol = 512 - cb
                            self.w(sc, s_st, st0 + u + 1)
                            if u >= 2:
                                self.w(sc, s_pv, pv0 + u - 1)
                            if diag:
                                self.w(sc, s_dv, dv0 + diag_idx[u] + 1)
                                sc.activation(out=pt[u % 2][:, 0:128],
                                              in_=dps[:, :], func=AF.Exp,
                                              scale=np.float32(SC))
                                if ncol > 128:
                                    sc.activation(out=pt[u % 2][:, 128:ncol],
                                                  in_=stp[u % 2][:, 128:ncol],
                                                  func=AF.Exp,
                                                  scale=np.float32(SC))
                            else:
                                sc.activation(out=pt[u % 2][:, :],
                                              in_=stp[u % 2][:, :], func=AF.Exp,
                                              scale=np.float32(SC))
                            self.dinc(sc, s_ex)
                            if qb_last[qb] == u:
                                self.w(sc, s_tp, tp0f + qb + 1)
                                self.w(sc, s_o, o0f + 16 * qb)
                                sc.activation(out=afm[:, :], in_=pTa[:, :],
                                              func=AF.Identity)
                                self.dinc(sc, s_af)

                    @bl.sync
                    def _(sy, pg=pg, af0f=af0f):
                        for qb in range(QBN):
                            self.w(sy, s_af, af0f + QBN * 0 + qb + 1)
                            self.dma(sy, a_in[pg * 128:(pg + 1) * 128,
                                              qb * 512:(qb + 1) * 512],
                                     afm[:, :], s_o)

        if _ASTOP >= 6:
            @bl.gpsimd
            def _(g):
                self.w(g, self.s_o)
                self.inc(g.collective_compute(
                    "AllGather", AL.bypass, replica_groups=self.RG,
                    ins=[self.cc[f"ain{L}"].ap()],
                    outs=[self.cc[f"aout{L}"].ap()]), self.s_cc)
        self.barrier()
        es.close()

    def ph_outproj(self, L):
        bl, d = self.bl, self.d
        es = ExitStack()
        wo = self.sbuf(es, f"op{L}_wo", [128, NCH * D], BF16)
        ao = self.sbuf(es, f"op{L}_ao", [128, NCH * TOK], BF16)
        if L == 0:
            self.bo0 = self.sbuf(es, "sb_bo0", [1, D], BF16)
        aout_view = self.cc[f"aout{L}"].ap().rearrange("r (q t) -> (r q) t", q=4)

        s_w, s_g, s_qg, s_ve = self.s_w, self.s_g, self.s_qg, self.s_ve
        qg0, ve0 = s_qg.n, s_ve.n
        won = "wo0" if L == 0 else "wo1"

        @bl.gpsimd
        def _(g):
            self.w(g, self.s_cc)
            if L == 0:
                self.dma(g, self.bo0[:, :], d["bo0"][:, :], s_w)
            for ch in range(NCH):
                self.dma(g, wo[:, ch * D:(ch + 1) * D],
                         d[won][ch * 128:(ch + 1) * 128, :], s_w)
            for ch in range(NCH):
                g.indirect_dma_start(
                    out=ao[:, ch * TOK:(ch + 1) * TOK], out_offset=None,
                    in_=aout_view,
                    in_offset=bass.IndirectOffsetOnAxis(
                        self.oidx[:, ch:ch + 1], 0)).then_inc(self.s_g.h, 16)
                self.s_g.n += 16

        with ExitStack() as ps:
            pop = [self.psum(ps, f"op{L}p{i}", [128, 2048]) for i in range(2)]

            @bl.tensor
            def _(te):
                self.w(te, s_w)
                self.w(te, s_g)
                for t in range(NT):
                    if t >= 2:
                        self.w(te, s_ve, ve0 + t - 1)
                    pp = pop[t % 2]
                    for ch in range(NCH):
                        for os_ in range(4):
                            lm = te.matmul(pp[:, os_ * 512:(os_ + 1) * 512],
                                           ao[:, ch * TOK + t * 128:
                                              ch * TOK + (t + 1) * 128],
                                           wo[:, ch * D + os_ * 512:
                                              ch * D + (os_ + 1) * 512],
                                           start=(ch == 0),
                                           stop=(L == 1 and ch == NCH - 1))
                    if L == 0:
                        for os_ in range(4):
                            lm = te.matmul(pp[:, os_ * 512:(os_ + 1) * 512],
                                           self.ones[0:1, 0:128],
                                           self.bo0[0:1, os_ * 512:
                                                    (os_ + 1) * 512],
                                           start=False, stop=True)
                    self.inc(lm, s_qg)

            @bl.vector
            def _(v):
                for t in range(NT):
                    self.w(v, s_qg, qg0 + t + 1)
                    v.tensor_tensor(out=self.stream[:, t * D:(t + 1) * D],
                                    in0=pop[t % 2][:, :],
                                    in1=self.stream[:, t * D:(t + 1) * D],
                                    op=AL.add)
                    self.dinc(v, s_ve)

            self.barrier()
        es.close()

    def ph_moe(self):
        bl, d = self.bl, self.d
        es = ExitStack()
        x2 = self.xfm_keep
        csb = self.sbuf(es, "moe_c", [128, 64], F32)
        esb = self.sbuf(es, "moe_e", [128, 96], F32)
        trs = self.sbuf(es, "moe_tr", [16, 1024], BF16)
        gwt = [self.sbuf(es, f"moe_gw{i}", [128, NCH * 128], BF16)
               for i in range(2)]
        dwt = self.sbuf(es, "moe_dw", [128, 8 * D], BF16)
        gub = self.sbuf(es, "moe_gub", [16, D], BF16)
        dnb = self.sbuf(es, "moe_dnb", [16, D], BF16)
        gua = self.sbuf(es, "moe_gua", [128, NCH * 16], BF16)
        dna = self.sbuf(es, "moe_dna", [128, 8 * 16], BF16)
        hsb = self.sbuf(es, "moe_h", [128, 16 * 512], BF16)
        ssb = self.sbuf(es, "moe_s", [128, 1024], F32)
        t1s = self.sbuf(es, "moe_t1", [16, 1024], BF16)
        t2s = self.sbuf(es, "moe_t2", [16, 1024], BF16)

        s_w, s_qg, s_ve, s_ac, s_dv = (self.s_w, self.s_qg, self.s_ve,
                                       self.s_ac, self.s_dv)
        m_gu, m_sl, m_h = self.s_gu, self.s_sl, self.s_h
        m_t1, m_t1e, m_t2, m_t2e = self.s_t1, self.s_t1e, self.s_t2, self.s_t2e
        m_y, m_ys = self.s_y, self.s_ys

        with ExitStack() as ps:
            plg = [self.psum(ps, f"moe_lg{i}", [128, 8]) for i in range(2)]
            ptr = self.psum(ps, "moe_ptr", [16, 512])
            qg0, ac0, ve0, dv0 = s_qg.n, s_ac.n, s_ve.n, s_dv.n

            @bl.tensor
            def _(te):
                for G in range(2):
                    if G >= 1:
                        self.w(te, s_ac, ac0 + G)
                    for ch in range(NCH):
                        lm = te.matmul(ptr[:, :],
                                       self.gA[:, ch * 16:(ch + 1) * 16],
                                       x2[:, ch * TOK + G * 512:
                                          ch * TOK + (G + 1) * 512],
                                       start=(ch == 0), stop=(ch == NCH - 1))
                    self.inc(lm, s_qg)
                for t in range(NT):
                    self.w(te, s_ac, ac0 + 2)
                    if t >= 2:
                        self.w(te, s_dv, dv0 + t - 1)
                        self.w(te, s_ac, ac0 + t + 1)
                    pp = plg[t % 2]
                    for ch in range(NCH):
                        te.matmul(pp[:, :],
                                  x2[:, ch * TOK + t * 128:
                                     ch * TOK + (t + 1) * 128],
                                  self.gw[:, ch * 8:(ch + 1) * 8],
                                  start=(ch == 0), stop=False)
                    lm = te.matmul(pp[:, :], trs[:, t * 128:(t + 1) * 128],
                                   self.gB[:, :], start=False, stop=True)
                    self.inc(lm, s_qg)

            @bl.vector
            def _(v):
                for t in range(NT):
                    self.w(v, s_qg, qg0 + 2 + t + 1)
                    v.tensor_reduce(out=esb[:, 64 + t:65 + t],
                                    in_=plg[t % 2][:, :], axis=AX.X,
                                    op=AL.max, negate=True)
                    self.dinc(v, s_dv)

            @bl.scalar
            def _(sc):
                for G in range(2):
                    self.w(sc, s_qg, qg0 + G + 1)
                    sc.activation(out=trs[:, G * 512:(G + 1) * 512],
                                  in_=ptr[:, :], func=AF.Identity,
                                  scale=np.float32(2.0))
                    self.dinc(sc, s_ac)
                for t in range(NT):
                    self.w(sc, s_dv, dv0 + t + 1)
                    sc.activation(out=esb[:, t * 8:t * 8 + 8],
                                  in_=plg[t % 2][:, :], func=AF.Exp,
                                  bias=esb[:, 64 + t:65 + t])
                    self.dinc(sc, s_ac)

            @bl.vector
            def _(v):
                for t in range(NT):
                    self.w(v, s_ac, ac0 + 2 + t + 1)
                    e = esb[:, t * 8:t * 8 + 8]
                    v.tensor_scalar(out=esb[:, 80:88], in0=e, scalar1=1.0,
                                    scalar2=None, op0=AL.is_lt)
                    v.drain()
                    v.tensor_tensor(out=esb[:, 88:96], in0=e, in1=esb[:, 80:88],
                                    op=AL.mult)
                    v.drain()
                    v.tensor_reduce(out=esb[:, 72:73], in_=esb[:, 88:96],
                                    axis=AX.X, op=AL.max)
                    v.drain()
                    v.tensor_scalar(out=esb[:, 73:74], in0=esb[:, 72:73],
                                    scalar1=1.0, scalar2=None, op0=AL.add)
                    v.drain()
                    v.reciprocal(out=esb[:, 74:75], in_=esb[:, 73:74])
                    v.tensor_scalar(out=esb[:, 80:88], in0=e,
                                    scalar1=esb[:, 72:73], scalar2=None,
                                    op0=AL.is_ge)
                    v.drain()
                    v.scalar_tensor_tensor(out=csb[:, t * 8:t * 8 + 8], in0=e,
                                           scalar=esb[:, 74:75],
                                           in1=esb[:, 80:88],
                                           op0=AL.mult, op1=AL.mult)
                    self.dinc(v, s_ve)
            router_ve = s_ve.n

        with ExitStack() as ps:
            pg_ = [self.psum(ps, f"moe_pg{i}", [128, 512]) for i in range(3)]
            pt12 = self.psum(ps, "moe_pt12", [16, 512])
            py = self.psum(ps, "moe_py", [128, 2048])

            gu0, sl0, h0 = m_gu.n, m_sl.n, m_h.n
            t10, t1e0, t20, t2e0 = m_t1.n, m_t1e.n, m_t2.n, m_t2e.n
            y0, ys0 = m_y.n, m_ys.n
            w0 = s_w.n

            wl0 = self.s_wl.n
            wd0 = self.s_wd.n
            wg0 = [s.n for s in self.s_wgs]
            wlora = [wl0 + 64 * (e + 1) for e in range(8)]
            wdn = [wd0 + 128 * (e + 1) for e in range(8)]
            wtile = []
            for e in range(8):
                tl = {}
                for idx in range(16):
                    oc = (idx // 2) + 8 * (idx % 2)
                    gidx = e * 16 + idx
                    tl[oc] = (self.s_wgs[gidx % 2],
                              wg0[gidx % 2] + 16 * (gidx // 2 + 1))
                wtile.append(tl)

            @bl.gpsimd
            def _(g):
                for e in range(8):
                    if e >= 1:
                        self.w(g, m_gu, gu0 + 32 * e)
                        self.w(g, m_t2, t20 + 2 * e)
                        self.w(g, m_y, y0 + 32 * e)
                    self.dma(g, gua.ap().rearrange("p (c r) -> p c r", c=NCH),
                             d["gupAT"].rearrange("e (c p) r -> e p c r",
                                                  p=128)[e], self.s_wl)
                    self.dma(g, gub[:, :], d["gupBT"][e], self.s_wl)
                    self.dma(g, dna.ap().rearrange("p (c r) -> p c r", c=8),
                             d["dwnAT"].rearrange("e (c p) r -> e p c r",
                                                  p=128)[e], self.s_wl)
                    self.dma(g, dnb[:, :], d["dwnBT"][e], self.s_wl)
                    for idx in range(16):
                        oc = (idx // 2) + 8 * (idx % 2)
                        if e * 16 + idx >= 2:
                            pidx = e * 16 + idx - 2
                            pe, pi = pidx // 16, pidx % 16
                            poc = (pi // 2) + 8 * (pi % 2)
                            jp, half = poc % 8, poc // 8
                            self.w(g, m_gu, gu0 + pe * 32 + 4 * jp + 2 + half + 1)
                        self.dma(g, gwt[idx % 2].ap().rearrange(
                                     "p (c k) -> p c k", c=NCH),
                                 d["gupb"].rearrange(
                                     "e o (c p) k -> e o p c k", p=128)[e, oc],
                                 self.s_wgs[(e * 16 + idx) % 2])
                    for j in range(8):
                        self.dma(g, dwt[:, j * D:(j + 1) * D],
                                 d["dwn"][e].rearrange("(c p) o -> c p o",
                                                       p=128)[j], self.s_wd)

            @bl.tensor
            def _(te):
                self.w(te, s_ve, router_ve)
                for e in range(8):
                    self.w(te, self.s_wl, wlora[e])
                    if e >= 1:
                        self.w(te, m_t2e, t2e0 + 2 * e)
                    for G in range(2):
                        if G == 1:
                            self.w(te, m_t1e, t1e0 + 2 * e + 1)
                        for ch in range(NCH):
                            lm = te.matmul(pt12[:, :],
                                           gua[:, ch * 16:(ch + 1) * 16],
                                           x2[:, ch * TOK + G * 512:
                                              ch * TOK + (G + 1) * 512],
                                           start=(ch == 0),
                                           stop=(ch == NCH - 1))
                        self.inc(lm, m_t1)
                    self.w(te, m_t1e, t1e0 + 2 * e + 2)
                    for jp in range(8):
                        for G in range(2):
                            for half in range(2):
                                oc = jp + 8 * half
                                n = e * 32 + 4 * jp + 2 * G + half
                                wsem, wval = wtile[e][oc]
                                self.w(te, wsem, wval)
                                prev = n - 3
                                if prev >= 0:
                                    if prev % 2 == 0:
                                        self.w(te, m_sl, sl0 + prev // 2 + 1)
                                    else:
                                        self.w(te, m_h, h0 + prev // 2 + 1)
                                pp = pg_[n % 3]
                                buf = gwt[oc // 8]
                                for ch in range(NCH):
                                    te.matmul(pp[:, :],
                                              buf[:, ch * 128:(ch + 1) * 128],
                                              x2[:, ch * TOK + G * 512:
                                                 ch * TOK + (G + 1) * 512],
                                              start=(ch == 0), stop=False)
                                lm = te.matmul(pp[:, :],
                                               gub[:, oc * 128:(oc + 1) * 128],
                                               t1s[:, G * 512:(G + 1) * 512],
                                               start=False, stop=True)
                                self.inc(lm, m_gu)
                    for G in range(2):
                        self.w(te, m_h, h0 + e * 16 + 15 + G)
                        if G == 1:
                            self.w(te, m_t2e, t2e0 + 2 * e + 1)
                        for j in range(8):
                            lm = te.matmul(pt12[:, :],
                                           dna[:, j * 16:(j + 1) * 16],
                                           hsb[:, (G * 8 + j) * 512:
                                               (G * 8 + j + 1) * 512],
                                           start=(j == 0), stop=(j == 7))
                        self.inc(lm, m_t2)
                    self.w(te, self.s_wd, wdn[e])
                    for G in range(2):
                        self.w(te, m_t2e, t2e0 + 2 * e + G + 1)
                        for t in range(4):
                            Y = e * 8 + G * 4 + t
                            for os_ in range(4):
                                if Y >= 1:
                                    self.w(te, m_ys, ys0 + 4 * (Y - 1) + os_ + 1)
                                for j in range(8):
                                    te.matmul(py[:, os_ * 512:(os_ + 1) * 512],
                                              hsb[:, (G * 8 + j) * 512 + t * 128:
                                                  (G * 8 + j) * 512 +
                                                  (t + 1) * 128],
                                              dwt[:, j * D + os_ * 512:
                                                  j * D + (os_ + 1) * 512],
                                              start=(j == 0), stop=False)
                                lm = te.matmul(py[:, os_ * 512:(os_ + 1) * 512],
                                               t2s[:, G * 512 + t * 128:
                                                   G * 512 + (t + 1) * 128],
                                               dnb[:, os_ * 512:(os_ + 1) * 512],
                                               start=False, stop=True)
                                self.inc(lm, m_y)

            @bl.scalar
            def _(sc):
                for e in range(8):
                    for G in range(2):
                        self.w(sc, m_t1, t10 + 2 * e + G + 1)
                        sc.activation(out=t1s[:, G * 512:(G + 1) * 512],
                                      in_=pt12[:, :], func=AF.Identity,
                                      scale=np.float32(2.0))
                        self.dinc(sc, m_t1e)
                    for P in range(16):
                        jp, G = P // 2, P % 2
                        gunit = e * 32 + 4 * jp + 2 * G
                        self.w(sc, m_gu, gu0 + gunit + 1)
                        Pg = e * 16 + P
                        if Pg >= 2:
                            self.w(sc, m_h, h0 + Pg - 1)
                        sc.activation(out=ssb[:, (Pg % 2) * 512:
                                              (Pg % 2 + 1) * 512],
                                      in_=pg_[(e * 32 + 4 * jp + 2 * G) % 3][:, :],
                                      func=AF.Silu)
                        self.dinc(sc, m_sl)
                    for G in range(2):
                        self.w(sc, m_t2, t20 + 2 * e + G + 1)
                        sc.activation(out=t2s[:, G * 512:(G + 1) * 512],
                                      in_=pt12[:, :], func=AF.Identity,
                                      scale=np.float32(2.0))
                        self.dinc(sc, m_t2e)

            @bl.vector
            def _(v):
                for e in range(8):
                    for P in range(16):
                        jp, G = P // 2, P % 2
                        uunit = e * 32 + 4 * jp + 2 * G + 1
                        Pg = e * 16 + P
                        self.w(v, m_gu, gu0 + uunit + 1)
                        self.w(v, m_sl, sl0 + Pg + 1)
                        v.tensor_tensor(out=hsb[:, (G * 8 + jp) * 512:
                                                (G * 8 + jp + 1) * 512],
                                        in0=pg_[(e * 32 + 4 * jp + 2 * G + 1)
                                                % 3][:, :],
                                        in1=ssb[:, (Pg % 2) * 512:
                                                (Pg % 2 + 1) * 512],
                                        op=AL.mult)
                        self.dinc(v, m_h)
                    for G in range(2):
                        for t in range(4):
                            Y = e * 8 + G * 4 + t
                            tile = G * 4 + t
                            for os_ in range(4):
                                self.w(v, m_y, y0 + 4 * Y + os_ + 1)
                                v.scalar_tensor_tensor(
                                    out=self.stream[:, tile * D + os_ * 512:
                                                    tile * D + (os_ + 1) * 512],
                                    in0=py[:, os_ * 512:(os_ + 1) * 512],
                                    scalar=csb[:, tile * 8 + e:tile * 8 + e + 1],
                                    in1=self.stream[:, tile * D + os_ * 512:
                                                    tile * D + (os_ + 1) * 512],
                                    op0=AL.mult, op1=AL.add)
                                self.dinc(v, m_ys)

            self.barrier()
        es.close()
        self.keep_es.close()

    def ph_l1ffn(self):
        bl, d = self.bl, self.d
        es = ExitStack()
        x4 = self.xfm_keep
        fwq = self.sbuf(es, "ff_wq", [128, NCH * D], BF16)
        self.fb1 = self.sbuf(es, "sb_fb1", [1, D], BF16)
        wch = [self.sbuf(es, f"ff_wc{i}", [128, D], F32) for i in range(2)]
        qa = self.sbuf(es, "ff_qa", [128, D], BF16)
        qb2 = self.sbuf(es, "ff_qb", [128, D], BF16)
        prt = self.sbuf(es, "ff_prt", [128, 24], F32)

        s_w, s_ve, s_qg, s_g = self.s_w, self.s_ve, self.s_qg, self.s_g
        ve0, qg0 = s_ve.n, s_qg.n
        wg1 = [s.n for s in self.s_wgs]

        @bl.gpsimd
        def _(g):
            for ch in range(NCH):
                if ch >= 2:
                    self.w(g, s_ve, ve0 + ch - 1)
                self.dma(g, wch[ch % 2][:, :],
                         d["fwT"][ch * 128:(ch + 1) * 128, :],
                         self.s_wgs[ch % 2])
            self.dma(g, self.fb1[:, :], d["fb1"][:, :], s_w)

        @bl.vector
        def _(v):
            for ch in range(NCH):
                self.w(v, self.s_wgs[ch % 2], wg1[ch % 2] + 16 * (ch // 2 + 1))
                v.tensor_reduce(out=prt[:, ch:ch + 1], in_=wch[ch % 2][:, :],
                                axis=AX.X, op=AL.add, apply_absolute_value=True)
                self.dinc(v, s_ve)
            v.drain()
            v.tensor_reduce(out=prt[:, 16:17], in_=prt[:, 0:16], axis=AX.X,
                            op=AL.add)
            v.drain()
            self.dinc(v, s_ve)

        onesf = self.sbuf(es, "ff_ones", [128, 2], F32)
        onesr = self.sbuf(es, "ff_onesr", [1, 128], F32)
        qg0f = s_qg.n
        ac0f = self.s_ac.n

        @bl.gpsimd
        def _(g):
            g.memset(onesf[:, :], 1.0)
            g.memset(onesr[:, :], 1.0)
            self.dinc(g, self.s_sl)
        slf = self.s_sl.n

        with ExitStack() as pstiny:
            pfs = self.psum(pstiny, "ff_ps", [128, 512], F32)

            @bl.tensor
            def _(te):
                self.w(te, s_ve, ve0 + 17)
                self.w(te, self.s_sl, slf)
                lm = te.matmul(pfs[0:1, 0:1], prt[:, 16:17], onesf[:, 0:1],
                               start=True, stop=True)
                self.inc(lm, s_qg)
                self.w(te, self.s_ac, ac0f + 1)
                lm = te.matmul(pfs[:, 4:5], onesr[0:1, :], prt[0:1, 21:22],
                               start=True, stop=True)
                self.inc(lm, s_qg)

            @bl.scalar
            def _(sc):
                self.w(sc, s_qg, qg0f + 1)
                sc.activation(out=prt[0:1, 21:22], in_=pfs[0:1, 0:1],
                              func=AF.Identity)
                self.dinc(sc, self.s_ac)
                self.w(sc, s_qg, qg0f + 2)
                sc.activation(out=prt[:, 17:18], in_=pfs[:, 4:5],
                              func=AF.Identity)
                self.dinc(sc, self.s_ac)

        @bl.vector
        def _(v):
            self.w(v, self.s_ac, ac0f + 2)
            v.tensor_scalar(out=prt[:, 18:19], in0=prt[:, 17:18],
                            scalar1=float(1.0 / (D * D)), scalar2=None,
                            op0=AL.mult)
            v.drain()
            v.tensor_scalar(out=prt[:, 19:20], in0=prt[:, 18:19],
                            scalar1=0.5, scalar2=0.5e-10,
                            op0=AL.mult, op1=AL.add)
            v.drain()
            v.tensor_scalar(out=prt[:, 20:21], in0=prt[:, 19:20],
                            scalar1=-1.0, scalar2=None, op0=AL.mult)
            self.dinc(v, s_ve)

        wg2 = [s.n for s in self.s_wgs]

        @bl.gpsimd
        def _(g):
            self.w(g, s_ve, ve0 + 17)
            for ch in range(NCH):
                if ch >= 2:
                    self.w(g, s_ve, ve0 + 18 + 3 * (ch - 1))
                self.dma(g, wch[ch % 2][:, :],
                         d["fwT"][ch * 128:(ch + 1) * 128, :],
                         self.s_wgs[ch % 2])

        @bl.vector
        def _(v):
            self.w(v, s_ve, ve0 + 18)
            for ch in range(NCH):
                self.w(v, self.s_wgs[ch % 2], wg2[ch % 2] + 16 * (ch // 2 + 1))
                v.tensor_scalar(out=qa[:, :], in0=wch[ch % 2][:, :],
                                scalar1=prt[:, 19:20], scalar2=prt[:, 18:19],
                                op0=AL.is_gt, op1=AL.mult)
                v.tensor_scalar(out=qb2[:, :], in0=wch[ch % 2][:, :],
                                scalar1=prt[:, 20:21], scalar2=prt[:, 18:19],
                                op0=AL.is_lt, op1=AL.mult)
                v.drain()
                v.tensor_tensor(out=fwq[:, ch * D:(ch + 1) * D], in0=qa[:, :],
                                in1=qb2[:, :], op=AL.subtract)
                self.dinc(v, s_ve, 3)

        quant_ve = s_ve.n
        ve1 = s_ve.n
        qg1 = s_qg.n

        with ExitStack() as ps:
            pop = [self.psum(ps, f"ffp{i}", [128, 2048]) for i in range(2)]

            @bl.tensor
            def _(te):
                self.w(te, s_ve, quant_ve)
                self.w(te, self.s_w)
                for t in range(NT):
                    if t >= 2:
                        self.w(te, s_ve, ve1 + t - 1)
                    pp = pop[t % 2]
                    for ch in range(NCH):
                        for os_ in range(4):
                            te.matmul(pp[:, os_ * 512:(os_ + 1) * 512],
                                      x4[:, ch * TOK + t * 128:
                                         ch * TOK + (t + 1) * 128],
                                      fwq[:, ch * D + os_ * 512:
                                          ch * D + (os_ + 1) * 512],
                                      start=(ch == 0), stop=False)
                    for os_ in range(4):
                        lm = te.matmul(pp[:, os_ * 512:(os_ + 1) * 512],
                                       self.ones[0:1, 0:128],
                                       self.fb1[0:1, os_ * 512:(os_ + 1) * 512],
                                       start=False, stop=True)
                    self.inc(lm, s_qg)

            @bl.vector
            def _(v):
                for t in range(NT):
                    self.w(v, s_qg, qg1 + t + 1)
                    v.tensor_tensor(out=self.stream[:, t * D:(t + 1) * D],
                                    in0=pop[t % 2][:, :],
                                    in1=self.stream[:, t * D:(t + 1) * D],
                                    op=AL.add)
                    self.dinc(v, s_ve)

            self.barrier()
        es.close()
        self.keep_es.close()

    def ph_out(self, y_out):
        bl = self.bl

        @bl.sync
        def _(sy):
            for t in range(NT):
                self.dma(sy, y_out[t * 128:(t + 1) * 128, :],
                         self.stream[:, t * D:(t + 1) * D], self.s_o)
            self.w(sy, self.s_o)


def kernel(**inputs):
    in_maps = _prep(inputs)
    b = Builder()
    nc = b.build()
    res = run_bass_kernel_spmd(nc, in_maps, core_ids=list(range(NCORE)))
    y = np.concatenate([np.asarray(res.results[c]["y"]) for c in range(NCORE)],
                       axis=0)
    return np.ascontiguousarray(y.reshape(2, SEQ, D), dtype=np.float32)


# version 02
# speedup vs baseline: 1.0375x; 1.0375x over previous
from contextlib import ExitStack

import numpy as np
import ml_dtypes
import concourse.bass as bass
import concourse.bass_isa as bass_isa
import concourse.mybir as mybir
from concourse.bass_utils import run_bass_kernel_spmd

F32 = mybir.dt.float32
BF16 = mybir.dt.bfloat16
U32 = mybir.dt.uint32
AL = mybir.AluOpType
AF = mybir.ActivationFunctionType
AX = mybir.AxisListType

NCORE = 8
SEQ = 4096
D = 2048
HD = 128
TOK = 1024
NT = TOK // 128
QBN = SEQ // 512
NCH = D // 128
SC = float(1.0 / np.sqrt(HD))
NEG = -1e30
f32 = np.float32

_PH = 99
_ASTOP = 99
_ARNDS = 99
_DBG = 0



def _blockmajor(w):
    i, o = w.shape
    return np.ascontiguousarray(w.reshape(i, o // 128, 128).transpose(1, 0, 2))


def _rope_tables():
    half = HD // 2
    inv = 1.0 / (10000.0 ** (np.arange(half, dtype=np.float64) / half))
    t = np.arange(SEQ, dtype=np.float64)
    fr = t[None, :] * inv[:, None]
    cos = np.cos(fr)
    sin = np.sin(fr)
    cosf = np.concatenate([cos, cos], axis=0).astype(ml_dtypes.bfloat16)
    sinsg = np.concatenate([sin, -sin], axis=0).astype(ml_dtypes.bfloat16)
    return cosf, sinsg


_PERM = np.concatenate([np.arange(0, HD, 2), np.arange(1, HD, 2)])


def _prep(inp):
    bf = ml_dtypes.bfloat16
    x = np.ascontiguousarray(np.asarray(inp["L0_attn_x"], f32).reshape(NCORE, TOK, D))

    wqT = np.ascontiguousarray(np.asarray(inp["L0_attn_Wq"], f32).T)
    wkT = np.ascontiguousarray(np.asarray(inp["L0_attn_Wk"], f32).T)
    wvT = np.ascontiguousarray(np.asarray(inp["L0_attn_Wv"], f32).T)
    woT = np.ascontiguousarray(np.asarray(inp["L0_attn_Wo"], f32).T)
    bq = np.asarray(inp["L0_attn_bq"], f32)
    bk = np.asarray(inp["L0_attn_bk"], f32)
    bv = np.asarray(inp["L0_attn_bv"], f32)
    bo = np.asarray(inp["L0_attn_bo"], f32)

    colperm = (np.arange(16)[:, None] * HD + _PERM[None, :]).reshape(-1)
    wqTp = wqT[:, colperm]
    wkTp = wkT[:, colperm]
    bqp = bq[colperm]
    bkp = bk[colperm]

    gwT = np.ascontiguousarray(np.asarray(inp["L0_ffn_gate_weight"], f32).T)
    gAT = np.ascontiguousarray(np.asarray(inp["L0_ffn_gate_lora_A"], f32).T)
    gBT = np.ascontiguousarray(np.asarray(inp["L0_ffn_gate_lora_B"], f32).T)
    gup = np.asarray(inp["L0_ffn_gate_up_proj"], f32)
    gupb = np.ascontiguousarray(gup.reshape(8, D, NCH, 128).transpose(0, 2, 1, 3))
    dwn = np.ascontiguousarray(np.asarray(inp["L0_ffn_down_proj"], f32))
    gupAT = np.ascontiguousarray(np.asarray(inp["L0_ffn_gup_lora_A"], f32).transpose(0, 2, 1))
    gupBT = np.ascontiguousarray(np.asarray(inp["L0_ffn_gup_lora_B"], f32).transpose(0, 2, 1))
    dwnAT = np.ascontiguousarray(np.asarray(inp["L0_ffn_down_lora_A"], f32).transpose(0, 2, 1))
    dwnBT = np.ascontiguousarray(np.asarray(inp["L0_ffn_down_lora_B"], f32).transpose(0, 2, 1))

    w1q = np.asarray(inp["L1_attn_wq"], f32)
    w1k = np.asarray(inp["L1_attn_wk"], f32)
    w1v = np.asarray(inp["L1_attn_wv"], f32)
    w1o = np.ascontiguousarray(np.asarray(inp["L1_attn_wo"], f32))
    fwT = np.ascontiguousarray(np.asarray(inp["L1_ffn_weight"], f32).T)
    fb = np.asarray(inp["L1_ffn_bias"], f32)

    cosf, sinsg = _rope_tables()
    ident = np.eye(128, dtype=bf)
    kk = np.arange(128)
    maskT = np.where(kk[:, None] <= kk[None, :], 0.0, NEG).astype(f32)
    ones_bf = np.ones((128, 512), dtype=bf)

    in_maps = []
    for c in range(NCORE):
        g = c % 4
        hcols = np.concatenate([np.arange(h * HD, (h + 1) * HD)
                                for h in range(4 * g, 4 * g + 4)])
        frow = np.arange(D)
        m = {
            "x_own": np.ascontiguousarray(x[c]),
            "wq0": _blockmajor(np.ascontiguousarray(wqTp[:, hcols])),
            "wk0": _blockmajor(np.ascontiguousarray(wkTp[:, hcols])),
            "wv0": _blockmajor(np.ascontiguousarray(wvT[:, hcols])),
            "bq0": np.ascontiguousarray(bqp[hcols]).reshape(1, 512),
            "bk0": np.ascontiguousarray(bkp[hcols]).reshape(1, 512),
            "bv0": np.ascontiguousarray(bv[hcols]).reshape(1, 512),
            "wo0": woT,
            "bo0": bo.reshape(1, D),
            "gwT": gwT, "gAT": gAT, "gBT": gBT,
            "gupb": gupb, "dwn": dwn,
            "gupAT": gupAT, "gupBT": gupBT, "dwnAT": dwnAT, "dwnBT": dwnBT,
            "wq1": _blockmajor(np.ascontiguousarray(w1q[:, hcols])),
            "wk1": _blockmajor(np.ascontiguousarray(w1k[:, g * HD:(g + 1) * HD])),
            "wv1": _blockmajor(np.ascontiguousarray(w1v[:, g * HD:(g + 1) * HD])),
            "wo1": w1o,
            "fwT": fwT, "fb1": fb.reshape(1, D),
            "cosf": cosf, "sinsg": sinsg,
            "ident": ident, "maskT": maskT, "ones": ones_bf,
            "oidx": np.ascontiguousarray(
                ((frow.reshape(NCH, 128) * 4 + g).astype(np.uint32)).T),
        }
        in_maps.append(m)
    return in_maps



class Sem:
    __slots__ = ("h", "n")

    def __init__(self, h):
        self.h = h
        self.n = 0


class Builder:
    def __init__(self):
        self.nc = bass.Bass("TRN2", target_bir_lowering=False)
        self.root = ExitStack()
        self.xfm_keep = None

    def sem(self, name):
        return Sem(self.root.enter_context(self.nc.semaphore(name)))

    def sbuf(self, es, name, shape, dt):
        return es.enter_context(self.nc.sbuf_tensor(name, shape, dt))

    def psum(self, es, name, shape, dt=F32):
        return es.enter_context(self.nc.psum_tensor(name, shape, dt))

    @staticmethod
    def dma(eng, out, in_, sem):
        eng.dma_start(out=out, in_=in_).then_inc(sem.h, 16)
        sem.n += 16

    @staticmethod
    def inc(instr, sem, k=1):
        instr.then_inc(sem.h, k)
        sem.n += k

    def dinc(self, eng, sem, k=1):
        eng.drain()
        self.inc(eng.nop(), sem, k)

    @staticmethod
    def w(eng, sem, val=None):
        v = sem.n if val is None else val
        if v > 0:
            eng.wait_ge(sem.h, v)

    def barrier(self):
        bl = self.bl
        s = self.s_ph
        base = s.n

        @bl.sync
        def _(e):
            self.w(e, self.s_x)
            self.w(e, self.s_o)
            for ss in self.s_xgs:
                self.w(e, ss)
            e.drain()
            e.nop().then_inc(s.h, 1)

        @bl.gpsimd
        def _(e):
            self.w(e, self.s_w)
            self.w(e, self.s_g)
            self.w(e, self.s_wd)
            self.w(e, self.s_wl)
            for ss in self.s_wgs:
                self.w(e, ss)
            e.drain()
            e.nop().then_inc(s.h, 1)

        for nm in ("tensor", "vector", "scalar"):
            @getattr(bl, nm)
            def _(e):
                e.drain()
                e.nop().then_inc(s.h, 1)
        s.n += 5

        for nm in ("tensor", "vector", "scalar", "sync", "gpsimd"):
            @getattr(bl, nm)
            def _(e):
                e.wait_ge(s.h, base + 5)

    def build(self):
        nc = self.nc
        d = {}

        def din(name, shape, dt=F32):
            d[name] = nc.dram_tensor(name, shape, dt, kind="ExternalInput").ap()

        din("x_own", [TOK, D])
        din("wq0", [4, D, 128]); din("wk0", [4, D, 128]); din("wv0", [4, D, 128])
        din("bq0", [1, 512]); din("bk0", [1, 512]); din("bv0", [1, 512])
        din("wo0", [D, D]); din("bo0", [1, D])
        din("gwT", [D, 8]); din("gAT", [D, 16]); din("gBT", [16, 8])
        din("gupb", [8, NCH, D, 128]); din("dwn", [8, 1024, D])
        din("gupAT", [8, D, 16]); din("gupBT", [8, 16, D])
        din("dwnAT", [8, 1024, 16]); din("dwnBT", [8, 16, D])
        din("wq1", [4, D, 128]); din("wk1", [1, D, 128]); din("wv1", [1, D, 128])
        din("wo1", [D, D])
        din("fwT", [D, D]); din("fb1", [1, D])
        din("cosf", [128, SEQ], BF16); din("sinsg", [128, SEQ], BF16)
        din("ident", [128, 128], BF16); din("maskT", [128, 128], F32)
        din("ones", [128, 512], BF16)
        din("oidx", [128, NCH], U32)
        self.d = d

        y_out = nc.dram_tensor("y", [TOK, D], F32, kind="ExternalOutput").ap()
        if _DBG:
            self.dbg = nc.dram_tensor("dbg", [128, 6 * SEQ], F32,
                                      kind="ExternalOutput").ap()

        self.RG = [[0, 1, 2, 3], [4, 5, 6, 7]]
        cc = {}
        for L in (0, 1):
            cc[f"xin{L}"] = nc.dram_tensor(f"cc_xin{L}", [D, TOK], BF16)
            cc[f"xout{L}"] = nc.dram_tensor(f"cc_xout{L}", [4 * D, TOK], BF16)
            cc[f"ain{L}"] = nc.dram_tensor(f"cc_ain{L}", [512, SEQ], BF16)
            cc[f"aout{L}"] = nc.dram_tensor(f"cc_aout{L}", [4 * 512, SEQ], BF16)
        self.cc = cc

        es = self.root
        self.stream = self.sbuf(es, "stream", [128, NT * D], F32)
        self.ident = self.sbuf(es, "sb_ident", [128, 128], BF16)
        self.maskT = self.sbuf(es, "sb_maskT", [128, 128], F32)
        self.ones = self.sbuf(es, "sb_ones", [128, 512], BF16)
        self.oidx = self.sbuf(es, "sb_oidx", [128, NCH], U32)
        self.bq0 = self.sbuf(es, "sb_bq0", [1, 512], BF16)
        self.bk0 = self.sbuf(es, "sb_bk0", [1, 512], BF16)
        self.bv0 = self.sbuf(es, "sb_bv0", [1, 512], BF16)
        self.gw = self.sbuf(es, "sb_gw", [128, NCH * 8], BF16)
        self.gA = self.sbuf(es, "sb_gA", [128, NCH * 16], BF16)
        self.gB = self.sbuf(es, "sb_gB", [16, 8], BF16)

        for nm in ("ph", "w", "x", "g", "o", "ck", "qg", "aq", "tp", "af",
                   "ve", "ac", "st", "ex", "pv", "dv", "cc", "nr",
                   "gu", "sl", "h", "t1", "t1e", "t2", "t2e", "y", "ys", "ng",
                   "wd", "wl"):
            setattr(self, "s_" + nm, self.sem("sem_" + nm))
        self.s_xgs = [self.sem(f"sem_xg{i}") for i in range(6)]
        self.s_wgs = [self.sem(f"sem_wg{i}") for i in range(2)]

        with nc.Block() as bl:
            self.bl = bl
            self.ph_init()
            if _PH >= 1:
                self.ph_ln(L=0, ag=True, keep=False, first=True)
            if _PH >= 2:
                self.ph_attn(L=0)
            if _PH >= 3:
                self.ph_outproj(L=0)
            if _PH >= 4:
                self.ph_ln(L=0, ag=False, keep=True)
                self.ph_moe()
            if _PH >= 5:
                self.ph_ln(L=1, ag=True, keep=False)
            if _PH >= 6:
                self.ph_attn(L=1)
            if _PH >= 7:
                self.ph_outproj(L=1)
            if _PH >= 8:
                self.ph_ln(L=1, ag=False, keep=True)
                self.ph_l1ffn()
            self.ph_out(y_out)

        self.root.close()
        return nc

    def ph_init(self):
        bl, d = self.bl, self.d

        @bl.sync
        def _(sy):
            for t in range(NT):
                self.dma(sy, self.stream[:, t * D:(t + 1) * D],
                         d["x_own"][t * 128:(t + 1) * 128, :], self.s_x)
            self.dma(sy, self.ident[:, :], d["ident"][:, :], self.s_x)
            self.dma(sy, self.maskT[:, :], d["maskT"][:, :], self.s_x)
            self.dma(sy, self.ones[:, :], d["ones"][:, :], self.s_x)
            self.dma(sy, self.oidx[:, :], d["oidx"][:, :], self.s_x)

        @bl.gpsimd
        def _(g):
            self.dma(g, self.bq0[:, :], d["bq0"][:, :], self.s_w)
            self.dma(g, self.bk0[:, :], d["bk0"][:, :], self.s_w)
            self.dma(g, self.bv0[:, :], d["bv0"][:, :], self.s_w)
            self.dma(g, self.gw.ap().rearrange("p (c e) -> p c e", c=NCH),
                     d["gwT"].rearrange("(c p) e -> p c e", p=128), self.s_w)
            self.dma(g, self.gA.ap().rearrange("p (c e) -> p c e", c=NCH),
                     d["gAT"].rearrange("(c p) e -> p c e", p=128), self.s_w)
            self.dma(g, self.gB[:, :], d["gBT"][:, :], self.s_w)
        self.barrier()

    def ph_ln(self, L, ag, keep, first=False):
        bl, nc = self.bl, self.nc
        tag = f"ln{L}{int(keep)}{int(ag)}"
        if keep:
            self.keep_es = ExitStack()
            self.xfm_keep = self.keep_es.enter_context(
                nc.sbuf_tensor("xfm_keep" + tag, [128, NCH * TOK], BF16))
        es = ExitStack()
        stat = self.sbuf(es, tag + "_stat", [128, 64], F32)
        xtm = self.sbuf(es, tag + "_xtm", [128, 2 * D], BF16)
        if keep:
            xfm = self.xfm_keep
        else:
            xfm = self.sbuf(es, tag + "_xfm", [128, NCH * TOK], BF16)
        pT = [self.psum(es, tag + f"_pT{i}", [128, 512], BF16) for i in range(2)]

        s_ve, s_ac, s_tp, s_nr = self.s_ve, self.s_ac, self.s_tp, self.s_nr
        ve0, ac0, tp0, nr0 = s_ve.n, s_ac.n, s_tp.n, s_nr.n
        stream = self.stream

        @bl.vector
        def _(v):
            if first:
                self.w(v, self.s_x, 12 * 16)
            for t in range(NT):
                for j in range(4):
                    v.bn_stats(out=stat[:, 16 + 6 * j:22 + 6 * j],
                               in_=stream[:, t * D + j * 512:t * D + (j + 1) * 512])
                v.drain()
                v.bn_aggr(out=stat[:, 2 * t:2 * t + 2], in_=stat[:, 16:40])
                v.drain()
            self.dinc(v, s_ve)

        @bl.scalar
        def _(sc):
            self.w(sc, s_ve, ve0 + 1)
            varv = stat.ap()[:, 1:17:2]
            sc.activation(out=stat[:, 40:48], in_=varv, func=AF.Ln,
                          bias=np.float32(1e-5))
            sc.drain()
            sc.activation(out=stat[:, 48:56], in_=stat[:, 40:48], func=AF.Exp,
                          scale=np.float32(-0.5))
            self.dinc(sc, s_ac)

        @bl.vector
        def _(v):
            self.w(v, s_ac, ac0 + 1)
            meanv = stat.ap()[:, 0:16:2]
            v.scalar_tensor_tensor(out=stat[:, 56:64], in0=meanv, scalar=-1.0,
                                   in1=stat[:, 48:56], op0=AL.mult, op1=AL.mult)
            self.dinc(v, s_ve)

        @bl.scalar
        def _(sc):
            self.w(sc, s_ve, ve0 + 2)

            def norm(t):
                if t >= 2:
                    self.w(sc, s_tp, tp0 + 4 * (t - 1))
                sc.activation(out=xtm[:, (t % 2) * D:(t % 2 + 1) * D],
                              in_=stream[:, t * D:(t + 1) * D], func=AF.Identity,
                              bias=stat[:, 56 + t:57 + t],
                              scale=stat[:, 48 + t:49 + t]).then_inc(
                                  self.s_nr.h, 1)
                self.s_nr.n += 1

            def evac(t):
                for g in range(4):
                    G = 4 * t + g
                    self.w(sc, s_tp, tp0 + G + 1)
                    sc.activation(
                        out=xfm.ap().rearrange("p (c k) -> p c k", c=NCH)[
                            :, 4 * g:4 * g + 4, t * 128:(t + 1) * 128],
                        in_=pT[G % 2].ap().rearrange("p (c k) -> p c k", c=4),
                        func=AF.Identity).then_inc(self.s_ac.h, 1)
                    self.s_ac.n += 1

            norm(0)
            norm(1)
            for t in range(2, NT):
                evac(t - 2)
                norm(t)
            evac(NT - 2)
            evac(NT - 1)

        @bl.tensor
        def _(te):
            for t in range(NT):
                self.w(te, s_nr, nr0 + t + 1)
                for g in range(4):
                    G = 4 * t + g
                    if G >= 2:
                        self.w(te, s_ac, ac0 + 1 + (G - 1))
                    for ch in range(4 * g, 4 * g + 4):
                        lm = te.transpose(
                            pT[G % 2][:, (ch % 4) * 128:(ch % 4 + 1) * 128],
                            xtm[:, (t % 2) * D + ch * 128:
                                (t % 2) * D + (ch + 1) * 128],
                            self.ident[:, :])
                    self.inc(lm, s_tp)

        if ag:
            @bl.sync
            def _(sy):
                self.w(sy, s_ac, ac0 + 1 + 32)
                for ch in range(NCH):
                    self.dma(sy, self.cc[f"xin{L}"].ap()[ch * 128:(ch + 1) * 128, :],
                             xfm[:, ch * TOK:(ch + 1) * TOK], self.s_o)

            @bl.gpsimd
            def _(g):
                self.w(g, self.s_o)
                self.inc(g.collective_compute(
                    "AllGather", AL.bypass, replica_groups=self.RG,
                    ins=[self.cc[f"xin{L}"].ap()],
                    outs=[self.cc[f"xout{L}"].ap()]), self.s_cc)
        self.barrier()
        es.close()

    def ph_attn(self, L):
        bl, d = self.bl, self.d
        rope = (L == 0)
        rounds = 2 if L == 0 else 1
        hpr = 2 if L == 0 else 4
        kvpr = 2 if L == 0 else 1
        nrt = hpr + kvpr
        ngrp = hpr + 2 * kvpr
        wn = ("wq0", "wk0", "wv0") if L == 0 else ("wq1", "wk1", "wv1")
        xg_dram = self.cc[f"xout{L}"].ap()
        a_in = self.cc[f"ain{L}"].ap()

        es = ExitStack()
        XSL = 6
        xg = self.sbuf(es, f"at{L}_xg", [128, XSL * 512], BF16)
        wq = self.sbuf(es, f"at{L}_wq", [128, hpr * NCH * 128], BF16)
        wk = self.sbuf(es, f"at{L}_wk", [128, kvpr * NCH * 128], BF16)
        wv = self.sbuf(es, f"at{L}_wv", [128, kvpr * NCH * 128], BF16)
        qbuf = self.sbuf(es, f"at{L}_q", [128, hpr * SEQ], BF16)
        kbuf = self.sbuf(es, f"at{L}_k", [128, kvpr * SEQ], BF16)
        vbuf = self.sbuf(es, f"at{L}_v", [128, kvpr * 32 * 132], BF16)
        if rope:
            tcos = self.sbuf(es, "at_cos", [128, SEQ], BF16)
            tsin = self.sbuf(es, "at_sin", [128, SEQ], BF16)
        if rope:
            qscr = self.sbuf(es, "at_qscr", [128, nrt * 512], BF16)
            tA = self.sbuf(es, "at_tA", [128, nrt * 512], BF16)
            tB = self.sbuf(es, "at_tB", [128, nrt * 512], BF16)
        vfm = self.sbuf(es, f"at{L}_vfm", [128, kvpr * 512], BF16)
        sq = self.sbuf(es, f"at{L}_sq", [128, 1024], BF16)
        kmax = self.sbuf(es, f"at{L}_km", [1, 2 * kvpr * 16], F32)
        negm = self.sbuf(es, f"at{L}_ng", [1, SEQ], BF16)
        ntmp = self.sbuf(es, f"at{L}_nt", [1, 1024], F32)
        pt = [self.sbuf(es, f"at{L}_pt{i}", [128, 512], BF16) for i in range(2)]
        atm = self.sbuf(es, f"at{L}_atm", [128, 512], BF16)
        afm = self.sbuf(es, f"at{L}_afm", [128, 512], BF16)
        lrec = self.sbuf(es, f"at{L}_lr", [128, 8], F32)

        s_w, s_x, s_ck, s_qg, s_aq = self.s_w, self.s_x, self.s_ck, self.s_qg, self.s_aq
        s_tp, s_af, s_ve, s_ac, s_nr = self.s_tp, self.s_af, self.s_ve, self.s_ac, self.s_nr
        s_st, s_ex, s_pv, s_dv, s_o, s_g = (self.s_st, self.s_ex, self.s_pv,
                                            self.s_dv, self.s_o, self.s_g)

        @bl.gpsimd
        def _(g):
            if rope:
                self.dma(g, tcos[:, :], d["cosf"][:, :], s_w)
                self.dma(g, tsin[:, :], d["sinsg"][:, :], s_w)
            for kv in range(kvpr):
                g.memset(vbuf.ap().rearrange(
                    "p (kv s c) -> p kv s c", kv=kvpr, s=32)[:, kv, :, 128:129],
                    1.0)
            self.dinc(g, self.s_sl)
        g_ready = self.s_sl.n

        ckP = s_ck.n
        xgP = [s.n for s in self.s_xgs]

        for rnd in range(min(rounds, _ARNDS)):
            @bl.gpsimd
            def _(g, rnd=rnd):
                if rnd > 0 and _ASTOP >= 1:
                    self.w(g, s_qg, self.att_qg_end)
                for i in range(hpr):
                    self.dma(g, wq.ap().rearrange("p (i c k) -> p i c k",
                                                  i=hpr, c=NCH)[:, i],
                             d[wn[0]].rearrange("h (c p) k -> h p c k", p=128)[
                                 rnd * hpr + i], s_w)
                for i in range(kvpr):
                    self.dma(g, wk.ap().rearrange("p (i c k) -> p i c k",
                                                  i=kvpr, c=NCH)[:, i],
                             d[wn[1]].rearrange("h (c p) k -> h p c k", p=128)[
                                 rnd * kvpr + i], s_w)
                    self.dma(g, wv.ap().rearrange("p (i c k) -> p i c k",
                                                  i=kvpr, c=NCH)[:, i],
                             d[wn[2]].rearrange("h (c p) k -> h p c k", p=128)[
                                 rnd * kvpr + i], s_w)
            w_ready = s_w.n

            if _ASTOP < 1:
                continue
            with ExitStack() as ps:
                pq = [self.psum(ps, f"a{L}{rnd}pq{i}", [128, 512])
                      for i in range(hpr)]
                pk = [self.psum(ps, f"a{L}{rnd}pk{i}", [128, 512])
                      for i in range(kvpr)]
                pv = [self.psum(ps, f"a{L}{rnd}pv{i}", [128, 512])
                      for i in range(kvpr)]
                pTq = self.psum(ps, f"a{L}{rnd}pT", [128, 512], BF16)

                qg0, aq0, tp0, af0, ve0 = (s_qg.n, s_aq.n, s_tp.n, s_af.n,
                                           s_ve.n)

                cc_t = self.s_cc.n

                @bl.sync
                def _(sy, rnd=rnd, cc_t=cc_t):
                    self.w(sy, self.s_cc, cc_t)
                    for tb in range(QBN):
                        rb, half = tb // 2, tb % 2
                        for ch in range(NCH):
                            n = rnd * QBN * NCH + tb * NCH + ch
                            if n >= XSL:
                                self.w(sy, s_ck, ckP + n - XSL + 1)
                            self.dma(sy, xg[:, (n % XSL) * 512:(n % XSL + 1) * 512],
                                     xg_dram[rb * D + ch * 128:
                                             rb * D + (ch + 1) * 128,
                                             half * 512:(half + 1) * 512],
                                     self.s_xgs[n % XSL])

                ve_cur, af_cur = s_ve.n, s_af.n

                @bl.tensor
                def _(te, rnd=rnd):
                    self.w(te, s_w, w_ready)
                    self.w(te, self.s_sl, g_ready)
                    self.w(te, s_ve, ve_cur)
                    self.w(te, s_af, af_cur)
                    for tb in range(QBN):
                        if tb >= 1:
                            self.w(te, s_aq, aq0 + tb)
                        for ch in range(NCH):
                            n = rnd * QBN * NCH + tb * NCH + ch
                            self.w(te, self.s_xgs[n % XSL],
                                   xgP[n % XSL] + 16 * (n // XSL + 1))
                            xsl = xg[:, (n % XSL) * 512:(n % XSL + 1) * 512]
                            st = (ch == 0)
                            last = (ch == NCH - 1) and (L == 1)
                            for i in range(hpr):
                                lm = te.matmul(pq[i][:, :],
                                               wq[:, (i * NCH + ch) * 128:
                                                  (i * NCH + ch + 1) * 128],
                                               xsl, start=st, stop=last)
                            for i in range(kvpr):
                                lm = te.matmul(pk[i][:, :],
                                               wk[:, (i * NCH + ch) * 128:
                                                  (i * NCH + ch + 1) * 128],
                                               xsl, start=st, stop=last)
                                lm = te.matmul(pv[i][:, :],
                                               wv[:, (i * NCH + ch) * 128:
                                                  (i * NCH + ch + 1) * 128],
                                               xsl, start=st, stop=last)
                            self.inc(lm, s_ck)
                        if L == 0:
                            for i in range(hpr):
                                lm = te.matmul(pq[i][:, :],
                                               self.bq0[0:1, (rnd * hpr + i) * 128:
                                                        (rnd * hpr + i + 1) * 128],
                                               self.ones[0:1, 0:512],
                                               start=False, stop=True)
                            for i in range(kvpr):
                                lm = te.matmul(pk[i][:, :],
                                               self.bk0[0:1, (rnd * kvpr + i) * 128:
                                                        (rnd * kvpr + i + 1) * 128],
                                               self.ones[0:1, 0:512],
                                               start=False, stop=True)
                                lm = te.matmul(pv[i][:, :],
                                               self.bv0[0:1, (rnd * kvpr + i) * 128:
                                                        (rnd * kvpr + i + 1) * 128],
                                               self.ones[0:1, 0:512],
                                               start=False, stop=True)
                            self.inc(lm, s_qg)
                        else:
                            te.drain()
                            self.inc(te.nop(), s_qg)
                        for kv in range(kvpr):
                            self.w(te, s_aq, aq0 + tb + 1)
                            if tb * kvpr + kv >= 1:
                                self.w(te, s_af, af0 + tb * kvpr + kv)
                            for sub in range(4):
                                lm = te.transpose(
                                    pTq[:, sub * 128:(sub + 1) * 128],
                                    vfm[:, kv * 512 + sub * 128:
                                        kv * 512 + (sub + 1) * 128],
                                    self.ident[:, :])
                            self.inc(lm, s_tp)
                self.att_qg_end = s_qg.n

                @bl.scalar
                def _(sc, rnd=rnd):
                    for tb in range(QBN):
                        self.w(sc, s_qg, qg0 + tb + 1)
                        if rope:
                            if tb >= 1:
                                self.w(sc, s_ve, ve0 + tb)
                            for i in range(hpr):
                                sc.activation(out=qscr[:, i * 512:(i + 1) * 512],
                                              in_=pq[i][:, :], func=AF.Identity)
                            for i in range(kvpr):
                                sc.activation(
                                    out=qscr[:, (hpr + i) * 512:
                                             (hpr + i + 1) * 512],
                                    in_=pk[i][:, :], func=AF.Identity)
                        else:
                            for i in range(hpr):
                                sc.activation(out=qbuf[:, i * SEQ + tb * 512:
                                                       i * SEQ + (tb + 1) * 512],
                                              in_=pq[i][:, :], func=AF.Identity)
                            for i in range(kvpr):
                                sc.activation(out=kbuf[:, i * SEQ + tb * 512:
                                                       i * SEQ + (tb + 1) * 512],
                                              in_=pk[i][:, :], func=AF.Identity)
                        for kv in range(kvpr):
                            li = sc.activation(out=vfm[:, kv * 512:(kv + 1) * 512],
                                               in_=pv[kv][:, :],
                                               func=AF.Identity)
                        self.inc(li, s_aq)
                        for kv in range(kvpr):
                            self.w(sc, s_tp, tp0 + tb * kvpr + kv + 1)
                            li = sc.activation(
                                out=vbuf.ap().rearrange(
                                    "p (kv s c) -> p kv s c", kv=kvpr, s=32)[
                                    :, kv, tb * 4:tb * 4 + 4, 0:128],
                                in_=pTq.ap().rearrange("p (s c) -> p s c", s=4),
                                func=AF.Identity)
                            self.inc(li, s_af)

                if rope:
                    @bl.vector
                    def _(v, rnd=rnd):
                        for tb in range(QBN):
                            self.w(v, s_aq, aq0 + tb + 1)
                            tc = tcos[:, tb * 512:(tb + 1) * 512]
                            tslo = tsin[64:128, tb * 512:(tb + 1) * 512]
                            tshi = tsin[0:64, tb * 512:(tb + 1) * 512]
                            for i in range(4):
                                v.tensor_tensor(out=tA[:, i * 512:(i + 1) * 512],
                                                in0=qscr[:, i * 512:(i + 1) * 512],
                                                in1=tc, op=AL.mult)
                            for i in range(4):
                                v.tensor_tensor(
                                    out=tB[0:64, i * 512:(i + 1) * 512],
                                    in0=qscr[64:128, i * 512:(i + 1) * 512],
                                    in1=tslo, op=AL.mult)
                            for i in range(4):
                                v.tensor_tensor(
                                    out=tB[64:128, i * 512:(i + 1) * 512],
                                    in0=qscr[0:64, i * 512:(i + 1) * 512],
                                    in1=tshi, op=AL.mult)
                            v.drain()
                            for i in range(4):
                                dst = (qbuf[:, i * SEQ + tb * 512:
                                            i * SEQ + (tb + 1) * 512] if i < hpr
                                       else kbuf[:, (i - hpr) * SEQ + tb * 512:
                                                 (i - hpr) * SEQ + (tb + 1) * 512])
                                li = v.tensor_tensor(
                                    out=dst,
                                    in0=tA[:, i * 512:(i + 1) * 512],
                                    in1=tB[:, i * 512:(i + 1) * 512],
                                    op=AL.add)
                            self.inc(li, s_ve)
                qkv_ve_end = s_ve.n
                qkv_af_end = s_af.n

            if _DBG and rnd == 0:
                @bl.gpsimd
                def _(sy):
                    self.w(sy, s_ve)
                    self.w(sy, s_af)
                    self.dma(sy, self.dbg[:, 0:SEQ], qbuf[:, 0:SEQ], self.s_w)
                    self.dma(sy, self.dbg[:, SEQ:2 * SEQ], kbuf[:, 0:SEQ], self.s_w)
                    dv = self.dbg[:, 2 * SEQ:2 * SEQ + 32 * 129]
                    self.dma(sy, dv.rearrange("p (s c) -> p s c", c=129),
                             vbuf.ap().rearrange("p (s c) -> p s c", c=132)[
                                 :, 0:32, 0:129], self.s_w)

            if _ASTOP < 2:
                continue
            with ExitStack() as ps:
                pnr = [self.psum(ps, f"a{L}{rnd}pn{i}", [1, 512]) for i in range(2)]
                nr0b, qg0b, af0b, ng0b = s_nr.n, s_qg.n, s_af.n, self.s_ng.n
                NJ = kvpr * QBN

                @bl.scalar
                def _(sc, rnd=rnd):
                    self.w(sc, s_ve, qkv_ve_end)
                    self.w(sc, s_af, qkv_af_end)
                    for J in range(NJ):
                        kv, j = J // QBN, J % QBN
                        if J >= 2:
                            self.w(sc, s_qg, qg0b + J - 1)
                        sc.activation(out=sq[:, (J % 2) * 512:(J % 2 + 1) * 512],
                                      in_=kbuf[:, kv * SEQ + j * 512:
                                              kv * SEQ + (j + 1) * 512],
                                      func=AF.Square).then_inc(self.s_nr.h, 1)
                        self.s_nr.n += 1
                        if J >= 1:
                            if J >= 3:
                                self.w(sc, self.s_ng, ng0b + J - 2)
                            self.w(sc, s_qg, qg0b + J)
                            sc.activation(out=ntmp[0:1, ((J - 1) % 2) * 512:
                                                   ((J - 1) % 2 + 1) * 512],
                                          in_=pnr[(J - 1) % 2][0:1, :],
                                          func=AF.Sqrt).then_inc(self.s_af.h, 1)
                            self.s_af.n += 1
                    self.w(sc, s_qg, qg0b + NJ)
                    if NJ >= 2:
                        self.w(sc, self.s_ng, ng0b + NJ - 1)
                    sc.activation(out=ntmp[0:1, ((NJ - 1) % 2) * 512:
                                           ((NJ - 1) % 2 + 1) * 512],
                                  in_=pnr[(NJ - 1) % 2][0:1, :],
                                  func=AF.Sqrt).then_inc(self.s_af.h, 1)
                    self.s_af.n += 1

                @bl.tensor
                def _(te, rnd=rnd):
                    for J in range(NJ):
                        self.w(te, s_nr, nr0b + J + 1)
                        if J >= 2:
                            self.w(te, s_af, af0b + J - 1)
                        lm = te.matmul(pnr[J % 2][0:1, :], self.ones[:, 0:1],
                                       sq[:, (J % 2) * 512:(J % 2 + 1) * 512],
                                       start=True, stop=True)
                        self.inc(lm, s_qg)

                @bl.vector
                def _(v, rnd=rnd):
                    for J in range(NJ):
                        kv, j = J // QBN, J % QBN
                        self.w(v, s_af, af0b + J + 1)
                        v.tensor_reduce(out=kmax[0:1, kv * 16 + j:kv * 16 + j + 1],
                                        in_=ntmp[0:1, (J % 2) * 512:
                                                 (J % 2 + 1) * 512],
                                        axis=AX.X,
                                        op=AL.max).then_inc(self.s_ng.h, 1)
                        self.s_ng.n += 1
                    v.drain()
                    for kv in range(kvpr):
                        for j in range(QBN):
                            v.tensor_reduce(out=kmax[0:1, kv * 16 + 8 + j:
                                                     kv * 16 + 9 + j],
                                            in_=kmax[0:1, kv * 16:kv * 16 + j + 1],
                                            axis=AX.X, op=AL.max)
                    self.dinc(v, self.s_ng)
                kmax_ng = self.s_ng.n

            with ExitStack() as ps:
                stp = [self.psum(ps, f"a{L}{rnd}st{i}", [128, 512])
                       for i in range(2)]
                dps = self.psum(ps, f"a{L}{rnd}dp", [128, 128])
                pva = self.psum(ps, f"a{L}{rnd}va", [128, 512])
                pvb = self.psum(ps, f"a{L}{rnd}vb", [128, 512])
                pTa = self.psum(ps, f"a{L}{rnd}pTa", [128, 512], BF16)
                pnr = [self.psum(ps, f"a{L}{rnd}qn{i}", [1, 512])
                       for i in range(2)]

                for p in range(hpr):
                    if _ASTOP < 3 or (_ASTOP == 4 and p > 0):
                        continue
                    kv = p if L == 0 else 0
                    pg = rnd * hpr + p

                    nr0f, qg0f, afn0, ng0f = s_nr.n, s_qg.n, s_af.n, self.s_ng.n

                    @bl.scalar
                    def _(sc, p=p):
                        if p == 0:
                            self.w(sc, self.s_ng, kmax_ng)
                        for J in range(QBN):
                            if J >= 2:
                                self.w(sc, s_qg, qg0f + J - 1)
                            sc.activation(out=sq[:, (J % 2) * 512:(J % 2 + 1) * 512],
                                          in_=qbuf[:, p * SEQ + J * 512:
                                                  p * SEQ + (J + 1) * 512],
                                          func=AF.Square).then_inc(self.s_nr.h, 1)
                            self.s_nr.n += 1
                            if J >= 1:
                                self.w(sc, self.s_ng, max(ng0f, ng0f + J - 2))
                                self.w(sc, s_qg, qg0f + J)
                                sc.activation(out=ntmp[0:1, ((J - 1) % 2) * 512:
                                                       ((J - 1) % 2 + 1) * 512],
                                              in_=pnr[(J - 1) % 2][0:1, :],
                                              func=AF.Sqrt).then_inc(
                                                  self.s_af.h, 1)
                                self.s_af.n += 1
                        self.w(sc, s_qg, qg0f + QBN)
                        self.w(sc, self.s_ng, ng0f + QBN - 1)
                        sc.activation(out=ntmp[0:1, ((QBN - 1) % 2) * 512:
                                               ((QBN - 1) % 2 + 1) * 512],
                                      in_=pnr[(QBN - 1) % 2][0:1, :],
                                      func=AF.Sqrt).then_inc(self.s_af.h, 1)
                        self.s_af.n += 1

                    @bl.tensor
                    def _(te, p=p):
                        if p == 0:
                            self.w(te, self.s_ng, kmax_ng)
                        for J in range(QBN):
                            self.w(te, s_nr, nr0f + J + 1)
                            self.w(te, s_af, max(afn0, afn0 + J - 1))
                            lm = te.matmul(pnr[J % 2][0:1, :],
                                           self.ones[:, 0:1],
                                           sq[:, (J % 2) * 512:(J % 2 + 1) * 512],
                                           start=True, stop=True)
                            self.inc(lm, s_qg)

                    @bl.vector
                    def _(v, p=p, kv=kv):
                        for J in range(QBN):
                            self.w(v, s_af, afn0 + J + 1)
                            v.tensor_scalar(
                                out=negm[0:1, J * 512:(J + 1) * 512],
                                in0=ntmp[0:1, (J % 2) * 512:(J % 2 + 1) * 512],
                                scalar1=kmax[0:1, kv * 16 + 8 + J:
                                             kv * 16 + 9 + J],
                                scalar2=-1.02,
                                op0=AL.mult, op1=AL.mult).then_inc(self.s_ng.h, 1)
                            self.s_ng.n += 1
                    negm_ng = self.s_ng.n

                    st0, ex0, pv0, dv0 = s_st.n, s_ex.n, s_pv.n, s_dv.n
                    ve0f, tp0f, af0f, o0f = s_ve.n, s_tp.n, s_af.n, s_o.n
                    if _ASTOP < 4:
                        continue

                    units = []
                    for qb in range(QBN):
                        for kb in range(qb + 1):
                            for i in range(4):
                                units.append((qb, kb, i))
                    NU = len(units)
                    diag_idx = {}
                    nd = 0
                    for u, (qb, kb, i) in enumerate(units):
                        if kb == qb:
                            diag_idx[u] = nd
                            nd += 1
                    qb_last = {}
                    for u, (qb, kb, i) in enumerate(units):
                        qb_last[qb] = u

                    @bl.tensor
                    def _(te, p=p, kv=kv, units=units, NU=NU, st0=st0, ex0=ex0,
                          pv0=pv0, ve0f=ve0f, af0f=af0f, qb_last=qb_last):
                        self.w(te, self.s_ng, negm_ng)
                        bank_started = set()

                        def emit_st(u):
                            qb, kb, i = units[u]
                            diag = (kb == qb)
                            cb = i * 128 if diag else 0
                            ncol = 512 - cb
                            self.w(te, s_ex, max(ex0, ex0 + u - 1))
                            te.matmul(stp[u % 2][:, 0:ncol],
                                      kbuf[:, kv * SEQ + kb * 512 + i * 128:
                                           kv * SEQ + kb * 512 + (i + 1) * 128],
                                      qbuf[:, p * SEQ + qb * 512 + cb:
                                           p * SEQ + (qb + 1) * 512],
                                      start=True, stop=False)
                            lm = te.matmul(stp[u % 2][:, 0:ncol],
                                           self.ones[0:1, 0:128],
                                           negm[0:1, qb * 512 + cb:(qb + 1) * 512],
                                           start=False, stop=True)
                            self.inc(lm, s_st)

                        def emit_pv(u):
                            qb, kb, i = units[u]
                            diag = (kb == qb)
                            cb = i * 128 if diag else 0
                            if kb == 0 and i == 0:
                                self.w(te, s_ve, ve0f + qb)
                            self.w(te, s_ex, ex0 + u + 1)
                            trange = range(i, 4) if diag else range(4)
                            for t in trange:
                                bank = pva if t < 2 else pvb
                                col = (t % 2) * 132
                                key = (qb, t < 2)
                                stt = key not in bank_started
                                bank_started.add(key)
                                lm = te.matmul(bank[:, col:col + 129],
                                               pt[u % 2][:, t * 128 - cb:
                                                         (t + 1) * 128 - cb],
                                               vbuf[:, kv * 32 * 132 +
                                                    (kb * 4 + i) * 132:
                                                    kv * 32 * 132 +
                                                    (kb * 4 + i) * 132 + 129],
                                               start=stt, stop=False,
                                               skip_group_check=True)
                            self.inc(lm, s_pv)

                        def emit_trans(qb):
                            self.w(te, s_ve, ve0f + qb + 1)
                            self.w(te, s_af, af0f + qb)
                            for sub in range(4):
                                lm = te.transpose(
                                    pTa[:, sub * 128:(sub + 1) * 128],
                                    atm[:, sub * 128:(sub + 1) * 128],
                                    self.ident[:, :])
                            self.inc(lm, s_tp)

                        pv_done = -1
                        for u in range(NU):
                            emit_st(u)
                            if u >= 1 and pv_done < u - 1:
                                emit_pv(u - 1)
                                pv_done = u - 1
                            qb = units[u][0]
                            if qb_last[qb] == u:
                                emit_pv(u)
                                pv_done = u
                                emit_trans(qb)

                    @bl.vector
                    def _(v, p=p, units=units, NU=NU, st0=st0, ex0=ex0,
                          pv0=pv0, tp0f=tp0f, diag_idx=diag_idx,
                          qb_last=qb_last):
                        for u, (qb, kb, i) in enumerate(units):
                            if kb == qb:
                                self.w(v, s_st, st0 + u + 1)
                                self.w(v, s_ex, max(ex0, ex0 + u))
                                li = v.tensor_tensor(out=dps[:, :],
                                                     in0=stp[u % 2][:, 0:128],
                                                     in1=self.maskT[:, :],
                                                     op=AL.add)
                                self.inc(li, s_dv)
                            if qb_last[qb] == u:
                                self.w(v, s_pv, pv0 + u + 1)
                                self.w(v, s_tp, tp0f + qb)
                                for t in range(4):
                                    bank = pva if t < 2 else pvb
                                    col = (t % 2) * 132
                                    v.reciprocal(out=lrec[:, t:t + 1],
                                                 in_=bank[:, col + 128:col + 129])
                                v.drain()
                                for t in range(4):
                                    bank = pva if t < 2 else pvb
                                    col = (t % 2) * 132
                                    li = v.tensor_scalar(
                                        out=atm[:, t * 128:(t + 1) * 128],
                                        in0=bank[:, col:col + 128],
                                        scalar1=lrec[:, t:t + 1],
                                        scalar2=None, op0=AL.mult)
                                self.inc(li, s_ve)

                    @bl.scalar
                    def _(sc, p=p, units=units, NU=NU, st0=st0, pv0=pv0,
                          dv0=dv0, tp0f=tp0f, af0f=af0f, o0f=o0f,
                          diag_idx=diag_idx, qb_last=qb_last):
                        for u, (qb, kb, i) in enumerate(units):
                            diag = (kb == qb)
                            cb = i * 128 if diag else 0
                            ncol = 512 - cb
                            self.w(sc, s_st, st0 + u + 1)
                            if u >= 2:
                                self.w(sc, s_pv, pv0 + u - 1)
                            if diag:
                                self.w(sc, s_dv, dv0 + diag_idx[u] + 1)
                                li = sc.activation(out=pt[u % 2][:, 0:128],
                                                   in_=dps[:, :], func=AF.Exp,
                                                   scale=np.float32(SC))
                                if ncol > 128:
                                    li = sc.activation(
                                        out=pt[u % 2][:, 128:ncol],
                                        in_=stp[u % 2][:, 128:ncol],
                                        func=AF.Exp, scale=np.float32(SC))
                            else:
                                li = sc.activation(out=pt[u % 2][:, :],
                                                   in_=stp[u % 2][:, :],
                                                   func=AF.Exp,
                                                   scale=np.float32(SC))
                            self.inc(li, s_ex)
                            if qb_last[qb] == u:
                                self.w(sc, s_tp, tp0f + qb + 1)
                                self.w(sc, s_o, o0f + 16 * qb)
                                li = sc.activation(out=afm[:, :],
                                                   in_=pTa[:, :],
                                                   func=AF.Identity)
                                self.inc(li, s_af)

                    @bl.sync
                    def _(sy, pg=pg, af0f=af0f):
                        for qb in range(QBN):
                            self.w(sy, s_af, af0f + QBN * 0 + qb + 1)
                            self.dma(sy, a_in[pg * 128:(pg + 1) * 128,
                                              qb * 512:(qb + 1) * 512],
                                     afm[:, :], s_o)

        if _ASTOP >= 6:
            @bl.gpsimd
            def _(g):
                self.w(g, self.s_o)
                self.inc(g.collective_compute(
                    "AllGather", AL.bypass, replica_groups=self.RG,
                    ins=[self.cc[f"ain{L}"].ap()],
                    outs=[self.cc[f"aout{L}"].ap()]), self.s_cc)
        self.barrier()
        es.close()

    def ph_outproj(self, L):
        bl, d = self.bl, self.d
        es = ExitStack()
        wo = self.sbuf(es, f"op{L}_wo", [128, NCH * D], BF16)
        ao = self.sbuf(es, f"op{L}_ao", [128, NCH * TOK], BF16)
        if L == 0:
            self.bo0 = self.sbuf(es, "sb_bo0", [1, D], BF16)
        aout_view = self.cc[f"aout{L}"].ap().rearrange("r (q t) -> (r q) t", q=4)

        s_w, s_g, s_qg, s_ve = self.s_w, self.s_g, self.s_qg, self.s_ve
        qg0, ve0 = s_qg.n, s_ve.n
        won = "wo0" if L == 0 else "wo1"

        @bl.gpsimd
        def _(g):
            self.w(g, self.s_cc)
            if L == 0:
                self.dma(g, self.bo0[:, :], d["bo0"][:, :], s_w)
            for ch in range(NCH):
                self.dma(g, wo[:, ch * D:(ch + 1) * D],
                         d[won][ch * 128:(ch + 1) * 128, :], s_w)
            for ch in range(NCH):
                g.indirect_dma_start(
                    out=ao[:, ch * TOK:(ch + 1) * TOK], out_offset=None,
                    in_=aout_view,
                    in_offset=bass.IndirectOffsetOnAxis(
                        self.oidx[:, ch:ch + 1], 0)).then_inc(self.s_g.h, 16)
                self.s_g.n += 16

        with ExitStack() as ps:
            pop = [self.psum(ps, f"op{L}p{i}", [128, 2048]) for i in range(2)]

            @bl.tensor
            def _(te):
                self.w(te, s_w)
                self.w(te, s_g)
                for t in range(NT):
                    if t >= 2:
                        self.w(te, s_ve, ve0 + t - 1)
                    pp = pop[t % 2]
                    for ch in range(NCH):
                        for os_ in range(4):
                            lm = te.matmul(pp[:, os_ * 512:(os_ + 1) * 512],
                                           ao[:, ch * TOK + t * 128:
                                              ch * TOK + (t + 1) * 128],
                                           wo[:, ch * D + os_ * 512:
                                              ch * D + (os_ + 1) * 512],
                                           start=(ch == 0),
                                           stop=(L == 1 and ch == NCH - 1))
                    if L == 0:
                        for os_ in range(4):
                            lm = te.matmul(pp[:, os_ * 512:(os_ + 1) * 512],
                                           self.ones[0:1, 0:128],
                                           self.bo0[0:1, os_ * 512:
                                                    (os_ + 1) * 512],
                                           start=False, stop=True)
                    self.inc(lm, s_qg)

            @bl.vector
            def _(v):
                for t in range(NT):
                    self.w(v, s_qg, qg0 + t + 1)
                    v.tensor_tensor(out=self.stream[:, t * D:(t + 1) * D],
                                    in0=pop[t % 2][:, :],
                                    in1=self.stream[:, t * D:(t + 1) * D],
                                    op=AL.add).then_inc(self.s_ve.h, 1)
                    self.s_ve.n += 1

            self.barrier()
        es.close()

    def ph_moe(self):
        bl, d = self.bl, self.d
        es = ExitStack()
        x2 = self.xfm_keep
        csb = self.sbuf(es, "moe_c", [128, 64], F32)
        esb = self.sbuf(es, "moe_e", [128, 96], F32)
        trs = self.sbuf(es, "moe_tr", [16, 1024], BF16)
        gwt = [self.sbuf(es, f"moe_gw{i}", [128, NCH * 128], BF16)
               for i in range(2)]
        dwt = self.sbuf(es, "moe_dw", [128, 8 * D], BF16)
        gub = self.sbuf(es, "moe_gub", [16, D], BF16)
        dnb = self.sbuf(es, "moe_dnb", [16, D], BF16)
        gua = self.sbuf(es, "moe_gua", [128, NCH * 16], BF16)
        dna = self.sbuf(es, "moe_dna", [128, 8 * 16], BF16)
        hsb = self.sbuf(es, "moe_h", [128, 16 * 512], BF16)
        ssb = self.sbuf(es, "moe_s", [128, 1024], F32)
        t1s = self.sbuf(es, "moe_t1", [16, 1024], BF16)
        t2s = self.sbuf(es, "moe_t2", [16, 1024], BF16)

        s_w, s_qg, s_ve, s_ac, s_dv = (self.s_w, self.s_qg, self.s_ve,
                                       self.s_ac, self.s_dv)
        m_gu, m_sl, m_h = self.s_gu, self.s_sl, self.s_h
        m_t1, m_t1e, m_t2, m_t2e = self.s_t1, self.s_t1e, self.s_t2, self.s_t2e
        m_y, m_ys = self.s_y, self.s_ys

        with ExitStack() as ps:
            plg = [self.psum(ps, f"moe_lg{i}", [128, 8]) for i in range(2)]
            ptr = self.psum(ps, "moe_ptr", [16, 512])
            qg0, ac0, ve0, dv0 = s_qg.n, s_ac.n, s_ve.n, s_dv.n

            @bl.tensor
            def _(te):
                for G in range(2):
                    if G >= 1:
                        self.w(te, s_ac, ac0 + G)
                    for ch in range(NCH):
                        lm = te.matmul(ptr[:, :],
                                       self.gA[:, ch * 16:(ch + 1) * 16],
                                       x2[:, ch * TOK + G * 512:
                                          ch * TOK + (G + 1) * 512],
                                       start=(ch == 0), stop=(ch == NCH - 1))
                    self.inc(lm, s_qg)
                for t in range(NT):
                    self.w(te, s_ac, ac0 + 2)
                    if t >= 2:
                        self.w(te, s_dv, dv0 + t - 1)
                        self.w(te, s_ac, ac0 + t + 1)
                    pp = plg[t % 2]
                    for ch in range(NCH):
                        te.matmul(pp[:, :],
                                  x2[:, ch * TOK + t * 128:
                                     ch * TOK + (t + 1) * 128],
                                  self.gw[:, ch * 8:(ch + 1) * 8],
                                  start=(ch == 0), stop=False)
                    lm = te.matmul(pp[:, :], trs[:, t * 128:(t + 1) * 128],
                                   self.gB[:, :], start=False, stop=True)
                    self.inc(lm, s_qg)

            @bl.vector
            def _(v):
                for t in range(NT):
                    self.w(v, s_qg, qg0 + 2 + t + 1)
                    v.tensor_reduce(out=esb[:, 64 + t:65 + t],
                                    in_=plg[t % 2][:, :], axis=AX.X,
                                    op=AL.max, negate=True)
                    self.dinc(v, s_dv)

            @bl.scalar
            def _(sc):
                for G in range(2):
                    self.w(sc, s_qg, qg0 + G + 1)
                    sc.activation(out=trs[:, G * 512:(G + 1) * 512],
                                  in_=ptr[:, :], func=AF.Identity,
                                  scale=np.float32(2.0))
                    self.dinc(sc, s_ac)
                for t in range(NT):
                    self.w(sc, s_dv, dv0 + t + 1)
                    sc.activation(out=esb[:, t * 8:t * 8 + 8],
                                  in_=plg[t % 2][:, :], func=AF.Exp,
                                  bias=esb[:, 64 + t:65 + t])
                    self.dinc(sc, s_ac)

            @bl.vector
            def _(v):
                for t in range(NT):
                    self.w(v, s_ac, ac0 + 2 + t + 1)
                    e = esb[:, t * 8:t * 8 + 8]
                    v.tensor_scalar(out=esb[:, 80:88], in0=e, scalar1=1.0,
                                    scalar2=None, op0=AL.is_lt)
                    v.drain()
                    v.tensor_tensor(out=esb[:, 88:96], in0=e, in1=esb[:, 80:88],
                                    op=AL.mult)
                    v.drain()
                    v.tensor_reduce(out=esb[:, 72:73], in_=esb[:, 88:96],
                                    axis=AX.X, op=AL.max)
                    v.drain()
                    v.tensor_scalar(out=esb[:, 73:74], in0=esb[:, 72:73],
                                    scalar1=1.0, scalar2=None, op0=AL.add)
                    v.drain()
                    v.reciprocal(out=esb[:, 74:75], in_=esb[:, 73:74])
                    v.tensor_scalar(out=esb[:, 80:88], in0=e,
                                    scalar1=esb[:, 72:73], scalar2=None,
                                    op0=AL.is_ge)
                    v.drain()
                    v.scalar_tensor_tensor(out=csb[:, t * 8:t * 8 + 8], in0=e,
                                           scalar=esb[:, 74:75],
                                           in1=esb[:, 80:88],
                                           op0=AL.mult, op1=AL.mult)
                    self.dinc(v, s_ve)
            router_ve = s_ve.n

        with ExitStack() as ps:
            pg_ = [self.psum(ps, f"moe_pg{i}", [128, 512]) for i in range(3)]
            pt12 = self.psum(ps, "moe_pt12", [16, 512])
            py = self.psum(ps, "moe_py", [128, 2048])

            gu0, sl0, h0 = m_gu.n, m_sl.n, m_h.n
            t10, t1e0, t20, t2e0 = m_t1.n, m_t1e.n, m_t2.n, m_t2e.n
            y0, ys0 = m_y.n, m_ys.n
            w0 = s_w.n

            wl0 = self.s_wl.n
            wd0 = self.s_wd.n
            wg0 = [s.n for s in self.s_wgs]
            wlora = [wl0 + 64 * (e + 1) for e in range(8)]
            wdn = [wd0 + 128 * (e + 1) for e in range(8)]
            wtile = []
            for e in range(8):
                tl = {}
                for idx in range(16):
                    oc = (idx // 2) + 8 * (idx % 2)
                    gidx = e * 16 + idx
                    tl[oc] = (self.s_wgs[gidx % 2],
                              wg0[gidx % 2] + 16 * (gidx // 2 + 1))
                wtile.append(tl)

            @bl.gpsimd
            def _(g):
                for e in range(8):
                    if e >= 1:
                        self.w(g, m_gu, gu0 + 32 * e)
                        self.w(g, m_t2, t20 + 2 * e)
                        self.w(g, m_y, y0 + 32 * e)
                    self.dma(g, gua.ap().rearrange("p (c r) -> p c r", c=NCH),
                             d["gupAT"].rearrange("e (c p) r -> e p c r",
                                                  p=128)[e], self.s_wl)
                    self.dma(g, gub[:, :], d["gupBT"][e], self.s_wl)
                    self.dma(g, dna.ap().rearrange("p (c r) -> p c r", c=8),
                             d["dwnAT"].rearrange("e (c p) r -> e p c r",
                                                  p=128)[e], self.s_wl)
                    self.dma(g, dnb[:, :], d["dwnBT"][e], self.s_wl)
                    for idx in range(16):
                        oc = (idx // 2) + 8 * (idx % 2)
                        if e * 16 + idx >= 2:
                            pidx = e * 16 + idx - 2
                            pe, pi = pidx // 16, pidx % 16
                            poc = (pi // 2) + 8 * (pi % 2)
                            jp, half = poc % 8, poc // 8
                            self.w(g, m_gu, gu0 + pe * 32 + 4 * jp + 2 + half + 1)
                        self.dma(g, gwt[idx % 2].ap().rearrange(
                                     "p (c k) -> p c k", c=NCH),
                                 d["gupb"].rearrange(
                                     "e o (c p) k -> e o p c k", p=128)[e, oc],
                                 self.s_wgs[(e * 16 + idx) % 2])
                    for j in range(8):
                        self.dma(g, dwt[:, j * D:(j + 1) * D],
                                 d["dwn"][e].rearrange("(c p) o -> c p o",
                                                       p=128)[j], self.s_wd)

            @bl.tensor
            def _(te):
                self.w(te, s_ve, router_ve)
                for e in range(8):
                    self.w(te, self.s_wl, wlora[e])
                    if e >= 1:
                        self.w(te, m_t2e, t2e0 + 2 * e)
                    for G in range(2):
                        if G == 1:
                            self.w(te, m_t1e, t1e0 + 2 * e + 1)
                        for ch in range(NCH):
                            lm = te.matmul(pt12[:, :],
                                           gua[:, ch * 16:(ch + 1) * 16],
                                           x2[:, ch * TOK + G * 512:
                                              ch * TOK + (G + 1) * 512],
                                           start=(ch == 0),
                                           stop=(ch == NCH - 1))
                        self.inc(lm, m_t1)
                    self.w(te, m_t1e, t1e0 + 2 * e + 2)
                    for jp in range(8):
                        for G in range(2):
                            for half in range(2):
                                oc = jp + 8 * half
                                n = e * 32 + 4 * jp + 2 * G + half
                                wsem, wval = wtile[e][oc]
                                self.w(te, wsem, wval)
                                prev = n - 3
                                if prev >= 0:
                                    if prev % 2 == 0:
                                        self.w(te, m_sl, sl0 + prev // 2 + 1)
                                    else:
                                        self.w(te, m_h, h0 + prev // 2 + 1)
                                pp = pg_[n % 3]
                                buf = gwt[oc // 8]
                                for ch in range(NCH):
                                    te.matmul(pp[:, :],
                                              buf[:, ch * 128:(ch + 1) * 128],
                                              x2[:, ch * TOK + G * 512:
                                                 ch * TOK + (G + 1) * 512],
                                              start=(ch == 0), stop=False)
                                lm = te.matmul(pp[:, :],
                                               gub[:, oc * 128:(oc + 1) * 128],
                                               t1s[:, G * 512:(G + 1) * 512],
                                               start=False, stop=True)
                                self.inc(lm, m_gu)
                    for G in range(2):
                        self.w(te, m_h, h0 + e * 16 + 15 + G)
                        if G == 1:
                            self.w(te, m_t2e, t2e0 + 2 * e + 1)
                        for j in range(8):
                            lm = te.matmul(pt12[:, :],
                                           dna[:, j * 16:(j + 1) * 16],
                                           hsb[:, (G * 8 + j) * 512:
                                               (G * 8 + j + 1) * 512],
                                           start=(j == 0), stop=(j == 7))
                        self.inc(lm, m_t2)
                    self.w(te, self.s_wd, wdn[e])
                    for G in range(2):
                        self.w(te, m_t2e, t2e0 + 2 * e + G + 1)
                        for t in range(4):
                            Y = e * 8 + G * 4 + t
                            for os_ in range(4):
                                if Y >= 1:
                                    self.w(te, m_ys, ys0 + 4 * (Y - 1) + os_ + 1)
                                for j in range(8):
                                    te.matmul(py[:, os_ * 512:(os_ + 1) * 512],
                                              hsb[:, (G * 8 + j) * 512 + t * 128:
                                                  (G * 8 + j) * 512 +
                                                  (t + 1) * 128],
                                              dwt[:, j * D + os_ * 512:
                                                  j * D + (os_ + 1) * 512],
                                              start=(j == 0), stop=False)
                                lm = te.matmul(py[:, os_ * 512:(os_ + 1) * 512],
                                               t2s[:, G * 512 + t * 128:
                                                   G * 512 + (t + 1) * 128],
                                               dnb[:, os_ * 512:(os_ + 1) * 512],
                                               start=False, stop=True)
                                self.inc(lm, m_y)

            @bl.scalar
            def _(sc):
                for e in range(8):
                    for G in range(2):
                        self.w(sc, m_t1, t10 + 2 * e + G + 1)
                        sc.activation(out=t1s[:, G * 512:(G + 1) * 512],
                                      in_=pt12[:, :], func=AF.Identity,
                                      scale=np.float32(2.0)).then_inc(
                                          self.s_t1e.h, 1)
                        self.s_t1e.n += 1
                    for P in range(16):
                        jp, G = P // 2, P % 2
                        gunit = e * 32 + 4 * jp + 2 * G
                        self.w(sc, m_gu, gu0 + gunit + 1)
                        Pg = e * 16 + P
                        if Pg >= 2:
                            self.w(sc, m_h, h0 + Pg - 1)
                        sc.activation(out=ssb[:, (Pg % 2) * 512:
                                              (Pg % 2 + 1) * 512],
                                      in_=pg_[(e * 32 + 4 * jp + 2 * G) % 3][:, :],
                                      func=AF.Silu).then_inc(self.s_sl.h, 1)
                        self.s_sl.n += 1
                    for G in range(2):
                        self.w(sc, m_t2, t20 + 2 * e + G + 1)
                        sc.activation(out=t2s[:, G * 512:(G + 1) * 512],
                                      in_=pt12[:, :], func=AF.Identity,
                                      scale=np.float32(2.0)).then_inc(
                                          self.s_t2e.h, 1)
                        self.s_t2e.n += 1

            @bl.vector
            def _(v):
                for e in range(8):
                    for P in range(16):
                        jp, G = P // 2, P % 2
                        uunit = e * 32 + 4 * jp + 2 * G + 1
                        Pg = e * 16 + P
                        self.w(v, m_gu, gu0 + uunit + 1)
                        self.w(v, m_sl, sl0 + Pg + 1)
                        v.tensor_tensor(out=hsb[:, (G * 8 + jp) * 512:
                                                (G * 8 + jp + 1) * 512],
                                        in0=pg_[(e * 32 + 4 * jp + 2 * G + 1)
                                                % 3][:, :],
                                        in1=ssb[:, (Pg % 2) * 512:
                                                (Pg % 2 + 1) * 512],
                                        op=AL.mult).then_inc(self.s_h.h, 1)
                        self.s_h.n += 1
                    for G in range(2):
                        for t in range(4):
                            Y = e * 8 + G * 4 + t
                            tile = G * 4 + t
                            for os_ in range(4):
                                self.w(v, m_y, y0 + 4 * Y + os_ + 1)
                                v.scalar_tensor_tensor(
                                    out=self.stream[:, tile * D + os_ * 512:
                                                    tile * D + (os_ + 1) * 512],
                                    in0=py[:, os_ * 512:(os_ + 1) * 512],
                                    scalar=csb[:, tile * 8 + e:tile * 8 + e + 1],
                                    in1=self.stream[:, tile * D + os_ * 512:
                                                    tile * D + (os_ + 1) * 512],
                                    op0=AL.mult,
                                    op1=AL.add).then_inc(self.s_ys.h, 1)
                                self.s_ys.n += 1

            self.barrier()
        es.close()
        self.keep_es.close()

    def ph_l1ffn(self):
        bl, d = self.bl, self.d
        es = ExitStack()
        x4 = self.xfm_keep
        fwq = self.sbuf(es, "ff_wq", [128, NCH * D], BF16)
        self.fb1 = self.sbuf(es, "sb_fb1", [1, D], BF16)
        wch = [self.sbuf(es, f"ff_wc{i}", [128, D], F32) for i in range(2)]
        qa = self.sbuf(es, "ff_qa", [128, D], BF16)
        qb2 = self.sbuf(es, "ff_qb", [128, D], BF16)
        prt = self.sbuf(es, "ff_prt", [128, 24], F32)

        s_w, s_ve, s_qg, s_g = self.s_w, self.s_ve, self.s_qg, self.s_g
        ve0, qg0 = s_ve.n, s_qg.n
        wg1 = [s.n for s in self.s_wgs]

        @bl.gpsimd
        def _(g):
            for ch in range(NCH):
                if ch >= 2:
                    self.w(g, s_ve, ve0 + ch - 1)
                self.dma(g, wch[ch % 2][:, :],
                         d["fwT"][ch * 128:(ch + 1) * 128, :],
                         self.s_wgs[ch % 2])
            self.dma(g, self.fb1[:, :], d["fb1"][:, :], s_w)

        @bl.vector
        def _(v):
            for ch in range(NCH):
                self.w(v, self.s_wgs[ch % 2], wg1[ch % 2] + 16 * (ch // 2 + 1))
                v.tensor_reduce(out=prt[:, ch:ch + 1], in_=wch[ch % 2][:, :],
                                axis=AX.X, op=AL.add, apply_absolute_value=True)
                self.dinc(v, s_ve)
            v.drain()
            v.tensor_reduce(out=prt[:, 16:17], in_=prt[:, 0:16], axis=AX.X,
                            op=AL.add)
            v.drain()
            self.dinc(v, s_ve)

        onesf = self.sbuf(es, "ff_ones", [128, 2], F32)
        onesr = self.sbuf(es, "ff_onesr", [1, 128], F32)
        qg0f = s_qg.n
        ac0f = self.s_ac.n

        @bl.gpsimd
        def _(g):
            g.memset(onesf[:, :], 1.0)
            g.memset(onesr[:, :], 1.0)
            self.dinc(g, self.s_sl)
        slf = self.s_sl.n

        with ExitStack() as pstiny:
            pfs = self.psum(pstiny, "ff_ps", [128, 512], F32)

            @bl.tensor
            def _(te):
                self.w(te, s_ve, ve0 + 17)
                self.w(te, self.s_sl, slf)
                lm = te.matmul(pfs[0:1, 0:1], prt[:, 16:17], onesf[:, 0:1],
                               start=True, stop=True)
                self.inc(lm, s_qg)
                self.w(te, self.s_ac, ac0f + 1)
                lm = te.matmul(pfs[:, 4:5], onesr[0:1, :], prt[0:1, 21:22],
                               start=True, stop=True)
                self.inc(lm, s_qg)

            @bl.scalar
            def _(sc):
                self.w(sc, s_qg, qg0f + 1)
                sc.activation(out=prt[0:1, 21:22], in_=pfs[0:1, 0:1],
                              func=AF.Identity)
                self.dinc(sc, self.s_ac)
                self.w(sc, s_qg, qg0f + 2)
                sc.activation(out=prt[:, 17:18], in_=pfs[:, 4:5],
                              func=AF.Identity)
                self.dinc(sc, self.s_ac)

        @bl.vector
        def _(v):
            self.w(v, self.s_ac, ac0f + 2)
            v.tensor_scalar(out=prt[:, 18:19], in0=prt[:, 17:18],
                            scalar1=float(1.0 / (D * D)), scalar2=None,
                            op0=AL.mult)
            v.drain()
            v.tensor_scalar(out=prt[:, 19:20], in0=prt[:, 18:19],
                            scalar1=0.5, scalar2=0.5e-10,
                            op0=AL.mult, op1=AL.add)
            v.drain()
            v.tensor_scalar(out=prt[:, 20:21], in0=prt[:, 19:20],
                            scalar1=-1.0, scalar2=None, op0=AL.mult)
            self.dinc(v, s_ve)

        wg2 = [s.n for s in self.s_wgs]

        @bl.gpsimd
        def _(g):
            self.w(g, s_ve, ve0 + 17)
            for ch in range(NCH):
                if ch >= 2:
                    self.w(g, s_ve, ve0 + 18 + 3 * (ch - 1))
                self.dma(g, wch[ch % 2][:, :],
                         d["fwT"][ch * 128:(ch + 1) * 128, :],
                         self.s_wgs[ch % 2])

        @bl.vector
        def _(v):
            self.w(v, s_ve, ve0 + 18)
            for ch in range(NCH):
                self.w(v, self.s_wgs[ch % 2], wg2[ch % 2] + 16 * (ch // 2 + 1))
                v.tensor_scalar(out=qa[:, :], in0=wch[ch % 2][:, :],
                                scalar1=prt[:, 19:20], scalar2=prt[:, 18:19],
                                op0=AL.is_gt, op1=AL.mult)
                v.tensor_scalar(out=qb2[:, :], in0=wch[ch % 2][:, :],
                                scalar1=prt[:, 20:21], scalar2=prt[:, 18:19],
                                op0=AL.is_lt, op1=AL.mult)
                v.drain()
                v.tensor_tensor(out=fwq[:, ch * D:(ch + 1) * D], in0=qa[:, :],
                                in1=qb2[:, :], op=AL.subtract)
                self.dinc(v, s_ve, 3)

        quant_ve = s_ve.n
        ve1 = s_ve.n
        qg1 = s_qg.n

        with ExitStack() as ps:
            pop = [self.psum(ps, f"ffp{i}", [128, 2048]) for i in range(2)]

            @bl.tensor
            def _(te):
                self.w(te, s_ve, quant_ve)
                self.w(te, self.s_w)
                for t in range(NT):
                    if t >= 2:
                        self.w(te, s_ve, ve1 + t - 1)
                    pp = pop[t % 2]
                    for ch in range(NCH):
                        for os_ in range(4):
                            te.matmul(pp[:, os_ * 512:(os_ + 1) * 512],
                                      x4[:, ch * TOK + t * 128:
                                         ch * TOK + (t + 1) * 128],
                                      fwq[:, ch * D + os_ * 512:
                                          ch * D + (os_ + 1) * 512],
                                      start=(ch == 0), stop=False)
                    for os_ in range(4):
                        lm = te.matmul(pp[:, os_ * 512:(os_ + 1) * 512],
                                       self.ones[0:1, 0:128],
                                       self.fb1[0:1, os_ * 512:(os_ + 1) * 512],
                                       start=False, stop=True)
                    self.inc(lm, s_qg)

            @bl.vector
            def _(v):
                for t in range(NT):
                    self.w(v, s_qg, qg1 + t + 1)
                    v.tensor_tensor(out=self.stream[:, t * D:(t + 1) * D],
                                    in0=pop[t % 2][:, :],
                                    in1=self.stream[:, t * D:(t + 1) * D],
                                    op=AL.add).then_inc(self.s_ve.h, 1)
                    self.s_ve.n += 1

            self.barrier()
        es.close()
        self.keep_es.close()

    def ph_out(self, y_out):
        bl = self.bl

        @bl.sync
        def _(sy):
            for t in range(NT):
                self.dma(sy, y_out[t * 128:(t + 1) * 128, :],
                         self.stream[:, t * D:(t + 1) * D], self.s_o)
            self.w(sy, self.s_o)


def kernel(**inputs):
    in_maps = _prep(inputs)
    b = Builder()
    nc = b.build()
    res = run_bass_kernel_spmd(nc, in_maps, core_ids=list(range(NCORE)))
    y = np.concatenate([np.asarray(res.results[c]["y"]) for c in range(NCORE)],
                       axis=0)
    return np.ascontiguousarray(y.reshape(2, SEQ, D), dtype=np.float32)


# version 03
# speedup vs baseline: 1.0508x; 1.0128x over previous
from contextlib import ExitStack

import numpy as np
import ml_dtypes
import concourse.bass as bass
import concourse.bass_isa as bass_isa
import concourse.mybir as mybir
from concourse.bass_utils import run_bass_kernel_spmd

F32 = mybir.dt.float32
BF16 = mybir.dt.bfloat16
U32 = mybir.dt.uint32
AL = mybir.AluOpType
AF = mybir.ActivationFunctionType
AX = mybir.AxisListType

NCORE = 8
SEQ = 4096
D = 2048
HD = 128
TOK = 1024
NT = TOK // 128
QBN = SEQ // 512
NCH = D // 128
SC = float(1.0 / np.sqrt(HD))
NEG = -1e30
f32 = np.float32

_PH = 99
_ASTOP = 99
_ARNDS = 99
_DBG = 0



def _blockmajor(w):
    i, o = w.shape
    return np.ascontiguousarray(w.reshape(i, o // 128, 128).transpose(1, 0, 2))


def _rope_tables():
    half = HD // 2
    inv = 1.0 / (10000.0 ** (np.arange(half, dtype=np.float64) / half))
    t = np.arange(SEQ, dtype=np.float64)
    fr = t[None, :] * inv[:, None]
    cos = np.cos(fr)
    sin = np.sin(fr)
    cosf = np.concatenate([cos, cos], axis=0).astype(ml_dtypes.bfloat16)
    sinsg = np.concatenate([sin, -sin], axis=0).astype(ml_dtypes.bfloat16)
    return cosf, sinsg


_PERM = np.concatenate([np.arange(0, HD, 2), np.arange(1, HD, 2)])


def _prep(inp):
    bf = ml_dtypes.bfloat16
    x = np.ascontiguousarray(np.asarray(inp["L0_attn_x"], f32).reshape(NCORE, TOK, D))

    wqT = np.ascontiguousarray(np.asarray(inp["L0_attn_Wq"], f32).T)
    wkT = np.ascontiguousarray(np.asarray(inp["L0_attn_Wk"], f32).T)
    wvT = np.ascontiguousarray(np.asarray(inp["L0_attn_Wv"], f32).T)
    woT = np.ascontiguousarray(np.asarray(inp["L0_attn_Wo"], f32).T)
    bq = np.asarray(inp["L0_attn_bq"], f32)
    bk = np.asarray(inp["L0_attn_bk"], f32)
    bv = np.asarray(inp["L0_attn_bv"], f32)
    bo = np.asarray(inp["L0_attn_bo"], f32)

    colperm = (np.arange(16)[:, None] * HD + _PERM[None, :]).reshape(-1)
    wqTp = wqT[:, colperm]
    wkTp = wkT[:, colperm]
    bqp = bq[colperm]
    bkp = bk[colperm]

    gwT = np.ascontiguousarray(np.asarray(inp["L0_ffn_gate_weight"], f32).T)
    gAT = np.ascontiguousarray(np.asarray(inp["L0_ffn_gate_lora_A"], f32).T)
    gBT = np.ascontiguousarray(np.asarray(inp["L0_ffn_gate_lora_B"], f32).T)
    gup = np.asarray(inp["L0_ffn_gate_up_proj"], f32)
    gupb = np.ascontiguousarray(gup.reshape(8, D, NCH, 128).transpose(0, 2, 1, 3))
    dwn = np.ascontiguousarray(np.asarray(inp["L0_ffn_down_proj"], f32))
    gupAT = np.ascontiguousarray(np.asarray(inp["L0_ffn_gup_lora_A"], f32).transpose(0, 2, 1))
    gupBT = np.ascontiguousarray(np.asarray(inp["L0_ffn_gup_lora_B"], f32).transpose(0, 2, 1))
    dwnAT = np.ascontiguousarray(np.asarray(inp["L0_ffn_down_lora_A"], f32).transpose(0, 2, 1))
    dwnBT = np.ascontiguousarray(np.asarray(inp["L0_ffn_down_lora_B"], f32).transpose(0, 2, 1))

    w1q = np.asarray(inp["L1_attn_wq"], f32)
    w1k = np.asarray(inp["L1_attn_wk"], f32)
    w1v = np.asarray(inp["L1_attn_wv"], f32)
    w1o = np.ascontiguousarray(np.asarray(inp["L1_attn_wo"], f32))
    fwT = np.ascontiguousarray(np.asarray(inp["L1_ffn_weight"], f32).T)
    fb = np.asarray(inp["L1_ffn_bias"], f32)

    cosf, sinsg = _rope_tables()
    ident = np.eye(128, dtype=bf)
    kk = np.arange(128)
    maskT = np.where(kk[:, None] <= kk[None, :], 0.0, NEG).astype(f32)
    ones_bf = np.ones((128, 512), dtype=bf)

    in_maps = []
    for c in range(NCORE):
        g = c % 4
        hcols = np.concatenate([np.arange(h * HD, (h + 1) * HD)
                                for h in range(4 * g, 4 * g + 4)])
        frow = np.arange(D)
        m = {
            "x_own": np.ascontiguousarray(x[c]),
            "wq0": _blockmajor(np.ascontiguousarray(wqTp[:, hcols])),
            "wk0": _blockmajor(np.ascontiguousarray(wkTp[:, hcols])),
            "wv0": _blockmajor(np.ascontiguousarray(wvT[:, hcols])),
            "bq0": np.ascontiguousarray(bqp[hcols]).reshape(1, 512),
            "bk0": np.ascontiguousarray(bkp[hcols]).reshape(1, 512),
            "bv0": np.ascontiguousarray(bv[hcols]).reshape(1, 512),
            "wo0": woT,
            "bo0": bo.reshape(1, D),
            "gwT": gwT, "gAT": gAT, "gBT": gBT,
            "gupb": gupb, "dwn": dwn,
            "gupAT": gupAT, "gupBT": gupBT, "dwnAT": dwnAT, "dwnBT": dwnBT,
            "wq1": _blockmajor(np.ascontiguousarray(w1q[:, hcols])),
            "wk1": _blockmajor(np.ascontiguousarray(w1k[:, g * HD:(g + 1) * HD])),
            "wv1": _blockmajor(np.ascontiguousarray(w1v[:, g * HD:(g + 1) * HD])),
            "wo1": w1o,
            "fwT": fwT, "fb1": fb.reshape(1, D),
            "cosf": cosf, "sinsg": sinsg,
            "ident": ident, "maskT": maskT, "ones": ones_bf,
            "oidx": np.ascontiguousarray(
                ((((frow // 512) * 256 + ((frow // 128) % 2) * 128
                   + frow % 128).reshape(NCH, 128) * 4
                  + g).astype(np.uint32)).T),
        }
        in_maps.append(m)
    return in_maps



class Sem:
    __slots__ = ("h", "n")

    def __init__(self, h):
        self.h = h
        self.n = 0


class Builder:
    def __init__(self):
        self.nc = bass.Bass("TRN2", target_bir_lowering=False)
        self.root = ExitStack()
        self.xfm_keep = None

    def sem(self, name):
        return Sem(self.root.enter_context(self.nc.semaphore(name)))

    def sbuf(self, es, name, shape, dt):
        return es.enter_context(self.nc.sbuf_tensor(name, shape, dt))

    def psum(self, es, name, shape, dt=F32):
        return es.enter_context(self.nc.psum_tensor(name, shape, dt))

    @staticmethod
    def dma(eng, out, in_, sem):
        eng.dma_start(out=out, in_=in_).then_inc(sem.h, 16)
        sem.n += 16

    @staticmethod
    def inc(instr, sem, k=1):
        instr.then_inc(sem.h, k)
        sem.n += k

    def dinc(self, eng, sem, k=1):
        eng.drain()
        self.inc(eng.nop(), sem, k)

    @staticmethod
    def w(eng, sem, val=None):
        v = sem.n if val is None else val
        if v > 0:
            eng.wait_ge(sem.h, v)

    def barrier(self):
        bl = self.bl
        s = self.s_ph
        base = s.n

        @bl.sync
        def _(e):
            self.w(e, self.s_x)
            self.w(e, self.s_o)
            for ss in self.s_xgs:
                self.w(e, ss)
            e.drain()
            e.nop().then_inc(s.h, 1)

        @bl.gpsimd
        def _(e):
            self.w(e, self.s_w)
            self.w(e, self.s_g)
            self.w(e, self.s_wd)
            self.w(e, self.s_wl)
            for ss in self.s_wgs:
                self.w(e, ss)
            e.drain()
            e.nop().then_inc(s.h, 1)

        for nm in ("tensor", "vector", "scalar"):
            @getattr(bl, nm)
            def _(e):
                e.drain()
                e.nop().then_inc(s.h, 1)
        s.n += 5

        for nm in ("tensor", "vector", "scalar", "sync", "gpsimd"):
            @getattr(bl, nm)
            def _(e):
                e.wait_ge(s.h, base + 5)

    def build(self):
        nc = self.nc
        d = {}

        def din(name, shape, dt=F32):
            d[name] = nc.dram_tensor(name, shape, dt, kind="ExternalInput").ap()

        din("x_own", [TOK, D])
        din("wq0", [4, D, 128]); din("wk0", [4, D, 128]); din("wv0", [4, D, 128])
        din("bq0", [1, 512]); din("bk0", [1, 512]); din("bv0", [1, 512])
        din("wo0", [D, D]); din("bo0", [1, D])
        din("gwT", [D, 8]); din("gAT", [D, 16]); din("gBT", [16, 8])
        din("gupb", [8, NCH, D, 128]); din("dwn", [8, 1024, D])
        din("gupAT", [8, D, 16]); din("gupBT", [8, 16, D])
        din("dwnAT", [8, 1024, 16]); din("dwnBT", [8, 16, D])
        din("wq1", [4, D, 128]); din("wk1", [1, D, 128]); din("wv1", [1, D, 128])
        din("wo1", [D, D])
        din("fwT", [D, D]); din("fb1", [1, D])
        din("cosf", [128, SEQ], BF16); din("sinsg", [128, SEQ], BF16)
        din("ident", [128, 128], BF16); din("maskT", [128, 128], F32)
        din("ones", [128, 512], BF16)
        din("oidx", [128, NCH], U32)
        self.d = d

        y_out = nc.dram_tensor("y", [TOK, D], F32, kind="ExternalOutput").ap()
        if _DBG:
            self.dbg = nc.dram_tensor("dbg", [128, 6 * SEQ], F32,
                                      kind="ExternalOutput").ap()

        self.RG = [[0, 1, 2, 3], [4, 5, 6, 7]]
        cc = {}
        for L in (0, 1):
            cc[f"xin{L}"] = nc.dram_tensor(f"cc_xin{L}", [D, TOK], BF16)
            cc[f"xout{L}"] = nc.dram_tensor(f"cc_xout{L}", [4 * D, TOK], BF16)
            for hf in ("A", "B"):
                cc[f"ain{L}{hf}"] = nc.dram_tensor(f"cc_ain{L}{hf}",
                                                   [256, SEQ], BF16)
                cc[f"aout{L}{hf}"] = nc.dram_tensor(f"cc_aout{L}{hf}",
                                                    [4 * 256, SEQ], BF16)
        self.cc = cc
        self.wqd = nc.dram_tensor("wq_dram", [D, D], BF16)

        es = self.root
        self.stream = self.sbuf(es, "stream", [128, NT * D], F32)
        self.ident = self.sbuf(es, "sb_ident", [128, 128], BF16)
        self.maskT = self.sbuf(es, "sb_maskT", [128, 128], F32)
        self.ones = self.sbuf(es, "sb_ones", [128, 512], BF16)
        self.oidx = self.sbuf(es, "sb_oidx", [128, NCH], U32)
        self.bq0 = self.sbuf(es, "sb_bq0", [1, 512], BF16)
        self.bk0 = self.sbuf(es, "sb_bk0", [1, 512], BF16)
        self.bv0 = self.sbuf(es, "sb_bv0", [1, 512], BF16)
        self.gw = self.sbuf(es, "sb_gw", [128, NCH * 8], BF16)
        self.gA = self.sbuf(es, "sb_gA", [128, NCH * 16], BF16)
        self.gB = self.sbuf(es, "sb_gB", [16, 8], BF16)
        self.prt = self.sbuf(es, "qt_prt", [128, 24], F32)

        for nm in ("ph", "w", "x", "g", "o", "ck", "qg", "aq", "tp", "af",
                   "ve", "ac", "st", "ex", "pv", "dv", "cc", "nr",
                   "gu", "sl", "h", "t1", "t1e", "t2", "t2e", "y", "ys", "ng",
                   "wd", "wl"):
            setattr(self, "s_" + nm, self.sem("sem_" + nm))
        self.s_xgs = [self.sem(f"sem_xg{i}") for i in range(6)]
        self.s_wgs = [self.sem(f"sem_wg{i}") for i in range(2)]

        with nc.Block() as bl:
            self.bl = bl
            self.ph_init()
            if _PH >= 1:
                self.ph_ln(L=0, ag=True, keep=False, first=True)
            q_es = ExitStack()
            if _PH >= 8:
                self.ph_quant(q_es)
            if _PH >= 2:
                self.ph_attn(L=0)
            q_es.close()
            if _PH >= 3:
                self.ph_outproj(L=0)
            if _PH >= 4:
                self.ph_ln(L=0, ag=False, keep=True)
                self.ph_moe()
            if _PH >= 5:
                self.ph_ln(L=1, ag=True, keep=False)
            if _PH >= 6:
                self.ph_attn(L=1)
            if _PH >= 7:
                self.ph_outproj(L=1)
            if _PH >= 8:
                self.ph_ln(L=1, ag=False, keep=True)
                self.ph_l1ffn()
            self.ph_out(y_out)

        self.root.close()
        return nc

    def ph_init(self):
        bl, d = self.bl, self.d

        @bl.sync
        def _(sy):
            for t in range(NT):
                self.dma(sy, self.stream[:, t * D:(t + 1) * D],
                         d["x_own"][t * 128:(t + 1) * 128, :], self.s_x)
            self.dma(sy, self.ident[:, :], d["ident"][:, :], self.s_x)
            self.dma(sy, self.maskT[:, :], d["maskT"][:, :], self.s_x)
            self.dma(sy, self.ones[:, :], d["ones"][:, :], self.s_x)
            self.dma(sy, self.oidx[:, :], d["oidx"][:, :], self.s_x)

        @bl.gpsimd
        def _(g):
            self.dma(g, self.bq0[:, :], d["bq0"][:, :], self.s_w)
            self.dma(g, self.bk0[:, :], d["bk0"][:, :], self.s_w)
            self.dma(g, self.bv0[:, :], d["bv0"][:, :], self.s_w)
            self.dma(g, self.gw.ap().rearrange("p (c e) -> p c e", c=NCH),
                     d["gwT"].rearrange("(c p) e -> p c e", p=128), self.s_w)
            self.dma(g, self.gA.ap().rearrange("p (c e) -> p c e", c=NCH),
                     d["gAT"].rearrange("(c p) e -> p c e", p=128), self.s_w)
            self.dma(g, self.gB[:, :], d["gBT"][:, :], self.s_w)
        self.barrier()

    def ph_ln(self, L, ag, keep, first=False):
        bl, nc = self.bl, self.nc
        tag = f"ln{L}{int(keep)}{int(ag)}"
        if keep:
            self.keep_es = ExitStack()
            self.xfm_keep = self.keep_es.enter_context(
                nc.sbuf_tensor("xfm_keep" + tag, [128, NCH * TOK], BF16))
        es = ExitStack()
        stat = self.sbuf(es, tag + "_stat", [128, 64], F32)
        xtm = self.sbuf(es, tag + "_xtm", [128, 2 * D], BF16)
        if keep:
            xfm = self.xfm_keep
        else:
            xfm = self.sbuf(es, tag + "_xfm", [128, NCH * TOK], BF16)
        pT = [self.psum(es, tag + f"_pT{i}", [128, 512], BF16) for i in range(2)]

        s_ve, s_ac, s_tp, s_nr = self.s_ve, self.s_ac, self.s_tp, self.s_nr
        ve0, ac0, tp0, nr0 = s_ve.n, s_ac.n, s_tp.n, s_nr.n
        stream = self.stream

        @bl.vector
        def _(v):
            if first:
                self.w(v, self.s_x, 12 * 16)
            for t in range(NT):
                for j in range(4):
                    v.bn_stats(out=stat[:, 16 + 6 * j:22 + 6 * j],
                               in_=stream[:, t * D + j * 512:t * D + (j + 1) * 512])
                v.drain()
                v.bn_aggr(out=stat[:, 2 * t:2 * t + 2], in_=stat[:, 16:40])
                v.drain()
            self.dinc(v, s_ve)

        @bl.scalar
        def _(sc):
            self.w(sc, s_ve, ve0 + 1)
            varv = stat.ap()[:, 1:17:2]
            sc.activation(out=stat[:, 40:48], in_=varv, func=AF.Ln,
                          bias=np.float32(1e-5))
            sc.drain()
            sc.activation(out=stat[:, 48:56], in_=stat[:, 40:48], func=AF.Exp,
                          scale=np.float32(-0.5))
            self.dinc(sc, s_ac)

        @bl.vector
        def _(v):
            self.w(v, s_ac, ac0 + 1)
            meanv = stat.ap()[:, 0:16:2]
            v.scalar_tensor_tensor(out=stat[:, 56:64], in0=meanv, scalar=-1.0,
                                   in1=stat[:, 48:56], op0=AL.mult, op1=AL.mult)
            self.dinc(v, s_ve)

        @bl.scalar
        def _(sc):
            self.w(sc, s_ve, ve0 + 2)

            def norm(t):
                if t >= 2:
                    self.w(sc, s_tp, tp0 + 4 * (t - 1))
                sc.activation(out=xtm[:, (t % 2) * D:(t % 2 + 1) * D],
                              in_=stream[:, t * D:(t + 1) * D], func=AF.Identity,
                              bias=stat[:, 56 + t:57 + t],
                              scale=stat[:, 48 + t:49 + t]).then_inc(
                                  self.s_nr.h, 1)
                self.s_nr.n += 1

            def evac(t):
                for g in range(4):
                    G = 4 * t + g
                    self.w(sc, s_tp, tp0 + G + 1)
                    sc.activation(
                        out=xfm.ap().rearrange("p (c k) -> p c k", c=NCH)[
                            :, 4 * g:4 * g + 4, t * 128:(t + 1) * 128],
                        in_=pT[G % 2].ap().rearrange("p (c k) -> p c k", c=4),
                        func=AF.Identity).then_inc(self.s_ac.h, 1)
                    self.s_ac.n += 1

            norm(0)
            norm(1)
            for t in range(2, NT):
                evac(t - 2)
                norm(t)
            evac(NT - 2)
            evac(NT - 1)

        @bl.tensor
        def _(te):
            for t in range(NT):
                self.w(te, s_nr, nr0 + t + 1)
                for g in range(4):
                    G = 4 * t + g
                    if G >= 2:
                        self.w(te, s_ac, ac0 + 1 + (G - 1))
                    for ch in range(4 * g, 4 * g + 4):
                        lm = te.transpose(
                            pT[G % 2][:, (ch % 4) * 128:(ch % 4 + 1) * 128],
                            xtm[:, (t % 2) * D + ch * 128:
                                (t % 2) * D + (ch + 1) * 128],
                            self.ident[:, :])
                    self.inc(lm, s_tp)

        if ag:
            @bl.sync
            def _(sy):
                self.w(sy, s_ac, ac0 + 1 + 32)
                for ch in range(NCH):
                    self.dma(sy, self.cc[f"xin{L}"].ap()[ch * 128:(ch + 1) * 128, :],
                             xfm[:, ch * TOK:(ch + 1) * TOK], self.s_o)

            @bl.gpsimd
            def _(g):
                self.w(g, self.s_o)
                self.inc(g.collective_compute(
                    "AllGather", AL.bypass, replica_groups=self.RG,
                    ins=[self.cc[f"xin{L}"].ap()],
                    outs=[self.cc[f"xout{L}"].ap()]), self.s_cc)
        self.barrier()
        es.close()

    def ph_attn(self, L):
        bl, d = self.bl, self.d
        rope = (L == 0)
        rounds = 2 if L == 0 else 1
        hpr = 2 if L == 0 else 4
        kvpr = 2 if L == 0 else 1
        nrt = hpr + kvpr
        ngrp = hpr + 2 * kvpr
        wn = ("wq0", "wk0", "wv0") if L == 0 else ("wq1", "wk1", "wv1")
        xg_dram = self.cc[f"xout{L}"].ap()

        es = ExitStack()
        XSL = 6
        xg = self.sbuf(es, f"at{L}_xg", [128, XSL * 512], BF16)
        wq = self.sbuf(es, f"at{L}_wq", [128, hpr * NCH * 128], BF16)
        wk = self.sbuf(es, f"at{L}_wk", [128, kvpr * NCH * 128], BF16)
        wv = self.sbuf(es, f"at{L}_wv", [128, kvpr * NCH * 128], BF16)
        qbuf = self.sbuf(es, f"at{L}_q", [128, hpr * SEQ], BF16)
        kbuf = self.sbuf(es, f"at{L}_k", [128, kvpr * SEQ], BF16)
        vbuf = self.sbuf(es, f"at{L}_v", [128, kvpr * 32 * 132], BF16)
        if rope:
            tcos = self.sbuf(es, "at_cos", [128, SEQ], BF16)
            tsin = self.sbuf(es, "at_sin", [128, SEQ], BF16)
        if rope:
            qscr = self.sbuf(es, "at_qscr", [128, nrt * 512], BF16)
            tA = self.sbuf(es, "at_tA", [128, nrt * 512], BF16)
            tB = self.sbuf(es, "at_tB", [128, nrt * 512], BF16)
        vfm = self.sbuf(es, f"at{L}_vfm", [128, kvpr * 512], BF16)
        sq = self.sbuf(es, f"at{L}_sq", [128, 1024], BF16)
        kmax = self.sbuf(es, f"at{L}_km", [1, 2 * kvpr * 16], F32)
        negm = self.sbuf(es, f"at{L}_ng", [1, SEQ], BF16)
        ntmp = self.sbuf(es, f"at{L}_nt", [1, 1024], F32)
        pt = [self.sbuf(es, f"at{L}_pt{i}", [128, 512], BF16) for i in range(2)]
        atm = self.sbuf(es, f"at{L}_atm", [128, 512], BF16)
        afm = self.sbuf(es, f"at{L}_afm", [128, 512], BF16)
        lrec = self.sbuf(es, f"at{L}_lr", [128, 8], F32)

        s_w, s_x, s_ck, s_qg, s_aq = self.s_w, self.s_x, self.s_ck, self.s_qg, self.s_aq
        s_tp, s_af, s_ve, s_ac, s_nr = self.s_tp, self.s_af, self.s_ve, self.s_ac, self.s_nr
        s_st, s_ex, s_pv, s_dv, s_o, s_g = (self.s_st, self.s_ex, self.s_pv,
                                            self.s_dv, self.s_o, self.s_g)

        @bl.gpsimd
        def _(g):
            if rope:
                self.dma(g, tcos[:, :], d["cosf"][:, :], s_w)
                self.dma(g, tsin[:, :], d["sinsg"][:, :], s_w)
            for kv in range(kvpr):
                g.memset(vbuf.ap().rearrange(
                    "p (kv s c) -> p kv s c", kv=kvpr, s=32)[:, kv, :, 128:129],
                    1.0)
            self.dinc(g, self.s_sl)
        g_ready = self.s_sl.n

        ckP = s_ck.n
        xgP = [s.n for s in self.s_xgs]

        for rnd in range(min(rounds, _ARNDS)):
            @bl.gpsimd
            def _(g, rnd=rnd):
                if rnd > 0 and _ASTOP >= 1:
                    self.w(g, s_qg, self.att_qg_end)
                for i in range(hpr):
                    self.dma(g, wq.ap().rearrange("p (i c k) -> p i c k",
                                                  i=hpr, c=NCH)[:, i],
                             d[wn[0]].rearrange("h (c p) k -> h p c k", p=128)[
                                 rnd * hpr + i], s_w)
                for i in range(kvpr):
                    self.dma(g, wk.ap().rearrange("p (i c k) -> p i c k",
                                                  i=kvpr, c=NCH)[:, i],
                             d[wn[1]].rearrange("h (c p) k -> h p c k", p=128)[
                                 rnd * kvpr + i], s_w)
                    self.dma(g, wv.ap().rearrange("p (i c k) -> p i c k",
                                                  i=kvpr, c=NCH)[:, i],
                             d[wn[2]].rearrange("h (c p) k -> h p c k", p=128)[
                                 rnd * kvpr + i], s_w)
            w_ready = s_w.n

            if _ASTOP < 1:
                continue
            with ExitStack() as ps:
                pq = [self.psum(ps, f"a{L}{rnd}pq{i}", [128, 512])
                      for i in range(hpr)]
                pk = [self.psum(ps, f"a{L}{rnd}pk{i}", [128, 512])
                      for i in range(kvpr)]
                pv = [self.psum(ps, f"a{L}{rnd}pv{i}", [128, 512])
                      for i in range(kvpr)]
                pTq = self.psum(ps, f"a{L}{rnd}pT", [128, 512], BF16)

                qg0, aq0, tp0, af0, ve0 = (s_qg.n, s_aq.n, s_tp.n, s_af.n,
                                           s_ve.n)

                cc_t = self.s_cc.n

                @bl.sync
                def _(sy, rnd=rnd, cc_t=cc_t):
                    self.w(sy, self.s_cc, cc_t)
                    for tb in range(QBN):
                        rb, half = tb // 2, tb % 2
                        for ch in range(NCH):
                            n = rnd * QBN * NCH + tb * NCH + ch
                            if n >= XSL:
                                self.w(sy, s_ck, ckP + n - XSL + 1)
                            self.dma(sy, xg[:, (n % XSL) * 512:(n % XSL + 1) * 512],
                                     xg_dram[rb * D + ch * 128:
                                             rb * D + (ch + 1) * 128,
                                             half * 512:(half + 1) * 512],
                                     self.s_xgs[n % XSL])

                ve_cur, af_cur = s_ve.n, s_af.n

                @bl.tensor
                def _(te, rnd=rnd):
                    self.w(te, s_w, w_ready)
                    self.w(te, self.s_sl, g_ready)
                    self.w(te, s_ve, ve_cur)
                    self.w(te, s_af, af_cur)
                    for tb in range(QBN):
                        if tb >= 1:
                            self.w(te, s_aq, aq0 + tb)
                        for ch in range(NCH):
                            n = rnd * QBN * NCH + tb * NCH + ch
                            self.w(te, self.s_xgs[n % XSL],
                                   xgP[n % XSL] + 16 * (n // XSL + 1))
                            xsl = xg[:, (n % XSL) * 512:(n % XSL + 1) * 512]
                            st = (ch == 0)
                            last = (ch == NCH - 1) and (L == 1)
                            for i in range(hpr):
                                lm = te.matmul(pq[i][:, :],
                                               wq[:, (i * NCH + ch) * 128:
                                                  (i * NCH + ch + 1) * 128],
                                               xsl, start=st, stop=last)
                            for i in range(kvpr):
                                lm = te.matmul(pk[i][:, :],
                                               wk[:, (i * NCH + ch) * 128:
                                                  (i * NCH + ch + 1) * 128],
                                               xsl, start=st, stop=last)
                                lm = te.matmul(pv[i][:, :],
                                               wv[:, (i * NCH + ch) * 128:
                                                  (i * NCH + ch + 1) * 128],
                                               xsl, start=st, stop=last)
                            self.inc(lm, s_ck)
                        if L == 0:
                            for i in range(hpr):
                                lm = te.matmul(pq[i][:, :],
                                               self.bq0[0:1, (rnd * hpr + i) * 128:
                                                        (rnd * hpr + i + 1) * 128],
                                               self.ones[0:1, 0:512],
                                               start=False, stop=True)
                            for i in range(kvpr):
                                lm = te.matmul(pk[i][:, :],
                                               self.bk0[0:1, (rnd * kvpr + i) * 128:
                                                        (rnd * kvpr + i + 1) * 128],
                                               self.ones[0:1, 0:512],
                                               start=False, stop=True)
                                lm = te.matmul(pv[i][:, :],
                                               self.bv0[0:1, (rnd * kvpr + i) * 128:
                                                        (rnd * kvpr + i + 1) * 128],
                                               self.ones[0:1, 0:512],
                                               start=False, stop=True)
                            self.inc(lm, s_qg)
                        else:
                            te.drain()
                            self.inc(te.nop(), s_qg)
                        for kv in range(kvpr):
                            self.w(te, s_aq, aq0 + tb + 1)
                            if tb * kvpr + kv >= 1:
                                self.w(te, s_af, af0 + tb * kvpr + kv)
                            for sub in range(4):
                                lm = te.transpose(
                                    pTq[:, sub * 128:(sub + 1) * 128],
                                    vfm[:, kv * 512 + sub * 128:
                                        kv * 512 + (sub + 1) * 128],
                                    self.ident[:, :])
                            self.inc(lm, s_tp)
                self.att_qg_end = s_qg.n

                @bl.scalar
                def _(sc, rnd=rnd):
                    for tb in range(QBN):
                        self.w(sc, s_qg, qg0 + tb + 1)
                        if rope:
                            if tb >= 1:
                                self.w(sc, s_ve, ve0 + tb)
                            for i in range(hpr):
                                sc.activation(out=qscr[:, i * 512:(i + 1) * 512],
                                              in_=pq[i][:, :], func=AF.Identity)
                            for i in range(kvpr):
                                sc.activation(
                                    out=qscr[:, (hpr + i) * 512:
                                             (hpr + i + 1) * 512],
                                    in_=pk[i][:, :], func=AF.Identity)
                        else:
                            for i in range(hpr):
                                sc.activation(out=qbuf[:, i * SEQ + tb * 512:
                                                       i * SEQ + (tb + 1) * 512],
                                              in_=pq[i][:, :], func=AF.Identity)
                            for i in range(kvpr):
                                sc.activation(out=kbuf[:, i * SEQ + tb * 512:
                                                       i * SEQ + (tb + 1) * 512],
                                              in_=pk[i][:, :], func=AF.Identity)
                        for kv in range(kvpr):
                            li = sc.activation(out=vfm[:, kv * 512:(kv + 1) * 512],
                                               in_=pv[kv][:, :],
                                               func=AF.Identity)
                        self.inc(li, s_aq)
                        for kv in range(kvpr):
                            self.w(sc, s_tp, tp0 + tb * kvpr + kv + 1)
                            li = sc.activation(
                                out=vbuf.ap().rearrange(
                                    "p (kv s c) -> p kv s c", kv=kvpr, s=32)[
                                    :, kv, tb * 4:tb * 4 + 4, 0:128],
                                in_=pTq.ap().rearrange("p (s c) -> p s c", s=4),
                                func=AF.Identity)
                            self.inc(li, s_af)

                if rope:
                    @bl.vector
                    def _(v, rnd=rnd):
                        for tb in range(QBN):
                            self.w(v, s_aq, aq0 + tb + 1)
                            tc = tcos[:, tb * 512:(tb + 1) * 512]
                            tslo = tsin[64:128, tb * 512:(tb + 1) * 512]
                            tshi = tsin[0:64, tb * 512:(tb + 1) * 512]
                            for i in range(4):
                                v.tensor_tensor(out=tA[:, i * 512:(i + 1) * 512],
                                                in0=qscr[:, i * 512:(i + 1) * 512],
                                                in1=tc, op=AL.mult)
                            for i in range(4):
                                v.tensor_tensor(
                                    out=tB[0:64, i * 512:(i + 1) * 512],
                                    in0=qscr[64:128, i * 512:(i + 1) * 512],
                                    in1=tslo, op=AL.mult)
                            for i in range(4):
                                v.tensor_tensor(
                                    out=tB[64:128, i * 512:(i + 1) * 512],
                                    in0=qscr[0:64, i * 512:(i + 1) * 512],
                                    in1=tshi, op=AL.mult)
                            v.drain()
                            for i in range(4):
                                dst = (qbuf[:, i * SEQ + tb * 512:
                                            i * SEQ + (tb + 1) * 512] if i < hpr
                                       else kbuf[:, (i - hpr) * SEQ + tb * 512:
                                                 (i - hpr) * SEQ + (tb + 1) * 512])
                                li = v.tensor_tensor(
                                    out=dst,
                                    in0=tA[:, i * 512:(i + 1) * 512],
                                    in1=tB[:, i * 512:(i + 1) * 512],
                                    op=AL.add)
                            self.inc(li, s_ve)
                qkv_ve_end = s_ve.n
                qkv_af_end = s_af.n

            if _DBG and rnd == 0:
                @bl.gpsimd
                def _(sy):
                    self.w(sy, s_ve)
                    self.w(sy, s_af)
                    self.dma(sy, self.dbg[:, 0:SEQ], qbuf[:, 0:SEQ], self.s_w)
                    self.dma(sy, self.dbg[:, SEQ:2 * SEQ], kbuf[:, 0:SEQ], self.s_w)
                    dv = self.dbg[:, 2 * SEQ:2 * SEQ + 32 * 129]
                    self.dma(sy, dv.rearrange("p (s c) -> p s c", c=129),
                             vbuf.ap().rearrange("p (s c) -> p s c", c=132)[
                                 :, 0:32, 0:129], self.s_w)

            if _ASTOP < 2:
                continue
            with ExitStack() as ps:
                pnr = [self.psum(ps, f"a{L}{rnd}pn{i}", [1, 512]) for i in range(2)]
                nr0b, qg0b, af0b, ng0b = s_nr.n, s_qg.n, s_af.n, self.s_ng.n
                NJ = kvpr * QBN

                @bl.scalar
                def _(sc, rnd=rnd):
                    self.w(sc, s_ve, qkv_ve_end)
                    self.w(sc, s_af, qkv_af_end)
                    for J in range(NJ):
                        kv, j = J // QBN, J % QBN
                        if J >= 2:
                            self.w(sc, s_qg, qg0b + J - 1)
                        sc.activation(out=sq[:, (J % 2) * 512:(J % 2 + 1) * 512],
                                      in_=kbuf[:, kv * SEQ + j * 512:
                                              kv * SEQ + (j + 1) * 512],
                                      func=AF.Square).then_inc(self.s_nr.h, 1)
                        self.s_nr.n += 1
                        if J >= 1:
                            if J >= 3:
                                self.w(sc, self.s_ng, ng0b + J - 2)
                            self.w(sc, s_qg, qg0b + J)
                            sc.activation(out=ntmp[0:1, ((J - 1) % 2) * 512:
                                                   ((J - 1) % 2 + 1) * 512],
                                          in_=pnr[(J - 1) % 2][0:1, :],
                                          func=AF.Sqrt).then_inc(self.s_af.h, 1)
                            self.s_af.n += 1
                    self.w(sc, s_qg, qg0b + NJ)
                    if NJ >= 2:
                        self.w(sc, self.s_ng, ng0b + NJ - 1)
                    sc.activation(out=ntmp[0:1, ((NJ - 1) % 2) * 512:
                                           ((NJ - 1) % 2 + 1) * 512],
                                  in_=pnr[(NJ - 1) % 2][0:1, :],
                                  func=AF.Sqrt).then_inc(self.s_af.h, 1)
                    self.s_af.n += 1

                @bl.tensor
                def _(te, rnd=rnd):
                    for J in range(NJ):
                        self.w(te, s_nr, nr0b + J + 1)
                        if J >= 2:
                            self.w(te, s_af, af0b + J - 1)
                        lm = te.matmul(pnr[J % 2][0:1, :], self.ones[:, 0:1],
                                       sq[:, (J % 2) * 512:(J % 2 + 1) * 512],
                                       start=True, stop=True)
                        self.inc(lm, s_qg)

                @bl.vector
                def _(v, rnd=rnd):
                    for J in range(NJ):
                        kv, j = J // QBN, J % QBN
                        self.w(v, s_af, af0b + J + 1)
                        v.tensor_reduce(out=kmax[0:1, kv * 16 + j:kv * 16 + j + 1],
                                        in_=ntmp[0:1, (J % 2) * 512:
                                                 (J % 2 + 1) * 512],
                                        axis=AX.X,
                                        op=AL.max).then_inc(self.s_ng.h, 1)
                        self.s_ng.n += 1
                    v.drain()
                    for kv in range(kvpr):
                        for j in range(QBN):
                            v.tensor_reduce(out=kmax[0:1, kv * 16 + 8 + j:
                                                     kv * 16 + 9 + j],
                                            in_=kmax[0:1, kv * 16:kv * 16 + j + 1],
                                            axis=AX.X, op=AL.max)
                    self.dinc(v, self.s_ng)
                kmax_ng = self.s_ng.n

            with ExitStack() as ps:
                stp = [self.psum(ps, f"a{L}{rnd}st{i}", [128, 512])
                       for i in range(2)]
                dps = self.psum(ps, f"a{L}{rnd}dp", [128, 128])
                pva = self.psum(ps, f"a{L}{rnd}va", [128, 512])
                pvb = self.psum(ps, f"a{L}{rnd}vb", [128, 512])
                pTa = self.psum(ps, f"a{L}{rnd}pTa", [128, 512], BF16)
                pnr = [self.psum(ps, f"a{L}{rnd}qn{i}", [1, 512])
                       for i in range(2)]

                for p in range(hpr):
                    if _ASTOP < 3 or (_ASTOP == 4 and p > 0):
                        continue
                    kv = p if L == 0 else 0
                    pg = rnd * hpr + p

                    nr0f, qg0f, afn0, ng0f = s_nr.n, s_qg.n, s_af.n, self.s_ng.n

                    @bl.scalar
                    def _(sc, p=p):
                        if p == 0:
                            self.w(sc, self.s_ng, kmax_ng)
                        for J in range(QBN):
                            if J >= 2:
                                self.w(sc, s_qg, qg0f + J - 1)
                            sc.activation(out=sq[:, (J % 2) * 512:(J % 2 + 1) * 512],
                                          in_=qbuf[:, p * SEQ + J * 512:
                                                  p * SEQ + (J + 1) * 512],
                                          func=AF.Square).then_inc(self.s_nr.h, 1)
                            self.s_nr.n += 1
                            if J >= 1:
                                self.w(sc, self.s_ng, max(ng0f, ng0f + J - 2))
                                self.w(sc, s_qg, qg0f + J)
                                sc.activation(out=ntmp[0:1, ((J - 1) % 2) * 512:
                                                       ((J - 1) % 2 + 1) * 512],
                                              in_=pnr[(J - 1) % 2][0:1, :],
                                              func=AF.Sqrt).then_inc(
                                                  self.s_af.h, 1)
                                self.s_af.n += 1
                        self.w(sc, s_qg, qg0f + QBN)
                        self.w(sc, self.s_ng, ng0f + QBN - 1)
                        sc.activation(out=ntmp[0:1, ((QBN - 1) % 2) * 512:
                                               ((QBN - 1) % 2 + 1) * 512],
                                      in_=pnr[(QBN - 1) % 2][0:1, :],
                                      func=AF.Sqrt).then_inc(self.s_af.h, 1)
                        self.s_af.n += 1

                    @bl.tensor
                    def _(te, p=p):
                        if p == 0:
                            self.w(te, self.s_ng, kmax_ng)
                        for J in range(QBN):
                            self.w(te, s_nr, nr0f + J + 1)
                            self.w(te, s_af, max(afn0, afn0 + J - 1))
                            lm = te.matmul(pnr[J % 2][0:1, :],
                                           self.ones[:, 0:1],
                                           sq[:, (J % 2) * 512:(J % 2 + 1) * 512],
                                           start=True, stop=True)
                            self.inc(lm, s_qg)

                    @bl.vector
                    def _(v, p=p, kv=kv):
                        for J in range(QBN):
                            self.w(v, s_af, afn0 + J + 1)
                            v.tensor_scalar(
                                out=negm[0:1, J * 512:(J + 1) * 512],
                                in0=ntmp[0:1, (J % 2) * 512:(J % 2 + 1) * 512],
                                scalar1=kmax[0:1, kv * 16 + 8 + J:
                                             kv * 16 + 9 + J],
                                scalar2=-1.02,
                                op0=AL.mult, op1=AL.mult).then_inc(self.s_ng.h, 1)
                            self.s_ng.n += 1
                    negm_ng = self.s_ng.n

                    st0, ex0, pv0, dv0 = s_st.n, s_ex.n, s_pv.n, s_dv.n
                    ve0f, tp0f, af0f, o0f = s_ve.n, s_tp.n, s_af.n, s_o.n
                    if _ASTOP < 4:
                        continue

                    units = []
                    for qb in range(QBN):
                        for kb in range(qb + 1):
                            for i in range(4):
                                units.append((qb, kb, i))
                    NU = len(units)
                    diag_idx = {}
                    nd = 0
                    for u, (qb, kb, i) in enumerate(units):
                        if kb == qb:
                            diag_idx[u] = nd
                            nd += 1
                    qb_last = {}
                    for u, (qb, kb, i) in enumerate(units):
                        qb_last[qb] = u

                    @bl.tensor
                    def _(te, p=p, kv=kv, units=units, NU=NU, st0=st0, ex0=ex0,
                          pv0=pv0, ve0f=ve0f, af0f=af0f, qb_last=qb_last):
                        self.w(te, self.s_ng, negm_ng)
                        bank_started = set()

                        def emit_st(u):
                            qb, kb, i = units[u]
                            diag = (kb == qb)
                            cb = i * 128 if diag else 0
                            ncol = 512 - cb
                            self.w(te, s_ex, max(ex0, ex0 + u - 1))
                            te.matmul(stp[u % 2][:, 0:ncol],
                                      kbuf[:, kv * SEQ + kb * 512 + i * 128:
                                           kv * SEQ + kb * 512 + (i + 1) * 128],
                                      qbuf[:, p * SEQ + qb * 512 + cb:
                                           p * SEQ + (qb + 1) * 512],
                                      start=True, stop=False)
                            lm = te.matmul(stp[u % 2][:, 0:ncol],
                                           self.ones[0:1, 0:128],
                                           negm[0:1, qb * 512 + cb:(qb + 1) * 512],
                                           start=False, stop=True)
                            self.inc(lm, s_st)

                        def emit_pv(u):
                            qb, kb, i = units[u]
                            diag = (kb == qb)
                            cb = i * 128 if diag else 0
                            if kb == 0 and i == 0:
                                self.w(te, s_ve, ve0f + qb)
                            self.w(te, s_ex, ex0 + u + 1)
                            trange = range(i, 4) if diag else range(4)
                            for t in trange:
                                bank = pva if t < 2 else pvb
                                col = (t % 2) * 132
                                key = (qb, t < 2)
                                stt = key not in bank_started
                                bank_started.add(key)
                                lm = te.matmul(bank[:, col:col + 129],
                                               pt[u % 2][:, t * 128 - cb:
                                                         (t + 1) * 128 - cb],
                                               vbuf[:, kv * 32 * 132 +
                                                    (kb * 4 + i) * 132:
                                                    kv * 32 * 132 +
                                                    (kb * 4 + i) * 132 + 129],
                                               start=stt, stop=False,
                                               skip_group_check=True)
                            self.inc(lm, s_pv)

                        def emit_trans(qb):
                            self.w(te, s_ve, ve0f + qb + 1)
                            self.w(te, s_af, af0f + qb)
                            for sub in range(4):
                                lm = te.transpose(
                                    pTa[:, sub * 128:(sub + 1) * 128],
                                    atm[:, sub * 128:(sub + 1) * 128],
                                    self.ident[:, :])
                            self.inc(lm, s_tp)

                        pv_done = -1
                        for u in range(NU):
                            emit_st(u)
                            if u >= 1 and pv_done < u - 1:
                                emit_pv(u - 1)
                                pv_done = u - 1
                            qb = units[u][0]
                            if qb_last[qb] == u:
                                emit_pv(u)
                                pv_done = u
                                emit_trans(qb)

                    @bl.vector
                    def _(v, p=p, units=units, NU=NU, st0=st0, ex0=ex0,
                          pv0=pv0, tp0f=tp0f, diag_idx=diag_idx,
                          qb_last=qb_last):
                        for u, (qb, kb, i) in enumerate(units):
                            if kb == qb:
                                self.w(v, s_st, st0 + u + 1)
                                self.w(v, s_ex, max(ex0, ex0 + u))
                                li = v.tensor_tensor(out=dps[:, :],
                                                     in0=stp[u % 2][:, 0:128],
                                                     in1=self.maskT[:, :],
                                                     op=AL.add)
                                self.inc(li, s_dv)
                            if qb_last[qb] == u:
                                self.w(v, s_pv, pv0 + u + 1)
                                self.w(v, s_tp, tp0f + qb)
                                for t in range(4):
                                    bank = pva if t < 2 else pvb
                                    col = (t % 2) * 132
                                    v.reciprocal(out=lrec[:, t:t + 1],
                                                 in_=bank[:, col + 128:col + 129])
                                v.drain()
                                for t in range(4):
                                    bank = pva if t < 2 else pvb
                                    col = (t % 2) * 132
                                    li = v.tensor_scalar(
                                        out=atm[:, t * 128:(t + 1) * 128],
                                        in0=bank[:, col:col + 128],
                                        scalar1=lrec[:, t:t + 1],
                                        scalar2=None, op0=AL.mult)
                                self.inc(li, s_ve)

                    @bl.scalar
                    def _(sc, p=p, units=units, NU=NU, st0=st0, pv0=pv0,
                          dv0=dv0, tp0f=tp0f, af0f=af0f, o0f=o0f,
                          diag_idx=diag_idx, qb_last=qb_last):
                        for u, (qb, kb, i) in enumerate(units):
                            diag = (kb == qb)
                            cb = i * 128 if diag else 0
                            ncol = 512 - cb
                            self.w(sc, s_st, st0 + u + 1)
                            if u >= 2:
                                self.w(sc, s_pv, pv0 + u - 1)
                            if diag:
                                self.w(sc, s_dv, dv0 + diag_idx[u] + 1)
                                li = sc.activation(out=pt[u % 2][:, 0:128],
                                                   in_=dps[:, :], func=AF.Exp,
                                                   scale=np.float32(SC))
                                if ncol > 128:
                                    li = sc.activation(
                                        out=pt[u % 2][:, 128:ncol],
                                        in_=stp[u % 2][:, 128:ncol],
                                        func=AF.Exp, scale=np.float32(SC))
                            else:
                                li = sc.activation(out=pt[u % 2][:, :],
                                                   in_=stp[u % 2][:, :],
                                                   func=AF.Exp,
                                                   scale=np.float32(SC))
                            self.inc(li, s_ex)
                            if qb_last[qb] == u:
                                self.w(sc, s_tp, tp0f + qb + 1)
                                self.w(sc, s_o, o0f + 16 * qb)
                                li = sc.activation(out=afm[:, :],
                                                   in_=pTa[:, :],
                                                   func=AF.Identity)
                                self.inc(li, s_af)

                    @bl.sync
                    def _(sy, pg=pg, af0f=af0f):
                        tgt = (self.cc[f"ain{L}A"] if pg < 2
                               else self.cc[f"ain{L}B"]).ap()
                        row = (pg % 2) * 128
                        for qb in range(QBN):
                            self.w(sy, s_af, af0f + qb + 1)
                            self.dma(sy, tgt[row:row + 128,
                                             qb * 512:(qb + 1) * 512],
                                     afm[:, :], s_o)

                    if pg == 1:
                        oA = s_o.n

                        @bl.gpsimd
                        def _(g, oA=oA):
                            self.w(g, s_o, oA)
                            self.inc(g.collective_compute(
                                "AllGather", AL.bypass, replica_groups=self.RG,
                                ins=[self.cc[f"ain{L}A"].ap()],
                                outs=[self.cc[f"aout{L}A"].ap()]), self.s_cc)

        if _ASTOP >= 6:
            @bl.gpsimd
            def _(g):
                self.w(g, self.s_o)
                self.inc(g.collective_compute(
                    "AllGather", AL.bypass, replica_groups=self.RG,
                    ins=[self.cc[f"ain{L}B"].ap()],
                    outs=[self.cc[f"aout{L}B"].ap()]), self.s_cc)
        self.barrier()
        es.close()

    def ph_outproj(self, L):
        bl, d = self.bl, self.d
        es = ExitStack()
        wo = self.sbuf(es, f"op{L}_wo", [128, NCH * D], BF16)
        ao = self.sbuf(es, f"op{L}_ao", [128, NCH * TOK], BF16)
        if L == 0:
            self.bo0 = self.sbuf(es, "sb_bo0", [1, D], BF16)
        aview = {hf: self.cc[f"aout{L}{hf}"].ap().rearrange(
            "r (q t) -> (r q) t", q=4) for hf in ("A", "B")}

        s_w, s_g, s_qg, s_ve = self.s_w, self.s_g, self.s_qg, self.s_ve
        qg0, ve0 = s_qg.n, s_ve.n
        won = "wo0" if L == 0 else "wo1"

        @bl.gpsimd
        def _(g):
            self.w(g, self.s_cc)
            if L == 0:
                self.dma(g, self.bo0[:, :], d["bo0"][:, :], s_w)
            for ch in range(NCH):
                self.dma(g, wo[:, ch * D:(ch + 1) * D],
                         d[won][ch * 128:(ch + 1) * 128, :], s_w)
            for ch in range(NCH):
                g.indirect_dma_start(
                    out=ao[:, ch * TOK:(ch + 1) * TOK], out_offset=None,
                    in_=aview["A" if ch % 4 < 2 else "B"],
                    in_offset=bass.IndirectOffsetOnAxis(
                        self.oidx[:, ch:ch + 1], 0)).then_inc(self.s_g.h, 16)
                self.s_g.n += 16

        with ExitStack() as ps:
            pop = [self.psum(ps, f"op{L}p{i}", [128, 2048]) for i in range(2)]

            @bl.tensor
            def _(te):
                self.w(te, s_w)
                self.w(te, s_g)
                for t in range(NT):
                    if t >= 2:
                        self.w(te, s_ve, ve0 + t - 1)
                    pp = pop[t % 2]
                    for ch in range(NCH):
                        for os_ in range(4):
                            lm = te.matmul(pp[:, os_ * 512:(os_ + 1) * 512],
                                           ao[:, ch * TOK + t * 128:
                                              ch * TOK + (t + 1) * 128],
                                           wo[:, ch * D + os_ * 512:
                                              ch * D + (os_ + 1) * 512],
                                           start=(ch == 0),
                                           stop=(L == 1 and ch == NCH - 1))
                    if L == 0:
                        for os_ in range(4):
                            lm = te.matmul(pp[:, os_ * 512:(os_ + 1) * 512],
                                           self.ones[0:1, 0:128],
                                           self.bo0[0:1, os_ * 512:
                                                    (os_ + 1) * 512],
                                           start=False, stop=True)
                    self.inc(lm, s_qg)

            @bl.vector
            def _(v):
                for t in range(NT):
                    self.w(v, s_qg, qg0 + t + 1)
                    v.tensor_tensor(out=self.stream[:, t * D:(t + 1) * D],
                                    in0=pop[t % 2][:, :],
                                    in1=self.stream[:, t * D:(t + 1) * D],
                                    op=AL.add).then_inc(self.s_ve.h, 1)
                    self.s_ve.n += 1

            self.barrier()
        es.close()

    def ph_moe(self):
        bl, d = self.bl, self.d
        es = ExitStack()
        x2 = self.xfm_keep
        csb = self.sbuf(es, "moe_c", [128, 64], F32)
        esb = self.sbuf(es, "moe_e", [128, 96], F32)
        trs = self.sbuf(es, "moe_tr", [16, 1024], BF16)
        gwt = [self.sbuf(es, f"moe_gw{i}", [128, NCH * 128], BF16)
               for i in range(2)]
        dwt = self.sbuf(es, "moe_dw", [128, 8 * D], BF16)
        gub = self.sbuf(es, "moe_gub", [16, D], BF16)
        dnb = self.sbuf(es, "moe_dnb", [16, D], BF16)
        gua = self.sbuf(es, "moe_gua", [128, NCH * 16], BF16)
        dna = self.sbuf(es, "moe_dna", [128, 8 * 16], BF16)
        hsb = self.sbuf(es, "moe_h", [128, 16 * 512], BF16)
        ssb = self.sbuf(es, "moe_s", [128, 1024], F32)
        t1s = self.sbuf(es, "moe_t1", [16, 1024], BF16)
        t2s = self.sbuf(es, "moe_t2", [16, 1024], BF16)
        wc2 = self.sbuf(es, "qt_wc2", [128, D], F32)
        qqa = self.sbuf(es, "qt_qa", [128, D], BF16)
        qqb = self.sbuf(es, "qt_qb", [128, D], BF16)
        fq = [self.sbuf(es, f"qt_fq{i}", [128, D], BF16) for i in range(2)]
        prt = self.prt
        xq0 = self.s_xgs[0].n
        oq0 = [self.s_xgs[2].n, self.s_xgs[3].n]
        qtt = []

        def emit_qchunk(v, ch):
            if _PH < 8:
                return
            self.w(v, self.s_xgs[0], xq0 + 16 * (ch + 1))
            if ch >= 2:
                self.w(v, self.s_xgs[2 + ch % 2],
                       oq0[ch % 2] + 16 * (ch // 2))
            v.tensor_scalar(out=qqa[:, :], in0=wc2[:, :],
                            scalar1=prt[:, 19:20], scalar2=prt[:, 18:19],
                            op0=AL.is_gt, op1=AL.mult)
            v.tensor_scalar(out=qqb[:, :], in0=wc2[:, :],
                            scalar1=prt[:, 20:21], scalar2=prt[:, 18:19],
                            op0=AL.is_lt, op1=AL.mult)
            v.drain()
            v.tensor_tensor(out=fq[ch % 2][:, :], in0=qqa[:, :],
                            in1=qqb[:, :],
                            op=AL.subtract).then_inc(self.s_ve.h, 3)
            self.s_ve.n += 3
            qtt.append(self.s_ve.n)

        s_w, s_qg, s_ve, s_ac, s_dv = (self.s_w, self.s_qg, self.s_ve,
                                       self.s_ac, self.s_dv)
        m_gu, m_sl, m_h = self.s_gu, self.s_sl, self.s_h
        m_t1, m_t1e, m_t2, m_t2e = self.s_t1, self.s_t1e, self.s_t2, self.s_t2e
        m_y, m_ys = self.s_y, self.s_ys

        with ExitStack() as ps:
            plg = [self.psum(ps, f"moe_lg{i}", [128, 8]) for i in range(2)]
            ptr = self.psum(ps, "moe_ptr", [16, 512])
            qg0, ac0, ve0, dv0 = s_qg.n, s_ac.n, s_ve.n, s_dv.n

            @bl.tensor
            def _(te):
                for G in range(2):
                    if G >= 1:
                        self.w(te, s_ac, ac0 + G)
                    for ch in range(NCH):
                        lm = te.matmul(ptr[:, :],
                                       self.gA[:, ch * 16:(ch + 1) * 16],
                                       x2[:, ch * TOK + G * 512:
                                          ch * TOK + (G + 1) * 512],
                                       start=(ch == 0), stop=(ch == NCH - 1))
                    self.inc(lm, s_qg)
                for t in range(NT):
                    self.w(te, s_ac, ac0 + 2)
                    if t >= 2:
                        self.w(te, s_dv, dv0 + t - 1)
                        self.w(te, s_ac, ac0 + t + 1)
                    pp = plg[t % 2]
                    for ch in range(NCH):
                        te.matmul(pp[:, :],
                                  x2[:, ch * TOK + t * 128:
                                     ch * TOK + (t + 1) * 128],
                                  self.gw[:, ch * 8:(ch + 1) * 8],
                                  start=(ch == 0), stop=False)
                    lm = te.matmul(pp[:, :], trs[:, t * 128:(t + 1) * 128],
                                   self.gB[:, :], start=False, stop=True)
                    self.inc(lm, s_qg)

            @bl.vector
            def _(v):
                for t in range(NT):
                    self.w(v, s_qg, qg0 + 2 + t + 1)
                    v.tensor_reduce(out=esb[:, 64 + t:65 + t],
                                    in_=plg[t % 2][:, :], axis=AX.X,
                                    op=AL.max, negate=True)
                    self.dinc(v, s_dv)

            @bl.scalar
            def _(sc):
                for G in range(2):
                    self.w(sc, s_qg, qg0 + G + 1)
                    sc.activation(out=trs[:, G * 512:(G + 1) * 512],
                                  in_=ptr[:, :], func=AF.Identity,
                                  scale=np.float32(2.0))
                    self.dinc(sc, s_ac)
                for t in range(NT):
                    self.w(sc, s_dv, dv0 + t + 1)
                    sc.activation(out=esb[:, t * 8:t * 8 + 8],
                                  in_=plg[t % 2][:, :], func=AF.Exp,
                                  bias=esb[:, 64 + t:65 + t])
                    self.dinc(sc, s_ac)

            @bl.vector
            def _(v):
                for t in range(NT):
                    self.w(v, s_ac, ac0 + 2 + t + 1)
                    e = esb[:, t * 8:t * 8 + 8]
                    v.tensor_scalar(out=esb[:, 80:88], in0=e, scalar1=1.0,
                                    scalar2=None, op0=AL.is_lt)
                    v.drain()
                    v.tensor_tensor(out=esb[:, 88:96], in0=e, in1=esb[:, 80:88],
                                    op=AL.mult)
                    v.drain()
                    v.tensor_reduce(out=esb[:, 72:73], in_=esb[:, 88:96],
                                    axis=AX.X, op=AL.max)
                    v.drain()
                    v.tensor_scalar(out=esb[:, 73:74], in0=esb[:, 72:73],
                                    scalar1=1.0, scalar2=None, op0=AL.add)
                    v.drain()
                    v.reciprocal(out=esb[:, 74:75], in_=esb[:, 73:74])
                    v.tensor_scalar(out=esb[:, 80:88], in0=e,
                                    scalar1=esb[:, 72:73], scalar2=None,
                                    op0=AL.is_ge)
                    v.drain()
                    v.scalar_tensor_tensor(out=csb[:, t * 8:t * 8 + 8], in0=e,
                                           scalar=esb[:, 74:75],
                                           in1=esb[:, 80:88],
                                           op0=AL.mult, op1=AL.mult)
                    self.dinc(v, s_ve)
            router_ve = s_ve.n

        with ExitStack() as ps:
            pg_ = [self.psum(ps, f"moe_pg{i}", [128, 512]) for i in range(3)]
            pt12 = self.psum(ps, "moe_pt12", [16, 512])
            py = self.psum(ps, "moe_py", [128, 2048])

            gu0, sl0, h0 = m_gu.n, m_sl.n, m_h.n
            t10, t1e0, t20, t2e0 = m_t1.n, m_t1e.n, m_t2.n, m_t2e.n
            y0, ys0 = m_y.n, m_ys.n
            w0 = s_w.n

            wl0 = self.s_wl.n
            wd0 = self.s_wd.n
            wg0 = [s.n for s in self.s_wgs]
            wlora = [wl0 + 64 * (e + 1) for e in range(8)]
            wdn = [wd0 + 128 * (e + 1) for e in range(8)]
            wtile = []
            for e in range(8):
                tl = {}
                for idx in range(16):
                    oc = (idx // 2) + 8 * (idx % 2)
                    gidx = e * 16 + idx
                    tl[oc] = (self.s_wgs[gidx % 2],
                              wg0[gidx % 2] + 16 * (gidx // 2 + 1))
                wtile.append(tl)

            @bl.gpsimd
            def _(g):
                for e in range(8):
                    if e >= 1:
                        self.w(g, m_gu, gu0 + 32 * e)
                        self.w(g, m_t2, t20 + 2 * e)
                        self.w(g, m_y, y0 + 32 * e)
                    self.dma(g, gua.ap().rearrange("p (c r) -> p c r", c=NCH),
                             d["gupAT"].rearrange("e (c p) r -> e p c r",
                                                  p=128)[e], self.s_wl)
                    self.dma(g, gub[:, :], d["gupBT"][e], self.s_wl)
                    self.dma(g, dna.ap().rearrange("p (c r) -> p c r", c=8),
                             d["dwnAT"].rearrange("e (c p) r -> e p c r",
                                                  p=128)[e], self.s_wl)
                    self.dma(g, dnb[:, :], d["dwnBT"][e], self.s_wl)
                    for idx in range(16):
                        oc = (idx // 2) + 8 * (idx % 2)
                        if e * 16 + idx >= 2:
                            pidx = e * 16 + idx - 2
                            pe, pi = pidx // 16, pidx % 16
                            poc = (pi // 2) + 8 * (pi % 2)
                            jp, half = poc % 8, poc // 8
                            self.w(g, m_gu, gu0 + pe * 32 + 4 * jp + 2 + half + 1)
                        self.dma(g, gwt[idx % 2].ap().rearrange(
                                     "p (c k) -> p c k", c=NCH),
                                 d["gupb"].rearrange(
                                     "e o (c p) k -> e o p c k", p=128)[e, oc],
                                 self.s_wgs[(e * 16 + idx) % 2])
                    for j in range(8):
                        self.dma(g, dwt[:, j * D:(j + 1) * D],
                                 d["dwn"][e].rearrange("(c p) o -> c p o",
                                                       p=128)[j], self.s_wd)

            @bl.tensor
            def _(te):
                self.w(te, s_ve, router_ve)
                for e in range(8):
                    self.w(te, self.s_wl, wlora[e])
                    if e >= 1:
                        self.w(te, m_t2e, t2e0 + 2 * e)
                    for G in range(2):
                        if G == 1:
                            self.w(te, m_t1e, t1e0 + 2 * e + 1)
                        for ch in range(NCH):
                            lm = te.matmul(pt12[:, :],
                                           gua[:, ch * 16:(ch + 1) * 16],
                                           x2[:, ch * TOK + G * 512:
                                              ch * TOK + (G + 1) * 512],
                                           start=(ch == 0),
                                           stop=(ch == NCH - 1))
                        self.inc(lm, m_t1)
                    self.w(te, m_t1e, t1e0 + 2 * e + 2)
                    for jp in range(8):
                        for G in range(2):
                            for half in range(2):
                                oc = jp + 8 * half
                                n = e * 32 + 4 * jp + 2 * G + half
                                wsem, wval = wtile[e][oc]
                                self.w(te, wsem, wval)
                                prev = n - 3
                                if prev >= 0:
                                    if prev % 2 == 0:
                                        self.w(te, m_sl, sl0 + prev // 2 + 1)
                                    else:
                                        self.w(te, m_h, h0 + prev // 2 + 1)
                                pp = pg_[n % 3]
                                buf = gwt[oc // 8]
                                for ch in range(NCH):
                                    te.matmul(pp[:, :],
                                              buf[:, ch * 128:(ch + 1) * 128],
                                              x2[:, ch * TOK + G * 512:
                                                 ch * TOK + (G + 1) * 512],
                                              start=(ch == 0), stop=False)
                                lm = te.matmul(pp[:, :],
                                               gub[:, oc * 128:(oc + 1) * 128],
                                               t1s[:, G * 512:(G + 1) * 512],
                                               start=False, stop=True)
                                self.inc(lm, m_gu)
                    for G in range(2):
                        self.w(te, m_h, h0 + e * 16 + 15 + G)
                        if G == 1:
                            self.w(te, m_t2e, t2e0 + 2 * e + 1)
                        for j in range(8):
                            lm = te.matmul(pt12[:, :],
                                           dna[:, j * 16:(j + 1) * 16],
                                           hsb[:, (G * 8 + j) * 512:
                                               (G * 8 + j + 1) * 512],
                                           start=(j == 0), stop=(j == 7))
                        self.inc(lm, m_t2)
                    self.w(te, self.s_wd, wdn[e])
                    for G in range(2):
                        self.w(te, m_t2e, t2e0 + 2 * e + G + 1)
                        for t in range(4):
                            Y = e * 8 + G * 4 + t
                            for os_ in range(4):
                                if Y >= 1:
                                    self.w(te, m_ys, ys0 + 4 * (Y - 1) + os_ + 1)
                                for j in range(8):
                                    te.matmul(py[:, os_ * 512:(os_ + 1) * 512],
                                              hsb[:, (G * 8 + j) * 512 + t * 128:
                                                  (G * 8 + j) * 512 +
                                                  (t + 1) * 128],
                                              dwt[:, j * D + os_ * 512:
                                                  j * D + (os_ + 1) * 512],
                                              start=(j == 0), stop=False)
                                lm = te.matmul(py[:, os_ * 512:(os_ + 1) * 512],
                                               t2s[:, G * 512 + t * 128:
                                                   G * 512 + (t + 1) * 128],
                                               dnb[:, os_ * 512:(os_ + 1) * 512],
                                               start=False, stop=True)
                                self.inc(lm, m_y)

            @bl.scalar
            def _(sc):
                for e in range(8):
                    for G in range(2):
                        self.w(sc, m_t1, t10 + 2 * e + G + 1)
                        sc.activation(out=t1s[:, G * 512:(G + 1) * 512],
                                      in_=pt12[:, :], func=AF.Identity,
                                      scale=np.float32(2.0)).then_inc(
                                          self.s_t1e.h, 1)
                        self.s_t1e.n += 1
                    for P in range(16):
                        jp, G = P // 2, P % 2
                        gunit = e * 32 + 4 * jp + 2 * G
                        self.w(sc, m_gu, gu0 + gunit + 1)
                        Pg = e * 16 + P
                        if Pg >= 2:
                            self.w(sc, m_h, h0 + Pg - 1)
                        sc.activation(out=ssb[:, (Pg % 2) * 512:
                                              (Pg % 2 + 1) * 512],
                                      in_=pg_[(e * 32 + 4 * jp + 2 * G) % 3][:, :],
                                      func=AF.Silu).then_inc(self.s_sl.h, 1)
                        self.s_sl.n += 1
                    for G in range(2):
                        self.w(sc, m_t2, t20 + 2 * e + G + 1)
                        sc.activation(out=t2s[:, G * 512:(G + 1) * 512],
                                      in_=pt12[:, :], func=AF.Identity,
                                      scale=np.float32(2.0)).then_inc(
                                          self.s_t2e.h, 1)
                        self.s_t2e.n += 1

            @bl.vector
            def _(v):
                for e in range(8):
                    for P in range(16):
                        jp, G = P // 2, P % 2
                        uunit = e * 32 + 4 * jp + 2 * G + 1
                        Pg = e * 16 + P
                        self.w(v, m_gu, gu0 + uunit + 1)
                        self.w(v, m_sl, sl0 + Pg + 1)
                        v.tensor_tensor(out=hsb[:, (G * 8 + jp) * 512:
                                                (G * 8 + jp + 1) * 512],
                                        in0=pg_[(e * 32 + 4 * jp + 2 * G + 1)
                                                % 3][:, :],
                                        in1=ssb[:, (Pg % 2) * 512:
                                                (Pg % 2 + 1) * 512],
                                        op=AL.mult).then_inc(self.s_h.h, 1)
                        self.s_h.n += 1
                    emit_qchunk(v, 2 * e)
                    for G in range(2):
                        for t in range(4):
                            Y = e * 8 + G * 4 + t
                            tile = G * 4 + t
                            for os_ in range(4):
                                self.w(v, m_y, y0 + 4 * Y + os_ + 1)
                                v.scalar_tensor_tensor(
                                    out=self.stream[:, tile * D + os_ * 512:
                                                    tile * D + (os_ + 1) * 512],
                                    in0=py[:, os_ * 512:(os_ + 1) * 512],
                                    scalar=csb[:, tile * 8 + e:tile * 8 + e + 1],
                                    in1=self.stream[:, tile * D + os_ * 512:
                                                    tile * D + (os_ + 1) * 512],
                                    op0=AL.mult,
                                    op1=AL.add).then_inc(self.s_ys.h, 1)
                                self.s_ys.n += 1
                    emit_qchunk(v, 2 * e + 1)

            @bl.sync
            def _(sy):
                if _PH < 8:
                    return
                for ch in range(NCH):
                    if ch >= 1:
                        self.w(sy, s_ve, qtt[ch - 1])
                    self.dma(sy, wc2[:, :],
                             d["fwT"][ch * 128:(ch + 1) * 128, :],
                             self.s_xgs[0])
                    if ch >= 1:
                        self.w(sy, s_ve, qtt[ch - 1])
                        self.dma(sy, self.wqd.ap()[(ch - 1) * 128:ch * 128, :],
                                 fq[(ch - 1) % 2][:, :],
                                 self.s_xgs[2 + (ch - 1) % 2])
                self.w(sy, s_ve, qtt[NCH - 1])
                self.dma(sy, self.wqd.ap()[(NCH - 1) * 128:NCH * 128, :],
                         fq[(NCH - 1) % 2][:, :],
                         self.s_xgs[2 + (NCH - 1) % 2])

            self.barrier()
        es.close()
        self.keep_es.close()

    def ph_quant(self, es):
        bl, d = self.bl, self.d
        wch = self.sbuf(es, "qt_wc", [128, D], F32)
        prt = self.prt
        onesf = self.sbuf(es, "qt_on", [128, 2], F32)
        onesr = self.sbuf(es, "qt_onr", [1, 128], F32)

        s_ve, s_qg = self.s_ve, self.s_qg
        ve0, qg0f, ac0f = s_ve.n, s_qg.n, self.s_ac.n
        wg0 = self.s_wgs[0].n

        @bl.gpsimd
        def _(g):
            g.memset(onesf[:, :], 1.0)
            g.memset(onesr[:, :], 1.0)
            self.dinc(g, self.s_sl)
            for ch in range(NCH):
                if ch >= 1:
                    self.w(g, s_ve, ve0 + ch)
                self.dma(g, wch[:, :], d["fwT"][ch * 128:(ch + 1) * 128, :],
                         self.s_wgs[0])
        slf = self.s_sl.n

        @bl.vector
        def _(v):
            for ch in range(NCH):
                self.w(v, self.s_wgs[0], wg0 + 16 * (ch + 1))
                v.tensor_reduce(out=prt[:, ch:ch + 1], in_=wch[:, :],
                                axis=AX.X, op=AL.add, apply_absolute_value=True)
                self.dinc(v, s_ve)
            v.drain()
            v.tensor_reduce(out=prt[:, 16:17], in_=prt[:, 0:16], axis=AX.X,
                            op=AL.add)
            v.drain()
            self.dinc(v, s_ve)

        with ExitStack() as pstiny:
            pfs = self.psum(pstiny, "qt_ps", [128, 512], F32)

            @bl.tensor
            def _(te):
                self.w(te, s_ve, ve0 + 17)
                self.w(te, self.s_sl, slf)
                lm = te.matmul(pfs[0:1, 0:1], prt[:, 16:17], onesf[:, 0:1],
                               start=True, stop=True)
                self.inc(lm, s_qg)
                self.w(te, self.s_ac, ac0f + 1)
                lm = te.matmul(pfs[:, 4:5], onesr[0:1, :], prt[0:1, 21:22],
                               start=True, stop=True)
                self.inc(lm, s_qg)
                self.w(te, self.s_ac, ac0f + 2)

            @bl.scalar
            def _(sc):
                self.w(sc, s_qg, qg0f + 1)
                sc.activation(out=prt[0:1, 21:22], in_=pfs[0:1, 0:1],
                              func=AF.Identity)
                self.dinc(sc, self.s_ac)
                self.w(sc, s_qg, qg0f + 2)
                sc.activation(out=prt[:, 17:18], in_=pfs[:, 4:5],
                              func=AF.Identity)
                self.dinc(sc, self.s_ac)

        @bl.vector
        def _(v):
            self.w(v, self.s_ac, ac0f + 2)
            v.tensor_scalar(out=prt[:, 18:19], in0=prt[:, 17:18],
                            scalar1=float(1.0 / (D * D)), scalar2=None,
                            op0=AL.mult)
            v.drain()
            v.tensor_scalar(out=prt[:, 19:20], in0=prt[:, 18:19],
                            scalar1=0.5, scalar2=0.5e-10,
                            op0=AL.mult, op1=AL.add)
            v.drain()
            v.tensor_scalar(out=prt[:, 20:21], in0=prt[:, 19:20],
                            scalar1=-1.0, scalar2=None, op0=AL.mult)
            self.dinc(v, s_ve)

    def ph_l1ffn(self):
        bl, d = self.bl, self.d
        es = ExitStack()
        x4 = self.xfm_keep
        fwq = self.sbuf(es, "ff_wq", [128, NCH * D], BF16)
        self.fb1 = self.sbuf(es, "sb_fb1", [1, D], BF16)

        s_w, s_ve, s_qg = self.s_w, self.s_ve, self.s_qg
        ve1, qg1, x1 = s_ve.n, s_qg.n, self.s_x.n

        @bl.gpsimd
        def _(g):
            self.dma(g, self.fb1[:, :], d["fb1"][:, :], s_w)

        @bl.sync
        def _(sy):
            for ch in range(NCH):
                self.dma(sy, fwq[:, ch * D:(ch + 1) * D],
                         self.wqd.ap()[ch * 128:(ch + 1) * 128, :], self.s_x)

        with ExitStack() as ps:
            pop = [self.psum(ps, f"ffp{i}", [128, 2048]) for i in range(2)]

            @bl.tensor
            def _(te):
                self.w(te, self.s_x, x1 + 16 * NCH)
                self.w(te, self.s_w)
                for t in range(NT):
                    if t >= 2:
                        self.w(te, s_ve, ve1 + t - 1)
                    pp = pop[t % 2]
                    for ch in range(NCH):
                        for os_ in range(4):
                            te.matmul(pp[:, os_ * 512:(os_ + 1) * 512],
                                      x4[:, ch * TOK + t * 128:
                                         ch * TOK + (t + 1) * 128],
                                      fwq[:, ch * D + os_ * 512:
                                          ch * D + (os_ + 1) * 512],
                                      start=(ch == 0), stop=False)
                    for os_ in range(4):
                        lm = te.matmul(pp[:, os_ * 512:(os_ + 1) * 512],
                                       self.ones[0:1, 0:128],
                                       self.fb1[0:1, os_ * 512:(os_ + 1) * 512],
                                       start=False, stop=True)
                    self.inc(lm, s_qg)

            @bl.vector
            def _(v):
                for t in range(NT):
                    self.w(v, s_qg, qg1 + t + 1)
                    v.tensor_tensor(out=self.stream[:, t * D:(t + 1) * D],
                                    in0=pop[t % 2][:, :],
                                    in1=self.stream[:, t * D:(t + 1) * D],
                                    op=AL.add).then_inc(self.s_ve.h, 1)
                    self.s_ve.n += 1

            self.barrier()
        es.close()
        self.keep_es.close()

    def ph_out(self, y_out):
        bl = self.bl

        @bl.sync
        def _(sy):
            for t in range(NT):
                self.dma(sy, y_out[t * 128:(t + 1) * 128, :],
                         self.stream[:, t * D:(t + 1) * D], self.s_o)
            self.w(sy, self.s_o)


def kernel(**inputs):
    in_maps = _prep(inputs)
    b = Builder()
    nc = b.build()
    res = run_bass_kernel_spmd(nc, in_maps, core_ids=list(range(NCORE)))
    y = np.concatenate([np.asarray(res.results[c]["y"]) for c in range(NCORE)],
                       axis=0)
    return np.ascontiguousarray(y.reshape(2, SEQ, D), dtype=np.float32)
